# version 05
# baseline (speedup 1.0000x reference)
import numpy as np
import ml_dtypes
import concourse.bass as bass
import concourse.mybir as mybir
from concourse.bass_utils import run_bass_kernel_spmd
from contextlib import ExitStack

F32 = mybir.dt.float32
BF16 = mybir.dt.bfloat16
AF = mybir.ActivationFunctionType
ALU = mybir.AluOpType
AX = mybir.AxisListType

T = 2048
DM = 1024
NH = 32
M = 2
C = 128
NCH = T // C
NT = T // 128
E = 8
FF = 4096
VA = 33 * NH
LN_EPS = 1e-5
ATT_EPS = 1e-6
RMS_EPS = 1e-6
TRUNC = None
FCONSTS = (LN_EPS, RMS_EPS, -0.5, -1.0, 0.85, 0.9995, 0.25, -0.25, 1.0, 0.0,
           ATT_EPS, 2.0)
DEBUG = False

_ENG = ("te", "dv", "ac", "gp", "sy")
_BLK = dict(te="tensor", dv="vector", ac="scalar", gp="gpsimd", sy="sync")


class Prog:

    def __init__(self, nc):
        self.nc = nc
        self.ops = []
        self.keys = {}

    def op(self, eng, fn, after=(), key=None, dma=False, stream=None):
        assert eng in _ENG
        if isinstance(after, str):
            after = (after,)
        for k in after:
            assert k in self.keys, f"forward/unknown dep {k!r}"
        if key is not None:
            assert key not in self.keys, f"dup key {key!r}"
            self.keys[key] = len(self.ops)
        if dma and stream is None:
            assert key is not None
            parts = key.split(":")
            while parts and parts[-1].isdigit():
                parts.pop()
            stream = ":".join(parts) or key
        self.ops.append([eng, fn, tuple(after), key, dma, stream])

    def fence(self, tag):
        last = {}
        for i, o in enumerate(self.ops):
            last[o[0]] = i
        lkeys = []
        for eng, i in last.items():
            key = self.ops[i][3]
            if key is None:
                key = f"_f:{tag}:{eng}"
                self.ops[i][3] = key
                self.keys[key] = i
            lkeys.append(key)
        for eng in _ENG:
            self.op(eng, lambda h: h.drain(), after=tuple(lkeys), key=f"F:{tag}:{eng}")

    def lint(self):
        return

    def emit(self):
        awaited = set(k for o in self.ops for k in o[2])
        stream_idx = {}
        for i, o in enumerate(self.ops):
            if o[4]:
                stream_idx.setdefault(o[5], []).append(i)
        sems = {e: self.nc.alloc_semaphore(name=f"S_{e}") for e in _ENG}
        ssems = {s: self.nc.alloc_semaphore(name=f"D_{s}"[:31])
                 for s in stream_idx}
        cnt = dict.fromkeys(_ENG, 0)
        keyval = {}
        for eng, fn, after, key, dma, stream in self.ops:
            if dma:
                continue
            if key in awaited:
                cnt[eng] += 1
                keyval[key] = (eng, cnt[eng])

        def dma_wait(k, widx):
            oi = self.keys[k]
            s = self.ops[oi][5]
            lst = stream_idx[s]
            npre = sum(1 for i in lst if i < widx)
            pos = lst.index(oi) + 1
            assert npre >= pos, f"dep {k} after waiter?"
            return ssems[s], 16 * npre

        with self.nc.Block() as blk:
            def mkrun(eng):
                def run(h):
                    done = {}
                    for widx, (oeng, fn, after, key, dma, stream) in enumerate(self.ops):
                        if oeng != eng:
                            continue
                        for k in after:
                            if self.ops[self.keys[k]][4]:
                                sem, v = dma_wait(k, widx)
                            else:
                                se, v = keyval[k]
                                if se == eng:
                                    continue
                                sem = sems[se]
                            sid = sem.name if hasattr(sem, "name") else id(sem)
                            if done.get(sid, -1) >= v:
                                continue
                            h.wait_ge(sem, v)
                            done[sid] = v
                        inst = fn(h)
                        if not dma and eng in ("dv", "ac"):
                            fs = 9999
                            try:
                                outs = inst.ins.outs
                                for oo in outs:
                                    ba = getattr(oo, "bass_ap", None)
                                    if ba is not None:
                                        fs = min(fs, ba.free_size())
                            except Exception:
                                pass
                            if fs < 512:
                                inst = h.drain()
                        if dma:
                            inst.then_inc(ssems[stream], 16)
                        elif key in awaited:
                            inst.then_inc(sems[eng], 1)
                return run
            for eng in _ENG:
                getattr(blk, _BLK[eng])(mkrun(eng))
        return cnt


def apx(t, col0, dims, part0=0, nparts=128):
    pitch = int(np.prod(t.shape[1:]))
    return bass.AP(t, part0 * pitch + col0, [[pitch, nparts]] + [list(d) for d in dims])


def _seq_mm(h, psum_ap, lhs_list, rhs_list):
    r = None
    n = len(lhs_list)
    for i, (l, rr) in enumerate(zip(lhs_list, rhs_list)):
        r = h.matmul(psum_ap, l, rr, start=(i == 0), stop=(i == n - 1))
    return r


def build_program():
    nc = bass.Bass("TRN2", target_bir_lowering=False, detect_race_conditions=False)

    def din(name, shape, dtype=F32):
        return nc.dram_tensor(name, list(shape), dtype, kind="ExternalInput")

    x_hbm = din("x", [T, DM])
    cvec = din("cvec", [1, DM])
    Wq = din("Wq", [DM, DM]); Wk = din("Wk", [DM, DM]); Wv = din("Wv", [DM, DM])
    Wout = din("Wout", [DM, DM])
    Wbeta = din("Wbeta", [64, DM]); bbeta = din("bbeta", [1, 64])
    Wmix = din("Wmix", [64, DM]); bmix = din("bmix", [1, 64])
    boutv = din("boutv", [1, DM])
    ln_g = din("ln_g", [1, DM]); ln_b = din("ln_b", [1, DM])
    rms_w = din("rms_w", [1, DM])
    gate_w = din("gate_w", [E, DM])
    W1 = din("W1", [E, DM, FF]); b1 = din("b1", [E, FF])
    W2 = din("W2", [E, FF, DM]); b2 = din("b2", [E, DM])
    wmod = din("wmod", [DM, 6 * DM])
    identf = din("identf", [128, 128])
    identb = din("identb", [128, 128], BF16)
    maskb = din("maskb", [128, 128], BF16)
    cosTd = din("cosT", [T, 32])
    sinTd = din("sinT", [T, 32])
    cosTl = din("cosTl", [T, 32])
    sinTl = din("sinTl", [T, 32])
    Lsel = din("Lsel", [64, 128])
    Rsel = din("Rsel", [64, 16])
    onesb = din("onesb", [1, 128], BF16)
    fconst = din("fconst", [1, 16])

    out = nc.dram_tensor("out", [T, DM], F32, kind="ExternalOutput")
    sk = "ExternalOutput" if DEBUG else "Internal"
    h1sp = nc.dram_tensor("h1sp", [128, NT * DM], F32, kind=sk)
    xsp = nc.dram_tensor("xsp", [128, NT * DM], F32, kind=sk)
    phiqsp = nc.dram_tensor("phiqsp", [128, 8 * T], BF16, kind=sk)
    ktmsp = nc.dram_tensor("ktmsp", [128, NT * DM], BF16, kind=sk)
    vaugsp = nc.dram_tensor("vaugsp", [128, NT * VA], BF16, kind=sk)
    uacc = nc.dram_tensor("uacc", [128, NT * DM], F32, kind=sk)
    modsp = nc.dram_tensor("modsp", [16, DM], F32, kind=sk)
    if DEBUG:
        d_logb = nc.dram_tensor("d_logb", [64, T], F32, kind="ExternalOutput")
        d_mix = nc.dram_tensor("d_mix", [128, NT * 64], F32, kind="ExternalOutput")
        d_xnt = nc.dram_tensor("d_xnt", [128, 8 * T], BF16, kind="ExternalOutput")
        d_msc = nc.dram_tensor("d_msc", [128, NT * 8], F32, kind="ExternalOutput")
        d_h1t = nc.dram_tensor("d_h1t", [128, 8 * T], BF16, kind="ExternalOutput")

    P = Prog(nc)
    with ExitStack() as top:
        xnT = top.enter_context(nc.sbuf_tensor("xnT", [128, 8 * T], BF16))
        msc = top.enter_context(nc.sbuf_tensor("msc", [128, NT * 8], F32))
        sidf = top.enter_context(nc.sbuf_tensor("sidf", [128, 128], F32))
        sidb = top.enter_context(nc.sbuf_tensor("sidb", [128, 128], BF16))
        smask = top.enter_context(nc.sbuf_tensor("smask", [128, 128], BF16))
        sones = top.enter_context(nc.sbuf_tensor("sones", [1, 128], BF16))
        sgateT = top.enter_context(nc.sbuf_tensor("sgateT", [128, 64], BF16))
        sfc = top.enter_context(nc.sbuf_tensor("sfc", [128, 16], F32))
        P.op("sy", lambda h: h.dma_start(
            sfc[:, :], bass.AP(fconst, 0, [[0, 128], [1, 16]])), key="ld:fc", dma=True)
        P.fence("fc")
        for fi, fv in enumerate(FCONSTS):
            nc.const_aps.register(F32, fv, sfc[:, fi:fi + 1])

        with ExitStack() as mid:
            scosT = mid.enter_context(nc.sbuf_tensor("scosT", [128, NT * 32], F32))
            ssinT = mid.enter_context(nc.sbuf_tensor("ssinT", [128, NT * 32], F32))
            sLsel = mid.enter_context(nc.sbuf_tensor("sLsel", [64, 128], F32))
            sRsel = mid.enter_context(nc.sbuf_tensor("sRsel", [64, 16], F32))
            snbb = mid.enter_context(nc.sbuf_tensor("snbb", [64, 1], F32))
            WoT = mid.enter_context(nc.sbuf_tensor("WoT", [128, 8 * DM], BF16))
            WbT = mid.enter_context(nc.sbuf_tensor("WbT", [128, 8 * 64], BF16))
            WmT = mid.enter_context(nc.sbuf_tensor("WmT", [128, 8 * 64], BF16))
            logb = mid.enter_context(nc.sbuf_tensor("logb", [64, T], F32))
            mixa = mid.enter_context(nc.sbuf_tensor("mixa", [128, NT * 64], F32))
            colv = mid.enter_context(nc.sbuf_tensor("colv", [128, 96], F32))
            (CV_CA1, CV_SA, CV_CB1, CV_SB, CV_LGA, CV_LBA, CV_GB, CV_T0,
             CV_GA, CV_LNG, CV_LNB, CV_T1) = [8 * i for i in range(12)]

            def cv(off, w=8):
                return colv[:, off: off + w]

            with ExitStack() as s05:
                WqT = s05.enter_context(nc.sbuf_tensor("WqT", [128, 8 * DM], BF16))
                WkT = s05.enter_context(nc.sbuf_tensor("WkT", [128, 8 * DM], BF16))
                WvT = s05.enter_context(nc.sbuf_tensor("WvT", [128, 8 * DM], BF16))
                h1T = s05.enter_context(nc.sbuf_tensor("h1T", [128, 8 * T], BF16))

                with ExitStack() as p0:
                    wst = p0.enter_context(nc.sbuf_tensor("wst", [128, 8 * DM], BF16))
                    wbst = p0.enter_context(nc.sbuf_tensor("wbst", [128, DM], BF16))
                    wmst = p0.enter_context(nc.sbuf_tensor("wmst", [128, DM], BF16))
                    wgst = p0.enter_context(nc.sbuf_tensor("wgst", [128, DM], BF16))
                    modst = [p0.enter_context(nc.sbuf_tensor(f"modst{i}", [128, DM], BF16))
                             for i in range(2)]
                    tlo = p0.enter_context(nc.sbuf_tensor("tlo", [128, 1024], F32))
                    smodT = p0.enter_context(nc.sbuf_tensor("smodT", [128, 48], F32))
                    scv = p0.enter_context(nc.sbuf_tensor("scv", [128, 8], F32))
                    srow8 = p0.enter_context(nc.sbuf_tensor("srow8", [128, 128], F32))
                    scvb = p0.enter_context(nc.sbuf_tensor("scvb", [128, 8], BF16))
                    sbbv = p0.enter_context(nc.sbuf_tensor("sbbv", [64, 1], F32))
                    sbmc = p0.enter_context(nc.sbuf_tensor("sbmc", [32, 2], F32))
                    pmod = p0.enter_context(nc.psum_tensor("pmod", [128, 512], F32))
                    ptb = [p0.enter_context(nc.psum_tensor(f"p0tb{i}", [128, 128], BF16))
                           for i in range(2)]

                    for nm, dst, src in (
                        ("idf", sidf[:, :], identf[:, :]),
                        ("idb", sidb[:, :], identb[:, :]),
                        ("msk", smask[:, :], maskb[:, :]),
                        ("one", sones[:, :], onesb[:, :]),
                        ("lsl", sLsel[:, :], Lsel[:, :]),
                        ("rsl", sRsel[:, :], Rsel[:, :]),
                        ("cosT", scosT[:, :].rearrange("p (t f) -> p t f", f=32),
                         bass.AP(cosTd, 0, [[32, 128], [32 * 128, NT], [1, 32]])),
                        ("sinT", ssinT[:, :].rearrange("p (t f) -> p t f", f=32),
                         bass.AP(sinTd, 0, [[32, 128], [32 * 128, NT], [1, 32]])),
                        ("cosTl", tlo[:, 0:512].rearrange("p (t f) -> p t f", f=32),
                         bass.AP(cosTl, 0, [[32, 128], [32 * 128, NT], [1, 32]])),
                        ("sinTl", tlo[:, 512:1024].rearrange("p (t f) -> p t f", f=32),
                         bass.AP(sinTl, 0, [[32, 128], [32 * 128, NT], [1, 32]])),
                        ("bb", sbbv[:, :], bass.AP(bbeta, 0, [[1, 64], [1, 1]])),
                        ("bmc", sbmc[:, :], bass.AP(bmix, 0, [[2, 32], [1, 2]])),
                    ):
                        P.op("sy", (lambda d, s: lambda h: h.dma_start(d, s))(dst, src),
                             key=f"ld:{nm}", dma=True)

                    P.op("dv", lambda h: h.tensor_add(scosT[:, :], scosT[:, :],
                                                      tlo[:, 0:512]),
                         after=["ld:cosT", "ld:cosTl"], key="cosfix")
                    P.op("dv", lambda h: h.tensor_add(ssinT[:, :], ssinT[:, :],
                                                      tlo[:, 512:1024]),
                         after=["ld:sinT", "ld:sinTl"], key="sinfix")
                    def w_src(W, nch, wid):
                        return bass.AP(W, 0, [[wid, 128], [128 * wid, nch], [1, wid]])

                    for wi, (W, dstT) in enumerate(((Wq, WqT), (Wk, WkT), (Wv, WvT), (Wout, WoT))):
                        after = [f"wtr:{wi - 1}:63"] if wi else []
                        P.op("gp", (lambda W=W: lambda h: h.dma_start(
                            wst[:, :].rearrange("p (r d) -> p r d", d=DM), w_src(W, 8, DM)))(),
                            after=after, key=f"wld:{wi}", dma=True)
                        for i in range(64):
                            r, dc = i // 8, i % 8
                            pt = ptb[i % 2]
                            aft = [f"wld:{wi}", "ld:idb"]
                            if i >= 2:
                                aft.append(f"wev:{wi}:{i - 2}")
                            elif wi > 0:
                                aft.append(f"wev:{wi - 1}:{62 + i}")
                            P.op("te", (lambda pt=pt, r=r, dc=dc: lambda h: h.transpose(
                                pt[:, :], wst[:, DM * r + 128 * dc: DM * r + 128 * (dc + 1)],
                                sidb[:, :]))(),
                                after=aft, key=f"wtr:{wi}:{i}")
                            P.op("dv", (lambda pt=pt, dstT=dstT, r=r, dc=dc: lambda h: h.tensor_copy(
                                dstT[:, DM * dc + 128 * r: DM * dc + 128 * (r + 1)], pt[:, :]))(),
                                after=[f"wtr:{wi}:{i}"], key=f"wev:{wi}:{i}")
                    P.op("dv", lambda h: h.memset(wbst[:, :], 0.0), key="wz:b")
                    P.op("dv", lambda h: h.memset(wmst[:, :], 0.0), key="wz:m")
                    P.op("dv", lambda h: h.memset(wgst[:, :], 0.0), key="wz:g")
                    P.op("gp", lambda h: h.dma_start(wbst[0:64, :], Wbeta[:, :]), after=["wz:b"], key="wld:b", dma=True)
                    P.op("gp", lambda h: h.dma_start(wmst[0:64, :], Wmix[:, :]), after=["wz:m"], key="wld:m", dma=True)
                    P.op("gp", lambda h: h.dma_start(wgst[0:8, :], gate_w[:, :]), after=["wz:g"], key="wld:g", dma=True)
                    for si, (st, rows, dstT, wd, ldk) in enumerate((
                            (wbst, 64, WbT, 64, "wld:b"), (wmst, 64, WmT, 64, "wld:m"),
                            (wgst, 8, sgateT, 8, "wld:g"))):
                        for dc in range(8):
                            i = si * 8 + dc
                            pt = ptb[i % 2]
                            aft = [ldk, "wev:3:63", "wz:b", "wz:m", "wz:g"]
                            if i >= 2:
                                aft.append(f"swev:{i - 2}")
                            P.op("te", (lambda pt=pt, st=st, dc=dc: lambda h: h.transpose(
                                pt[:, :], st[:, 128 * dc: 128 * (dc + 1)],
                                sidb[:, :]))(),
                                after=aft, key=f"swtr:{i}")
                            P.op("dv", (lambda pt=pt, dstT=dstT, wd=wd, dc=dc, rows=rows:
                                        lambda h: h.tensor_copy(
                                dstT[:, wd * dc: wd * (dc + 1)], pt[0:128, 0:rows]))(),
                                after=[f"swtr:{i}"], key=f"swev:{i}")

                    P.op("dv", lambda h: h.memset(srow8[:, :], 0.0), key="r8:z")
                    prevk = []
                    for nm, dsrc, dst_ap in (
                            ("cv", cvec, scv[:, :]),
                            ("lngc", ln_g, apx(colv, CV_LNG, [[1, 8]])),
                            ("lnbc", ln_b, apx(colv, CV_LNB, [[1, 8]]))):
                        P.op("sy", (lambda dsrc=dsrc: lambda h: h.dma_start(
                            srow8[0:8, :], bass.AP(dsrc, 0, [[128, 8], [1, 128]])))(),
                            after=["r8:z"] + [k for k in prevk if k.startswith("cvt")],
                            key=f"rld:{nm}", dma=True)
                        P.op("te", (lambda: lambda h: h.transpose(
                            pmod[:, 128:256], srow8[:, :], sidf[:, :]))(),
                            after=[f"rld:{nm}", "ld:idf"] +
                                  [k for k in prevk if k.startswith("cve")],
                            key=f"cvt:{nm}")
                        P.op("ac", (lambda dst_ap=dst_ap: lambda h: h.copy(
                            dst_ap, pmod[:, 128:136]))(),
                            after=[f"cvt:{nm}"], key=f"cve:{nm}")
                        prevk = [f"cvt:{nm}", f"cve:{nm}"]
                    P.op("ac", lambda h: h.activation(scv[:, :], scv[:, :], AF.Silu),
                         after=["cve:cv"], key="silc")
                    P.op("dv", lambda h: h.tensor_copy(scvb[:, :], scv[:, :]),
                         after=["silc"], key="silcb")
                    P.op("ac", lambda h: h.mul(snbb[:, :], sbbv[:, :], -1.0),
                         after=["ld:bb"], key="nbb")
                    for jt in range(48):
                        mst = modst[jt % 2]
                        aft = [] if jt < 2 else [f"modmm:{jt - 2}"]
                        P.op("gp", (lambda jt=jt, mst=mst: lambda h: h.dma_start(
                            mst[:, :].rearrange("p (kc j) -> p kc j", j=128),
                            bass.AP(wmod, 128 * jt, [[6 * DM, 128], [128 * 6 * DM, 8], [1, 128]])))(),
                            after=aft, key=f"modld:{jt}", dma=True, stream=f"mod{jt % 2}")
                        P.op("te", (lambda mst=mst: lambda h: _seq_mm(
                            h, pmod[:, 0:1],
                            [mst[:, 128 * kc: 128 * (kc + 1)] for kc in range(8)],
                            [scvb[:, kc:kc + 1] for kc in range(8)]))(),
                            after=[f"modld:{jt}", "silcb"] +
                                  ([f"modev:{jt - 1}"] if jt else ["cve:lnbc"]),
                            key=f"modmm:{jt}")
                        P.op("ac", (lambda jt=jt: lambda h: h.copy(
                            smodT[:, jt:jt + 1], pmod[:, 0:1]))(),
                            after=[f"modmm:{jt}"], key=f"modev:{jt}")

                    def mTn(i):
                        return smodT[:, 8 * i: 8 * (i + 1)]
                    P.op("dv", lambda h: h.tensor_scalar_add(cv(CV_CA1), mTn(1), 1.0),
                         after=["modev:47"], key="v:ca1")
                    P.op("dv", lambda h: h.tensor_copy(cv(CV_SA), mTn(0)), key="v:sa")
                    P.op("dv", lambda h: h.tensor_scalar_add(cv(CV_CB1), mTn(4), 1.0), key="v:cb1")
                    P.op("dv", lambda h: h.tensor_copy(cv(CV_SB), mTn(3)), key="v:sb")
                    for gi, (src_i, dst) in enumerate(((2, CV_GA), (5, CV_GB))):
                        P.op("dv", (lambda src_i=src_i: lambda h: h.tensor_single_scalar(
                            cv(CV_T0), mTn(src_i), 0.0, ALU.is_ge))())
                        P.op("dv", lambda h: h.tensor_scalar(
                            cv(CV_T0), cv(CV_T0), 2.0, -1.0, ALU.mult, ALU.add))
                        P.op("dv", (lambda src_i=src_i: lambda h: h.tensor_scalar_mul(
                            cv(CV_T1), mTn(src_i), -1.0))())
                        P.op("dv", (lambda src_i=src_i: lambda h: h.tensor_tensor(
                            cv(CV_T1), cv(CV_T1), mTn(src_i), ALU.max))())
                        P.op("dv", (lambda gi=gi: lambda h: h.tensor_scalar_max(
                            cv(CV_T1), cv(CV_T1), 0.25))())
                        P.op("dv", (lambda dst=dst: lambda h: h.tensor_mul(
                            cv(dst), cv(CV_T0), cv(CV_T1)))(), key=f"g:sel:{gi}")
                    P.op("dv", lambda h: h.tensor_mul(cv(CV_LGA), cv(CV_LNG), cv(CV_GA)),
                         after=["g:sel:0", "cve:lngc"])
                    P.op("dv", lambda h: h.tensor_mul(cv(CV_LBA), cv(CV_LNB), cv(CV_GA)),
                         after=["cve:lnbc"])
                    P.op("dv", lambda h: h.tensor_tensor(
                        colv[0:32, CV_T0:CV_T0 + 1], sbmc[0:32, 0:1], sbmc[0:32, 1:2],
                        ALU.subtract), after=["ld:bmc"], key="v:dbm")

                    P.op("te", lambda h: h.transpose(
                        pmod[0:96, 128:256], colv[:, 0:96], sidf[:, :]),
                        after=["v:dbm", "modev:47"], key="cvt:sp")
                    P.op("ac", lambda h: h.copy(srow8[0:96, :], pmod[0:96, 128:256]),
                         after=["cvt:sp"], key="cve:sp")
                    P.op("gp", lambda h: h.dma_start(
                        bass.AP(modsp, 0, [[128, 96], [1, 128]]), srow8[0:96, :]),
                        after=["cve:sp"], key="sp:mod1", dma=True)
                    P.fence("p0")

                with ExitStack() as p1:
                    ca1x = p1.enter_context(nc.sbuf_tensor("ca1x", [128, DM], F32))
                    sax = p1.enter_context(nc.sbuf_tensor("sax", [128, DM], F32))
                    dbmx = p1.enter_context(nc.sbuf_tensor("dbmx", [128, 32], F32))
                    x_st = [p1.enter_context(nc.sbuf_tensor(f"x_st{i}", [128, DM], F32))
                            for i in range(2)]
                    h1_st = [p1.enter_context(nc.sbuf_tensor(f"h1_st{i}", [128, DM], F32))
                             for i in range(2)]
                    sstat = p1.enter_context(nc.sbuf_tensor("sstat", [128, 16], F32))
                    smv = p1.enter_context(nc.sbuf_tensor("smv", [128, 8], F32))
                    kraw1 = p1.enter_context(nc.sbuf_tensor("kraw1", [128, DM], F32))
                    krop = p1.enter_context(nc.sbuf_tensor("krop", [128, DM], F32))
                    krop2 = p1.enter_context(nc.sbuf_tensor("krop2", [128, DM], F32))
                    kbf = [p1.enter_context(nc.sbuf_tensor(f"kbf{i}", [128, DM], BF16))
                           for i in range(2)]
                    va_st = [p1.enter_context(nc.sbuf_tensor(f"va_st{i}", [128, VA], BF16))
                             for i in range(2)]
                    qtm1 = p1.enter_context(nc.sbuf_tensor("qtm1", [128, DM], BF16))
                    qfm1 = p1.enter_context(nc.sbuf_tensor("qfm1", [128, DM], BF16))
                    bst = p1.enter_context(nc.sbuf_tensor("bst", [64, 512], F32))
                    bst2 = p1.enter_context(nc.sbuf_tensor("bst2", [64, 512], F32))
                    mst1 = p1.enter_context(nc.sbuf_tensor("mst1", [128, 32], F32))
                    smix64 = p1.enter_context(nc.sbuf_tensor("smix64", [128, 64], F32))
                    mst2 = p1.enter_context(nc.sbuf_tensor("mst2", [128, 32], F32))
                    pp = [p1.enter_context(nc.psum_tensor(f"pp{i}", [128, 512], F32))
                          for i in range(2)]
                    ptf = [p1.enter_context(nc.psum_tensor(f"ptf{i}", [128, 128], F32))
                           for i in range(2)]
                    ptb1 = [p1.enter_context(nc.psum_tensor(f"p1tb{i}", [128, 128], BF16))
                            for i in range(2)]

                    def cvread(nm, dsttile, vi, deps=("sp:mod1",)):
                        P.op("sy", (lambda dsttile=dsttile, vi=vi: lambda h: h.dma_start(
                            dsttile[:, :],
                            bass.AP(modsp, vi * DM, [[0, 128], [1, DM]])))(),
                            after=deps, key=f"bc:{nm}", dma=True)
                    cvread("ca1", ca1x, 0)
                    cvread("sa", sax, 1)
                    P.op("sy", lambda h: h.dma_start(
                        dbmx[:, :], bass.AP(modsp, 7 * DM, [[0, 128], [1, 32]])),
                        after=["sp:mod1"], key="bc:dbm", dma=True)

                    for t in range(NT):
                        xs, hs = x_st[t % 2], h1_st[t % 2]
                        aft = [] if t < 2 else [f"h1:{t - 2}", f"h1tr:{t - 2}:7",
                                                      f"sp:h1:{t - 2}"]
                        P.op("sy", (lambda t=t, xs=xs: lambda h: h.dma_start(
                            xs[:, :], x_hbm[128 * t: 128 * (t + 1), :]))(),
                            after=aft, key=f"ld:x:{t}", dma=True, stream=f"x{t % 2}")
                        P.op("dv", (lambda xs=xs: lambda h: h.bn_stats(
                            sstat[:, 0:6], xs[:, 0:512]))(), after=[f"ld:x:{t}"])
                        P.op("dv", (lambda xs=xs: lambda h: h.bn_stats(
                            sstat[:, 6:12], xs[:, 512:1024]))())
                        P.op("dv", lambda h: h.bn_aggr(smv[:, 0:2], sstat[:, 0:12]),
                             key=f"h1s:{t}")
                        P.op("ac", lambda h: h.activation(smv[:, 2:3], smv[:, 1:2], AF.Ln,
                                                          bias=LN_EPS), after=[f"h1s:{t}"])
                        P.op("ac", lambda h: h.activation(smv[:, 3:4], smv[:, 2:3], AF.Exp,
                                                          scale=-0.5), key=f"h1rs:{t}")
                        P.op("dv", (lambda xs=xs, hs=hs: lambda h: h.tensor_scalar(
                            hs[:, :], xs[:, :], smv[:, 0:1], smv[:, 3:4],
                            ALU.subtract, ALU.mult))(),
                            after=[f"h1rs:{t}"] + ([f"h1tr:{t - 2}:7", f"sp:h1:{t - 2}"]
                                                   if t >= 2 else []))
                        P.op("dv", (lambda hs=hs: lambda h: h.tensor_mul(
                            hs[:, :], hs[:, :], ca1x[:, :]))(), after=["bc:ca1"])
                        P.op("dv", (lambda hs=hs: lambda h: h.tensor_add(
                            hs[:, :], hs[:, :], sax[:, :]))(), after=["bc:sa"], key=f"h1:{t}")
                        P.op("gp", (lambda t=t, hs=hs: lambda h: h.dma_start(
                            h1sp[:, DM * t: DM * (t + 1)], hs[:, :]))(),
                            after=[f"h1:{t}"], key=f"sp:h1:{t}", dma=True)
                        for b in range(8):
                            pt = ptf[b % 2]
                            aft = [f"h1:{t}", "ld:idf"]
                            if t * 8 + b >= 2:
                                pb, pbb = divmod(t * 8 + b - 2, 8)
                                aft.append(f"h1te:{pb}:{pbb}")
                            P.op("te", (lambda pt=pt, hs=hs, b=b: lambda h: h.transpose(
                                pt[:, :], hs[:, 128 * b: 128 * (b + 1)], sidf[:, :]))(),
                                after=aft, key=f"h1tr:{t}:{b}")
                            P.op("ac", (lambda pt=pt, t=t, b=b: lambda h: h.copy(
                                h1T[:, T * b + 128 * t: T * b + 128 * (t + 1)], pt[:, :]))(),
                                after=[f"h1tr:{t}:{b}"], key=f"h1te:{t}:{b}")
                    P.fence("p05")
                    if DEBUG:
                        P.op("gp", lambda h: h.dma_start(d_h1t[:, :], h1T[:, :]),
                             key="dbg:h1t", dma=True)

                    for n in range(4):
                        ppn = pp[n % 2]
                        aft = [] if n < 2 else [f"bt:ev:{n - 2}"]
                        P.op("te", (lambda n=n, ppn=ppn: lambda h: _seq_mm(
                            h, ppn[0:64, 0:512],
                            [WbT[:, 64 * kc: 64 * (kc + 1)] for kc in range(8)],
                            [h1T[:, T * kc + 512 * n: T * kc + 512 * (n + 1)] for kc in range(8)]))(),
                            after=aft, key=f"bt:mm:{n}")
                        P.op("ac", (lambda ppn=ppn: lambda h: h.activation(
                            bst[0:64, 0:512], ppn[0:64, 0:512], AF.Exp,
                            bias=snbb[:, :], scale=-1.0))(),
                            after=[f"bt:mm:{n}", "nbb"] +
                                  ([f"bt:cl:{n - 1}"] if n else []), key=f"bt:ev:{n}")
                        P.op("dv", lambda h: h.tensor_scalar_add(bst2[0:64, :], bst[0:64, :], 1.0),
                             after=[f"bt:ev:{n}"])
                        P.op("dv", lambda h: h.reciprocal(bst[0:64, :], bst2[0:64, :]))
                        P.op("dv", lambda h: h.tensor_scalar(
                            bst2[0:64, :], bst[0:64, :], 0.85, 0.9995, ALU.max, ALU.min),
                            key=f"bt:cl:{n}")
                        P.op("ac", (lambda n=n: lambda h: h.activation(
                            logb[:, 512 * n: 512 * (n + 1)], bst2[0:64, :], AF.Ln))(),
                            after=[f"bt:cl:{n}"], key=f"bt:ln:{n}")
                    if DEBUG:
                        P.op("gp", lambda h: h.dma_start(d_logb[:, :], logb[:, :]),
                             after=[f"bt:ln:{n}" for n in range(4)], key="dbg:logb", dma=True)

                    def proj_mm(t, n, ppn, WT):
                        def f(h):
                            return _seq_mm(
                                h, ppn[:, 0:512],
                                [h1T[:, T * kc + 128 * t: T * kc + 128 * (t + 1)] for kc in range(8)],
                                [WT[:, DM * kc + 512 * n: DM * kc + 512 * (n + 1)] for kc in range(8)])
                        return f

                    for t in range(NT):
                        ppn = pp[0]
                        aft = (["bt:ev:2", "bt:ev:3"] if t == 0
                               else [f"mx:w:{t - 1}", f"qph:{t - 1}"])
                        P.op("te", (lambda t=t, ppn=ppn: lambda h: _seq_mm(
                            h, ppn[:, 0:64],
                            [h1T[:, T * kc + 128 * t: T * kc + 128 * (t + 1)] for kc in range(8)],
                            [WmT[:, 64 * kc: 64 * (kc + 1)] for kc in range(8)]))(),
                            after=aft, key=f"mx:mm:{t}")
                        P.op("ac", (lambda ppn=ppn: lambda h: h.copy(
                            smix64[:, :], ppn[:, 0:64]))(),
                            after=[f"mx:mm:{t}"] + ([f"mx:d:{t - 1}"] if t else []),
                            key=f"mx:ev:{t}")
                        P.op("dv", (lambda: lambda h: h.tensor_tensor(
                            mst1[:, :], apx(smix64, 0, [[2, 32]]), apx(smix64, 1, [[2, 32]]),
                            ALU.subtract))(),
                            after=[f"mx:ev:{t}"] + ([f"mx:e:{t - 1}"] if t else []),
                            key=f"mx:d:{t}")
                        P.op("dv", lambda h: h.tensor_add(mst1[:, :], mst1[:, :], dbmx[:, :]),
                             after=["bc:dbm"], key=f"mx:d2:{t}")
                        P.op("ac", lambda h: h.activation(mst2[:, :], mst1[:, :], AF.Exp,
                                                          scale=-1.0),
                             after=[f"mx:d2:{t}"], key=f"mx:e:{t}")
                        P.op("dv", lambda h: h.tensor_scalar_add(mst1[:, :], mst2[:, :], 1.0),
                             after=[f"mx:e:{t}"])
                        P.op("dv", (lambda t=t: lambda h: h.reciprocal(
                            apx(mixa, 64 * t, [[2, 32]]), mst1[:, :]))())
                        P.op("dv", (lambda t=t: lambda h: h.tensor_scalar(
                            apx(mixa, 64 * t + 1, [[2, 32]]), apx(mixa, 64 * t, [[2, 32]]),
                            -1.0, 1.0, ALU.mult, ALU.add))(), key=f"mx:w:{t}")

                        va = va_st[t % 2]
                        for n in range(2):
                            ppn = pp[1] if n == 0 else pp[0]
                            aft = ([f"mx:d:{t}"] if n == 1 else
                                   ([f"qph:{t - 1}"] if t else []) +
                                   ([f"sp:va:{t - 2}"] if t >= 2 else []))
                            P.op("te", proj_mm(t, n, ppn, WvT), after=aft, key=f"v:mm:{t}:{n}")
                            P.op("ac", (lambda va=va, n=n, ppn=ppn: lambda h: h.activation(
                                apx(va, 33 * 16 * n, [[33, 16], [1, 32]]), ppn[:, 0:512],
                                AF.Copy))(), after=[f"v:mm:{t}:{n}"], key=f"v:ev:{t}:{n}")
                        P.op("dv", (lambda va=va: lambda h: h.memset(
                            apx(va, 32, [[33, 32]]), 1.0))(),
                            after=[f"v:ev:{t}:0", f"v:ev:{t}:1"], key=f"v:on:{t}")
                        P.op("gp", (lambda t=t, va=va: lambda h: h.dma_start(
                            vaugsp[:, VA * t: VA * (t + 1)], va[:, :]))(),
                            after=[f"v:on:{t}"], key=f"sp:va:{t}", dma=True)

                        for which, WT in (("k", WkT), ("q", WqT)):
                            kr = kraw1
                            prev_done = f"qph:{t - 1}" if t >= 1 else None
                            for n in range(2):
                                ppn = pp[n]
                                aft = []
                                if which == "k" and n == 0:
                                    aft = [f"v:ev:{t}:1"]
                                if which == "q" and n == 0:
                                    aft = [f"kph:{t}"]
                                if prev_done:
                                    aft.append(prev_done)
                                P.op("te", proj_mm(t, n, ppn, WT), after=aft,
                                     key=f"{which}:mm:{t}:{n}")
                                P.op("ac", (lambda kr=kr, n=n, ppn=ppn: lambda h: h.copy(
                                    kr[:, 512 * n: 512 * (n + 1)], ppn[:, 0:512]))(),
                                    after=[f"{which}:mm:{t}:{n}"], key=f"{which}:ev:{t}:{n}")
                            cos_ap = apx(scosT, 32 * t, [[0, 32], [1, 32]])
                            sin_lo = apx(ssinT, 32 * t, [[0, 32], [1, 16]])
                            sin_hi = apx(ssinT, 32 * t + 16, [[0, 32], [1, 16]])
                            P.op("dv", (lambda kr=kr, cos_ap=cos_ap: lambda h: h.tensor_tensor(
                                krop[:, :].rearrange("p (a b) -> p a b", b=32),
                                kr[:, :].rearrange("p (a b) -> p a b", b=32), cos_ap, ALU.mult))(),
                                after=[f"{which}:ev:{t}:0", f"{which}:ev:{t}:1", "cosfix"])
                            P.op("dv", (lambda kr=kr, sin_lo=sin_lo: lambda h: h.tensor_tensor(
                                apx(krop2, 0, [[32, 32], [1, 16]]),
                                apx(kr, 16, [[32, 32], [1, 16]]), sin_lo, ALU.mult))(),
                                after=["sinfix"])
                            P.op("dv", (lambda kr=kr, sin_hi=sin_hi: lambda h: h.tensor_tensor(
                                apx(krop2, 16, [[32, 32], [1, 16]]),
                                apx(kr, 0, [[32, 32], [1, 16]]), sin_hi, ALU.mult))())
                            P.op("dv", lambda h: h.tensor_add(krop[:, :], krop[:, :], krop2[:, :]))
                            P.op("dv", lambda h: h.tensor_scalar_min(krop2[:, :], krop[:, :], 0.0),
                                 key=f"{which}mn:{t}")
                            P.op("ac", lambda h: h.activation(krop2[:, :], krop2[:, :], AF.Exp),
                                 after=[f"{which}mn:{t}"], key=f"{which}ex:{t}")
                            P.op("dv", lambda h: h.tensor_scalar_max(krop[:, :], krop[:, :], 0.0))
                            kb = kbf[t % 2] if which == "k" else qtm1
                            bufw = []
                            if which == "k" and t >= 2:
                                bufw = [f"sp:kt:{t - 2}"]
                            if which == "q" and t >= 1:
                                bufw = [f"qtr:{t - 1}:7"]
                            P.op("dv", (lambda kb=kb: lambda h: h.tensor_add(
                                kb[:, :], krop[:, :], krop2[:, :]))(),
                                after=[f"{which}ex:{t}"] + bufw, key=f"{which}ph:{t}")
                            if which == "k":
                                P.op("gp", (lambda t=t, kb=kb: lambda h: h.dma_start(
                                    ktmsp[:, DM * t: DM * (t + 1)], kb[:, :]))(),
                                    after=[f"kph:{t}"], key=f"sp:kt:{t}", dma=True)
                        qb, qf = qtm1, qfm1
                        for b in range(8):
                            pt = ptb1[b % 2]
                            aft = [f"qph:{t}"]
                            if t * 8 + b >= 2:
                                pb, pbb = divmod(t * 8 + b - 2, 8)
                                aft.append(f"qte:{pb}:{pbb}")
                            if t >= 1 and b == 0:
                                aft.append(f"sp:q:{t - 1}")
                            P.op("te", (lambda pt=pt, qb=qb, b=b: lambda h: h.transpose(
                                pt[:, :], qb[:, 128 * b: 128 * (b + 1)], sidb[:, :]))(),
                                after=aft, key=f"qtr:{t}:{b}")
                            P.op("ac", (lambda pt=pt, qf=qf, b=b: lambda h: h.copy(
                                qf[:, 128 * b: 128 * (b + 1)], pt[:, :]))(),
                                after=[f"qtr:{t}:{b}"], key=f"qte:{t}:{b}")
                        P.op("gp", (lambda t=t, qf=qf: lambda h: h.dma_start(
                            bass.AP(phiqsp, 128 * t, [[8 * T, 128], [T, 8], [1, 128]]),
                            qf[:, :].rearrange("p (b j) -> p b j", j=128)))(),
                            after=[f"qte:{t}:{b}" for b in range(8)], key=f"sp:q:{t}", dma=True)
                    P.fence("p1")

            with ExitStack() as p2:
                cb1x = p2.enter_context(nc.sbuf_tensor("cb1x", [128, DM], F32))
                sbvx = p2.enter_context(nc.sbuf_tensor("sbvx", [128, DM], F32))
                lgax = p2.enter_context(nc.sbuf_tensor("lgax", [128, DM], F32))
                lbax = p2.enter_context(nc.sbuf_tensor("lbax", [128, DM], F32))
                rmswx = p2.enter_context(nc.sbuf_tensor("rmswx", [128, DM], F32))
                boutx = p2.enter_context(nc.sbuf_tensor("boutx", [128, DM], F32))
                phq_st = [p2.enter_context(nc.sbuf_tensor(f"phq{i}", [128, DM], BF16))
                          for i in range(2)]
                ktm_st = [p2.enter_context(nc.sbuf_tensor(f"ktm{i}", [128, DM], BF16))
                          for i in range(2)]
                va2_st = [p2.enter_context(nc.sbuf_tensor(f"va2{i}", [128, VA], BF16))
                          for i in range(2)]
                h12_st = [p2.enter_context(nc.sbuf_tensor(f"h12{i}", [128, DM], F32))
                          for i in range(2)]
                x2_st = [p2.enter_context(nc.sbuf_tensor(f"x2{i}", [128, DM], F32))
                         for i in range(2)]
                sPTall = p2.enter_context(nc.sbuf_tensor("sPTall", [128, NCH * 64], F32))
                sIvTall = p2.enter_context(nc.sbuf_tensor("sIvTall", [128, NCH * 64], BF16))
                sPCtall = p2.enter_context(nc.sbuf_tensor("sPCtall", [128, NCH * 16], F32))
                ktil = [p2.enter_context(nc.sbuf_tensor(f"ktil{i}", [128, DM], BF16))
                        for i in range(2)]
                ktilT = [p2.enter_context(nc.sbuf_tensor(f"ktilT{i}", [128, DM], BF16))
                         for i in range(2)]
                sAt = [p2.enter_context(nc.sbuf_tensor(f"sAt{i}", [128, 512], BF16))
                       for i in range(2)]
                oa = p2.enter_context(nc.sbuf_tensor("oa", [128, 16 * 132], F32))
                sden = p2.enter_context(nc.sbuf_tensor("sden", [128, 64], F32))
                stA = p2.enter_context(nc.sbuf_tensor("stA", [128, 64], F32))
                stB = p2.enter_context(nc.sbuf_tensor("stB", [128, 64], F32))
                srd = p2.enter_context(nc.sbuf_tensor("srd", [128, 64], F32))
                sytmp = p2.enter_context(nc.sbuf_tensor("sytmp", [128, 128], F32))
                sytmp2 = p2.enter_context(nc.sbuf_tensor("sytmp2", [128, DM], F32))
                ych = p2.enter_context(nc.sbuf_tensor("ych", [128, DM], F32))
                yTb = p2.enter_context(nc.sbuf_tensor("yTb", [128, DM], BF16))
                zst = p2.enter_context(nc.sbuf_tensor("zst", [128, DM], F32))
                h2st = p2.enter_context(nc.sbuf_tensor("h2st", [128, DM], F32))
                xnst = p2.enter_context(nc.sbuf_tensor("xnst", [128, DM], BF16))
                szstat = p2.enter_context(nc.sbuf_tensor("szstat", [128, 16], F32))
                szv = p2.enter_context(nc.sbuf_tensor("szv", [128, 8], F32))
                sel8 = p2.enter_context(nc.sbuf_tensor("sel8", [128, 8], F32))
                stmp8 = p2.enter_context(nc.sbuf_tensor("stmp8", [128, 4], F32))
                sZ = p2.enter_context(nc.sbuf_tensor("sZ", [128, 16 * 132], F32))
                sZb = p2.enter_context(nc.sbuf_tensor("sZb", [128, 16 * 132], BF16))
                mu = [p2.enter_context(nc.psum_tensor(f"mu{i}", [128, 512], F32))
                      for i in range(4)]
                pb = [p2.enter_context(nc.psum_tensor(f"pb{i}", [128, 272], F32))
                      for i in range(2)]
                ptb2 = [p2.enter_context(nc.psum_tensor(f"p2tb{i}", [128, 128], BF16))
                        for i in range(2)]

                def cvread2(nm, dsttile, vi, deps=("sp:mod1",)):
                    P.op("sy", (lambda dsttile=dsttile, vi=vi: lambda h: h.dma_start(
                        dsttile[:, :],
                        bass.AP(modsp, vi * DM, [[0, 128], [1, DM]])))(),
                        after=deps, key=f"bc:{nm}", dma=True)
                cvread2("cb1", cb1x, 2)
                cvread2("sb", sbvx, 3)
                cvread2("lga", lgax, 4)
                cvread2("lba", lbax, 5)
                P.op("sy", lambda h: h.dma_start(
                    rmswx[:, :], bass.AP(rms_w, 0, [[0, 128], [1, DM]])),
                    key="ld:rmw", dma=True)
                P.op("sy", lambda h: h.dma_start(
                    boutx[:, :], bass.AP(boutv, 0, [[0, 128], [1, DM]])),
                    key="ld:bov", dma=True)
                P.op("dv", lambda h: h.memset(sZ[:, :], 0.0), key="z:init")
                P.op("dv", lambda h: h.memset(sZb[:, :], 0.0), key="z:init:b")

                with ExitStack() as dp:
                    cumA = dp.enter_context(nc.sbuf_tensor("cumA", [64, T], F32))
                    cumB = dp.enter_context(nc.sbuf_tensor("cumB", [64, T], F32))
                    sPall = dp.enter_context(nc.sbuf_tensor("sPall", [128, T], F32))
                    sPC64a = dp.enter_context(nc.sbuf_tensor("sPC64a", [64, 16], F32))
                    sRvall = dp.enter_context(nc.sbuf_tensor("sRvall", [64, 256], F32))
                    for si, d in enumerate([1, 2, 4, 8, 16, 32, 64]):
                        if si == 0:
                            src = logb
                        else:
                            src = cumB if si % 2 == 1 else cumA
                        dst = cumB if si % 2 == 0 else cumA
                        aft = [f"bt:ln:{n}" for n in range(4)] if si == 0 else []
                        P.op("dv", (lambda src=src, dst=dst, d=d: lambda h: h.tensor_copy(
                            apx(dst, 0, [[128, NCH], [1, d]], 0, 64),
                            apx(src, 0, [[128, NCH], [1, d]], 0, 64)))(), after=aft)
                        P.op("dv", (lambda src=src, dst=dst, d=d: lambda h: h.tensor_tensor(
                            apx(dst, d, [[128, NCH], [1, 128 - d]], 0, 64),
                            apx(src, d, [[128, NCH], [1, 128 - d]], 0, 64),
                            apx(src, 0, [[128, NCH], [1, 128 - d]], 0, 64), ALU.add))(),
                            key=("dp:cum" if si == 6 else None))
                    P.op("dv", lambda h: h.memset(sPall[64:128, :], 0.0), key="dp:pz")
                    P.op("ac", lambda h: h.activation(sPall[0:64, :], cumB[0:64, :],
                                                      AF.Exp),
                         after=["dp:cum", "dp:pz"], key="dp:pexp")
                    P.op("ac", lambda h: h.activation(
                        sPC64a[0:64, :], apx(cumB, 127, [[128, NCH]], 0, 64), AF.Exp),
                        after=["dp:cum"], key="dp:pc64")
                    P.op("dv", lambda h: h.tensor_tensor(
                        apx(sRvall, 0, [[16, NCH], [1, 16]], 0, 64),
                        apx(sRsel, 0, [[0, NCH], [1, 16]], 0, 64),
                        apx(sPC64a, 0, [[1, NCH], [0, 16]], 0, 64), ALU.mult),
                        after=["dp:pc64", "ld:rsl"], key="dp:rv")
                    P.op("te", lambda h: h.matmul(pb[0][:, 0:256], sLsel[0:64, :],
                                                  sRvall[0:64, :], start=True, stop=True),
                         after=["dp:rv", "ld:lsl"], key="dp:pcmm")
                    P.op("ac", lambda h: h.copy(sPCtall[:, :], pb[0][:, 0:256]),
                         after=["dp:pcmm"], key="dp:pctev")
                    for k in range(4):
                        for j in range(4):
                            c = 4 * k + j
                            P.op("te", (lambda k=k, j=j, c=c: lambda h: h.transpose(
                                mu[k][:, 128 * j: 128 * (j + 1)],
                                sPall[:, 128 * c: 128 * (c + 1)], sidf[:, :]))(),
                                after=["dp:pexp"], key=f"dp:ptr:{c}")
                    for k in range(4):
                        P.op("ac", (lambda k=k: lambda h: h.copy(
                            sPTall[:, 256 * k: 256 * (k + 1)],
                            apx(mu[k], 0, [[128, 4], [1, 64]])))(),
                            after=[f"dp:ptr:{4 * k + 3}"], key=f"dp:ptev:{k}")
                    P.op("ac", lambda h: h.activation(sPall[0:64, :], cumB[0:64, :],
                                                      AF.Exp, scale=-1.0),
                         after=["dp:ptr:15"], key="dp:ivexp")
                    for k in range(4):
                        for j in range(4):
                            c = 4 * k + j
                            P.op("te", (lambda k=k, j=j, c=c: lambda h: h.transpose(
                                mu[k][:, 128 * j: 128 * (j + 1)],
                                sPall[:, 128 * c: 128 * (c + 1)], sidf[:, :]))(),
                                after=["dp:ivexp", f"dp:ptev:{k}"], key=f"dp:ivtr:{c}")
                    for k in range(4):
                        P.op("ac", (lambda k=k: lambda h: h.copy(
                            sIvTall[:, 256 * k: 256 * (k + 1)],
                            apx(mu[k], 0, [[128, 4], [1, 64]])))(),
                            after=[f"dp:ivtr:{4 * k + 3}"], key=f"dp:ivtev:{k}")

                for c in range(NCH):
                    dep_done = [f"ch:{c - 2}:done"] if c >= 2 else []
                    P.op("sy", (lambda c=c: lambda h: h.dma_start(
                        phq_st[c % 2][:, :].rearrange("p (b j) -> p b j", j=128),
                        bass.AP(phiqsp, 128 * c, [[8 * T, 128], [T, 8], [1, 128]])))(),
                        after=dep_done + [f"sp:q:{c}"], key=f"l2:phq:{c}", dma=True,
                        stream=f"lphq{c % 2}")
                    P.op("sy", (lambda c=c: lambda h: h.dma_start(
                        ktm_st[c % 2][:, :], ktmsp[:, DM * c: DM * (c + 1)]))(),
                        after=dep_done + [f"sp:kt:{c}"], key=f"l2:ktm:{c}", dma=True,
                        stream=f"lktm{c % 2}")
                    P.op("sy", (lambda c=c: lambda h: h.dma_start(
                        va2_st[c % 2][:, :], vaugsp[:, VA * c: VA * (c + 1)]))(),
                        after=dep_done + [f"sp:va:{c}"], key=f"l2:va:{c}", dma=True,
                        stream=f"lva{c % 2}")
                    P.op("sy", (lambda c=c: lambda h: h.dma_start(
                        h12_st[c % 2][:, :], h1sp[:, DM * c: DM * (c + 1)]))(),
                        after=dep_done + [f"sp:h1:{c}"], key=f"l2:h1:{c}", dma=True,
                        stream=f"lh1{c % 2}")
                    P.op("sy", (lambda c=c: lambda h: h.dma_start(
                        x2_st[c % 2][:, :], x_hbm[128 * c: 128 * (c + 1), :]))(),
                        after=dep_done, key=f"l2:x:{c}", dma=True, stream=f"lx{c % 2}")

                    for m in range(2):
                        P.op("dv", (lambda c=c, m=m: lambda h: h.tensor_tensor(
                            ktil[m][:, :].rearrange("p (a b) -> p a b", b=32),
                            ktm_st[c % 2][:, :].rearrange("p (a b) -> p a b", b=32),
                            apx(sIvTall, 64 * c + m, [[2, 32], [0, 32]]), ALU.mult))(),
                            after=[f"dp:ivtev:{c // 4}", f"l2:ktm:{c}"] +
                                  ([f"u:mm:{c - 1}:{14 + m}"] if c else []),
                            key=f"ktl:{c}:{m}")
                        for b in range(8):
                            i = m * 8 + b
                            pt = ptb2[i % 2]
                            aft = [f"ktl:{c}:{m}"]
                            if c > 0 and i < 2:
                                aft.append(f"xnte:{c - 1}:{6 + i}")
                            if i >= 2:
                                aft.append(f"kte:{c}:{(i - 2) // 8}:{(i - 2) % 8}")
                            P.op("te", (lambda pt=pt, m=m, b=b: lambda h: h.transpose(
                                pt[:, :], ktil[m][:, 128 * b: 128 * (b + 1)], sidb[:, :]))(),
                                after=aft, key=f"ktr:{c}:{m}:{b}")
                            P.op("ac", (lambda pt=pt, m=m, b=b: lambda h: h.copy(
                                ktilT[m][:, 128 * b: 128 * (b + 1)], pt[:, :]))(),
                                after=[f"ktr:{c}:{m}:{b}"], key=f"kte:{c}:{m}:{b}")

                    for g in range(16):
                        H4, m = g // 2, g % 2
                        sl = g % 2
                        aft = [f"kte:{c}:{m}:{H4}", f"l2:phq:{c}"]
                        if g >= 1:
                            aft.append(f"sc:ev:{c}:{g - 1}")
                        elif c > 0:
                            aft.extend([f"lgr:{c - 1}", f"lgel:{c - 1}",
                                        f"zdv:{c - 1}:1"])
                        def scmm(c=c, g=g, H4=H4, m=m):
                            def f(h):
                                r = None
                                for hh in range(4):
                                    r = h.matmul(
                                        mu[hh][:, 0:128],
                                        ktilT[m][32 * hh: 32 * (hh + 1), 128 * H4: 128 * (H4 + 1)],
                                        phq_st[c % 2][32 * hh: 32 * (hh + 1), 128 * H4: 128 * (H4 + 1)],
                                        start=True, stop=True, tile_position=(32 * hh, 0))
                                return r
                            return f
                        P.op("te", scmm(), after=aft, key=f"sc:mm:{c}:{g}")
                        def scev(sl=sl):
                            def f(h):
                                r = None
                                for hh in range(4):
                                    r = h.copy(sAt[sl][:, 128 * hh: 128 * (hh + 1)],
                                               mu[hh][:, 0:128])
                                return r
                            return f
                        P.op("ac", scev(), after=[f"sc:mm:{c}:{g}"] +
                             ([f"oa:mm:{c}:{g - 2}"] if g >= 2 else
                              ([f"oa:mm:{c - 1}:{14 + g}"] if c else [])),
                             key=f"sc:ev:{c}:{g}")
                        P.op("dv", (lambda sl=sl: lambda h: h.tensor_tensor(
                            sAt[sl][:, :].rearrange("p (a b) -> p a b", b=128),
                            sAt[sl][:, :].rearrange("p (a b) -> p a b", b=128),
                            apx(smask, 0, [[0, 4], [1, 128]]), ALU.mult))(),
                            after=[f"sc:ev:{c}:{g}", "ld:msk"], key=f"sc:mk:{c}:{g}")
                        aft = [f"sc:mk:{c}:{g}", f"l2:va:{c}"]
                        aft.append(f"zbf:{c - 1}:{g}" if c else "z:init:b")
                        if g >= 2:
                            aft.append(f"oa:ev:{c}:{g - 2}")
                            aft.append(f"zadd:{c}:{g - 2}")
                        elif c:
                            aft.append(f"oa:ev:{c - 1}:{14 + g}")
                        elif g == 0:
                            aft.append("dp:pctev")
                        def oamm(c=c, g=g, H4=H4, m=m, sl=sl):
                            def f(h):
                                r = None
                                for hh in range(4):
                                    h.matmul(
                                        pb[sl][:, 33 * hh: 33 * (hh + 1)],
                                        sAt[sl][:, 128 * hh: 128 * (hh + 1)],
                                        va2_st[c % 2][:, 33 * (4 * H4 + hh): 33 * (4 * H4 + hh) + 33],
                                        start=True, stop=False)
                                    r = h.matmul(
                                        pb[sl][:, 33 * hh: 33 * (hh + 1)],
                                        phq_st[c % 2][32 * hh: 32 * (hh + 1), 128 * H4: 128 * (H4 + 1)],
                                        sZb[32 * hh: 32 * (hh + 1), 132 * g + 33 * hh: 132 * g + 33 * (hh + 1)],
                                        start=False, stop=True, tile_position=(32 * hh, 0))
                                return r
                            return f
                        P.op("te", oamm(), after=aft, key=f"oa:mm:{c}:{g}")
                        aft = [f"ktl:{c}:{m}", f"l2:va:{c}"]
                        if g >= 2:
                            aft.append(f"zadd:{c}:{g - 2}")
                        elif c:
                            aft.append(f"zadd:{c - 1}:{14 + g}")
                        elif g == 0:
                            aft.append("dp:pctev")
                        P.op("te", (lambda c=c, g=g, H4=H4, m=m, sl=sl: lambda h: h.matmul(
                            pb[sl][:, 132:264], ktil[m][:, 128 * H4: 128 * (H4 + 1)],
                            va2_st[c % 2][:, 33 * 4 * H4: 33 * 4 * H4 + 132],
                            start=True, stop=True))(), after=aft, key=f"u:mm:{c}:{g}")
                        P.op("ac", (lambda c=c, g=g, sl=sl: lambda h: h.copy(
                            oa[:, 132 * g: 132 * (g + 1)], pb[sl][:, 0:132]))(),
                            after=[f"oa:mm:{c}:{g}", f"u:mm:{c}:{g}"] +
                                  ([f"y:{c - 1}"] if (c and g == 0) else []),
                            key=f"oa:ev:{c}:{g}")
                        P.op("dv", (lambda g=g, sl=sl: lambda h: h.tensor_tensor(
                            sZ[:, 132 * g: 132 * (g + 1)], sZ[:, 132 * g: 132 * (g + 1)],
                            pb[sl][:, 132:264], ALU.add))(),
                            after=[f"u:mm:{c}:{g}", "dp:pctev", f"oa:ev:{c}:{g}"] +
                                  (["z:init"] if c == 0 else []))
                        P.op("dv", (lambda c=c, g=g: lambda h: h.tensor_scalar(
                            sZ[:, 132 * g: 132 * (g + 1)], sZ[:, 132 * g: 132 * (g + 1)],
                            sPCtall[:, 16 * c + g: 16 * c + g + 1], None, ALU.mult))(),
                            key=f"zadd:{c}:{g}")
                        P.op("ac", (lambda g=g: lambda h: h.copy(
                            sZb[:, 132 * g: 132 * (g + 1)], sZ[:, 132 * g: 132 * (g + 1)]))(),
                            after=[f"zadd:{c}:{g}"], key=f"zbf:{c}:{g}")

                    PTsel = apx(sPTall, 64 * c, [[8, 8], [1, 2], [2, 4]])
                    mixsel = apx(mixa, 64 * c, [[8, 8], [1, 2], [2, 4]])
                    P.op("dv", lambda h: h.tensor_copy(
                        sden[:, :].rearrange("p (a b) -> p a b", b=4),
                        apx(oa, 32, [[132, 16], [33, 4]])),
                        after=[f"oa:ev:{c}:15", f"dp:ptev:{c // 4}"])
                    P.op("dv", (lambda PTsel=PTsel: lambda h: h.tensor_tensor(
                        stA[:, :].rearrange("p (a b c) -> p a b c", b=2, c=4),
                        sden[:, :].rearrange("p (a b c) -> p a b c", b=2, c=4),
                        PTsel, ALU.mult))())
                    P.op("dv", lambda h: h.tensor_scalar_add(stA[:, :], stA[:, :], ATT_EPS))
                    P.op("dv", lambda h: h.reciprocal(stA[:, :], stA[:, :]))
                    P.op("dv", (lambda PTsel=PTsel, mixsel=mixsel: lambda h: h.tensor_tensor(
                        stB[:, :].rearrange("p (a b c) -> p a b c", b=2, c=4),
                        PTsel, mixsel, ALU.mult))())
                    P.op("dv", lambda h: h.tensor_mul(srd[:, :], stA[:, :], stB[:, :]),
                         key=f"rd:{c}")

                    P.op("dv", (lambda c=c: lambda h: h.tensor_tensor(
                        ych[:, :].rearrange("p (a b c) -> p a b c", b=4, c=32),
                        apx(oa, 0, [[264, 8], [33, 4], [1, 32]]),
                        apx(srd, 0, [[8, 8], [1, 4], [0, 32]]), ALU.mult))(),
                        after=([f"ytr:{c - 1}:7"] if c else []))
                    P.op("dv", lambda h: h.tensor_tensor(
                        sytmp2[:, :].rearrange("p (a b c) -> p a b c", b=4, c=32),
                        apx(oa, 132, [[264, 8], [33, 4], [1, 32]]),
                        apx(srd, 4, [[8, 8], [1, 4], [0, 32]]), ALU.mult))
                    P.op("dv", (lambda c=c: lambda h: h.tensor_add(
                        ych[:, :], ych[:, :], sytmp2[:, :]))(), key=f"y:{c}")

                    for b in range(8):
                        pt = mu[2 + b % 2]
                        aft = [f"y:{c}", f"sc:ev:{c}:15"]
                        if b >= 2:
                            aft.append(f"yte:{c}:{b - 2}")
                        P.op("te", (lambda pt=pt, b=b: lambda h: h.transpose(
                            pt[:, 0:128], ych[:, 128 * b: 128 * (b + 1)], sidf[:, :]))(),
                            after=aft, key=f"ytr:{c}:{b}")
                        P.op("ac", (lambda pt=pt, b=b: lambda h: h.copy(
                            yTb[:, 128 * b: 128 * (b + 1)], pt[:, 0:128]))(),
                            after=[f"ytr:{c}:{b}"], key=f"yte:{c}:{b}")
                    for n in range(2):
                        aft = [f"yte:{c}:7", f"sc:ev:{c}:15"]
                        P.op("te", (lambda n=n: lambda h: _seq_mm(
                            h, mu[n][:, 0:512],
                            [yTb[:, 128 * kb: 128 * (kb + 1)] for kb in range(8)],
                            [WoT[:, DM * kb + 512 * n: DM * kb + 512 * (n + 1)] for kb in range(8)]))(),
                            after=aft, key=f"z:mm:{c}:{n}")
                        P.op("dv", (lambda c=c, n=n: lambda h: h.tensor_tensor(
                            zst[:, 512 * n: 512 * (n + 1)], mu[n][:, 0:512],
                            h12_st[c % 2][:, 512 * n: 512 * (n + 1)], ALU.add))(),
                            after=[f"z:mm:{c}:{n}", f"l2:h1:{c}"])
                        P.op("dv", (lambda n=n: lambda h: h.tensor_add(
                            zst[:, 512 * n: 512 * (n + 1)], zst[:, 512 * n: 512 * (n + 1)],
                            boutx[:, 512 * n: 512 * (n + 1)]))(),
                            after=["ld:bov"], key=f"zdv:{c}:{n}")
                    P.op("dv", lambda h: h.bn_stats(szstat[:, 0:6], zst[:, 0:512]))
                    P.op("dv", lambda h: h.bn_stats(szstat[:, 6:12], zst[:, 512:1024]))
                    P.op("dv", lambda h: h.bn_aggr(szv[:, 0:2], szstat[:, 0:12]),
                         key=f"zs:{c}")
                    P.op("ac", lambda h: h.activation(szv[:, 2:3], szv[:, 1:2], AF.Ln,
                                                      bias=LN_EPS), after=[f"zs:{c}"])
                    P.op("ac", lambda h: h.activation(szv[:, 3:4], szv[:, 2:3], AF.Exp,
                                                      scale=-0.5), key=f"zrs:{c}")
                    P.op("dv", lambda h: h.tensor_scalar(
                        h2st[:, :], zst[:, :], szv[:, 0:1], szv[:, 3:4],
                        ALU.subtract, ALU.mult), after=[f"zrs:{c}"])
                    P.op("dv", lambda h: h.tensor_mul(h2st[:, :], h2st[:, :], lgax[:, :]),
                         after=["bc:lga"])
                    P.op("dv", lambda h: h.tensor_add(h2st[:, :], h2st[:, :], lbax[:, :]),
                         after=["bc:lba"])
                    P.op("dv", (lambda c=c: lambda h: h.tensor_add(
                        x2_st[c % 2][:, :], x2_st[c % 2][:, :], h2st[:, :]))(),
                        after=[f"l2:x:{c}"], key=f"xn:{c}")
                    P.op("gp", (lambda c=c: lambda h: h.dma_start(
                        xsp[:, DM * c: DM * (c + 1)], x2_st[c % 2][:, :]))(),
                        after=[f"xn:{c}"], key=f"sp:x:{c}", dma=True)
                    P.op("dv", (lambda c=c: lambda h: h.bn_stats(
                        szstat[:, 0:6], x2_st[c % 2][:, 0:512]))(), after=[f"xn:{c}"])
                    P.op("dv", (lambda c=c: lambda h: h.bn_stats(
                        szstat[:, 6:12], x2_st[c % 2][:, 512:1024]))())
                    P.op("dv", lambda h: h.bn_aggr(szv[:, 0:2], szstat[:, 0:12]),
                         key=f"xs:{c}")
                    P.op("ac", lambda h: h.activation(szv[:, 2:3], szv[:, 1:2], AF.Ln,
                                                      bias=LN_EPS), after=[f"xs:{c}"])
                    P.op("ac", lambda h: h.activation(szv[:, 3:4], szv[:, 2:3], AF.Exp,
                                                      scale=-0.5), key=f"xrs:{c}")
                    P.op("dv", (lambda c=c: lambda h: h.tensor_scalar(
                        h2st[:, :], x2_st[c % 2][:, :], szv[:, 0:1], szv[:, 3:4],
                        ALU.subtract, ALU.mult))(), after=[f"xrs:{c}"])
                    P.op("dv", lambda h: h.tensor_mul(h2st[:, :], h2st[:, :], cb1x[:, :]),
                         after=["bc:cb1"])
                    P.op("dv", lambda h: h.tensor_add(h2st[:, :], h2st[:, :], sbvx[:, :]),
                         after=["bc:sb"], key=f"h2:{c}")
                    P.op("dv", lambda h: h.bn_stats(szstat[:, 0:6], h2st[:, 0:512]))
                    P.op("dv", lambda h: h.bn_stats(szstat[:, 6:12], h2st[:, 512:1024]))
                    P.op("dv", lambda h: h.bn_aggr(szv[:, 4:6], szstat[:, 0:12]),
                         key=f"h2s:{c}")
                    P.op("ac", lambda h: h.activation(szv[:, 6:7], szv[:, 4:5], AF.Square),
                         after=[f"h2s:{c}"], key=f"h2sq:{c}")
                    P.op("dv", lambda h: h.tensor_add(szv[:, 6:7], szv[:, 6:7], szv[:, 5:6]),
                         after=[f"h2sq:{c}"], key=f"ms:{c}")
                    P.op("ac", lambda h: h.activation(szv[:, 6:7], szv[:, 6:7], AF.Ln,
                                                      bias=RMS_EPS), after=[f"ms:{c}"])
                    P.op("ac", lambda h: h.activation(szv[:, 7:8], szv[:, 6:7], AF.Exp,
                                                      scale=-0.5), key=f"h2rs:{c}")
                    P.op("dv", lambda h: h.tensor_scalar(
                        h2st[:, :], h2st[:, :], szv[:, 7:8], None, ALU.mult),
                        after=[f"h2rs:{c}"])
                    P.op("dv", lambda h: h.tensor_mul(xnst[:, :], h2st[:, :], rmswx[:, :]),
                         after=["ld:rmw"] + ([f"xntr:{c - 1}:7"] if c else []),
                         key=f"xnst:{c}")
                    for b in range(8):
                        pt = ptb2[b % 2]
                        aft = [f"xnst:{c}", f"kte:{c}:1:{6 + (b % 2)}"]
                        if b >= 2:
                            aft.append(f"xnte:{c}:{b - 2}")
                        P.op("te", (lambda pt=pt, b=b: lambda h: h.transpose(
                            pt[:, :], xnst[:, 128 * b: 128 * (b + 1)], sidb[:, :]))(),
                            after=aft, key=f"xntr:{c}:{b}")
                        P.op("ac", (lambda pt=pt, c=c, b=b: lambda h: h.copy(
                            xnT[:, T * b + 128 * c: T * b + 128 * (c + 1)], pt[:, :]))(),
                            after=[f"xntr:{c}:{b}"], key=f"xnte:{c}:{b}")
                    P.op("te", (lambda c=c: lambda h: _seq_mm(
                        h, mu[0][:, 0:8],
                        [xnT[:, T * kb + 128 * c: T * kb + 128 * (c + 1)] for kb in range(8)],
                        [sgateT[:, 8 * kb: 8 * (kb + 1)] for kb in range(8)]))(),
                        after=[f"xnte:{c}:7", f"zdv:{c}:0"], key=f"lg:mm:{c}")
                    P.op("dv", lambda h: h.reduce_max(szv[:, 0:1], mu[0][:, 0:8], AX.X,
                                                      negate=True), after=[f"lg:mm:{c}"])
                    P.op("dv", lambda h: h.reduce_max(szv[:, 1:2], mu[0][:, 0:8], AX.X),
                         key=f"lgmx:{c}")
                    P.op("ac", lambda h: h.activation(sel8[:, :], mu[0][:, 0:8], AF.Exp,
                                                      bias=szv[:, 0:1]),
                         after=[f"lgmx:{c}"], key=f"lgel:{c}")
                    P.op("dv", lambda h: h.reduce_sum(szv[:, 3:4], sel8[:, :], AX.X),
                         after=[f"lgel:{c}"])
                    P.op("dv", lambda h: h.reciprocal(szv[:, 4:5], szv[:, 3:4]),
                         key=f"lgsc:{c}")
                    P.op("dv", (lambda c=c: lambda h: h.tensor_scalar(
                        sel8[:, :], mu[0][:, 0:8], szv[:, 1:2], None, ALU.is_ge))(),
                        after=[f"lgel:{c}", f"lgsc:{c}"], key=f"lgr:{c}")
                    P.op("dv", (lambda c=c: lambda h: h.tensor_scalar(
                        msc[:, 8 * c: 8 * (c + 1)], sel8[:, :], szv[:, 4:5], None,
                        ALU.mult))(), key=f"msc:{c}")
                    P.op("dv", lambda h: h.drain(),
                         after=[f"msc:{c}", f"u:mm:{c}:15", f"zbf:{c}:15", f"sp:x:{c}"],
                         key=f"ch:{c}:done")
                if DEBUG:
                    P.op("gp", lambda h: h.dma_start(d_mix[:, :], mixa[:, :]),
                         after=["ch:15:done"], key="dbg:mix", dma=True)
                    P.op("gp", lambda h: h.dma_start(d_msc[:, :], msc[:, :]),
                         after=["ch:15:done"], key="dbg:msc", dma=True)
                    P.op("gp", lambda h: h.dma_start(d_xnt[:, :], xnT[:, :]),
                         after=["ch:15:done"], key="dbg:xnt", dma=True)
                P.fence("p2")

        with ExitStack() as p3:
            W1bf = p3.enter_context(nc.sbuf_tensor("W1bf", [128, 8 * FF], BF16))
            W2bf = p3.enter_context(nc.sbuf_tensor("W2bf", [128, 32 * DM], BF16))
            hTb = p3.enter_context(nc.sbuf_tensor("hTb", [128, 32 * 512], BF16))
            b1c = p3.enter_context(nc.sbuf_tensor("b1c", [128, 32], F32))
            b1row = p3.enter_context(nc.sbuf_tensor("b1row", [128, 128], F32))
            b2bf = p3.enter_context(nc.sbuf_tensor("b2bf", [1, DM], BF16))
            ctb = [p3.enter_context(nc.sbuf_tensor(f"ctb{i}", [128, 512], F32))
                   for i in range(2)]
            pm1 = [p3.enter_context(nc.psum_tensor(f"pm1{i}", [128, 512], F32))
                   for i in range(2)]
            pm2 = [p3.enter_context(nc.psum_tensor(f"pm2{i}", [128, 512], F32))
                   for i in range(2)]

            uac_keys = []
            P.op("dv", lambda h: h.memset(b1row[:, :], 0.0), key="b1r:z")
            for e in range(E):
                aft1 = [f"m1:{e - 1}:3:31"] if e else []
                P.op("gp", (lambda e=e: lambda h: h.dma_start(
                    W1bf[:, :].rearrange("p (r f) -> p r f", f=FF),
                    bass.AP(W1, e * DM * FF, [[FF, 128], [128 * FF, 8], [1, FF]])))(),
                    after=aft1, key=f"w1ld:{e}", dma=True)
                aft2 = [f"m2:{e - 1}:3:3:1"] if e else []
                P.op("gp", (lambda e=e: lambda h: h.dma_start(
                    W2bf[:, :].rearrange("p (r d) -> p r d", d=DM),
                    bass.AP(W2, e * FF * DM, [[DM, 128], [128 * DM, 32], [1, DM]])))(),
                    after=aft2, key=f"w2ld:{e}", dma=True)
                P.op("sy", (lambda e=e: lambda h: h.dma_start(
                    b1row[0:32, :], bass.AP(b1, e * FF, [[128, 32], [1, 128]])))(),
                    after=([f"b1tr:{e - 1}"] if e else ["b1r:z"]), key=f"b1ld:{e}", dma=True)
                P.op("te", (lambda e=e: lambda h: h.transpose(
                    pm1[0][:, 0:128], b1row[:, :], sidf[:, :]))(),
                    after=[f"b1ld:{e}"] + ([f"m1ev:{e - 1}:3:31"] if e else []),
                    key=f"b1tr:{e}")
                P.op("ac", (lambda e=e: lambda h: h.copy(b1c[:, :], pm1[0][:, 0:32]))(),
                    after=[f"b1tr:{e}"], key=f"b1ev:{e}")
                P.op("gp", (lambda e=e: lambda h: h.dma_start(
                    b2bf[0:1, :], bass.AP(b2, e * DM, [[DM, 1], [1, DM]])))(),
                    after=aft2, key=f"b2ld:{e}", dma=True)
                for q in range(4):
                    for mf in range(32):
                        sl = mf % 2
                        aft = [f"w1ld:{e}"]
                        if q == 0 and mf == 0:
                            aft.append(f"b1ev:{e}")
                        if mf >= 2:
                            aft.append(f"m1ev:{e}:{q}:{mf - 2}")
                        elif q > 0:
                            aft.append(f"m1ev:{e}:{q - 1}:{30 + mf}")
                        elif e > 0:
                            aft.append(f"m1ev:{e - 1}:3:{30 + mf}")
                        P.op("te", (lambda e=e, q=q, mf=mf, sl=sl: lambda h: _seq_mm(
                            h, pm1[sl][:, 0:512],
                            [W1bf[:, FF * kc + 128 * mf: FF * kc + 128 * (mf + 1)]
                             for kc in range(8)],
                            [xnT[:, T * kc + 512 * q: T * kc + 512 * (q + 1)]
                             for kc in range(8)]))(),
                            after=aft, key=f"m1:{e}:{q}:{mf}")
                        P.op("ac", (lambda mf=mf, sl=sl: lambda h: h.activation(
                            hTb[:, 512 * mf: 512 * (mf + 1)], pm1[sl][:, 0:512],
                            AF.Silu, bias=b1c[:, mf:mf + 1]))(),
                            after=[f"m1:{e}:{q}:{mf}", f"b1ev:{e}"],
                            key=f"m1ev:{e}:{q}:{mf}")
                    for tt4 in range(4):
                        tau = 4 * q + tt4
                        for n in range(2):
                            idx = tt4 * 2 + n
                            sl = idx % 2
                            aft = [f"m1ev:{e}:{q}:31", f"w2ld:{e}", f"b2ld:{e}"]
                            if idx >= 2:
                                aft.append(f"m2ev:{e}:{q}:{tt4 - 1}:{n}")
                            elif q > 0:
                                aft.append(f"m2ev:{e}:{q - 1}:3:{n}")
                            elif e > 0:
                                aft.append(f"m2ev:{e - 1}:3:3:{n}")
                            def m2f(e=e, q=q, tt4=tt4, n=n, sl=sl):
                                def f(h):
                                    for kc in range(32):
                                        h.matmul(pm2[sl][:, 0:512],
                                                 hTb[:, 512 * kc + 128 * tt4: 512 * kc + 128 * (tt4 + 1)],
                                                 W2bf[:, DM * kc + 512 * n: DM * kc + 512 * (n + 1)],
                                                 start=(kc == 0), stop=False)
                                    return h.matmul(pm2[sl][:, 0:512], sones[0:1, 0:128],
                                                    b2bf[0:1, 512 * n: 512 * (n + 1)],
                                                    start=False, stop=True)
                                return f
                            P.op("te", m2f(), after=aft, key=f"m2:{e}:{q}:{tt4}:{n}")
                            aftv = [f"m2:{e}:{q}:{tt4}:{n}"]
                            if uac_keys[-2:] and len(uac_keys) >= 2:
                                aftv.append(uac_keys[-2])
                            P.op("ac", (lambda e=e, tau=tau, sl=sl: lambda h: h.activation(
                                ctb[sl][:, :], pm2[sl][:, 0:512], AF.Copy,
                                scale=msc[:, 8 * tau + e: 8 * tau + e + 1]))(),
                                after=aftv, key=f"m2ev:{e}:{q}:{tt4}:{n}")
                            uk = f"uac:{e}:{tau}:{n}"
                            P.op("gp", (lambda e=e, tau=tau, n=n, sl=sl: lambda h: h.dma_start(
                                uacc[:, DM * tau + 512 * n: DM * tau + 512 * (n + 1)],
                                ctb[sl][:, :],
                                accum_op=(ALU.bypass if e == 0 else ALU.add)))(),
                                after=[f"m2ev:{e}:{q}:{tt4}:{n}"], key=uk, dma=True)
                            uac_keys.append(uk)
            P.fence("p3")

        with ExitStack() as p4:
            gbx = p4.enter_context(nc.sbuf_tensor("gbx", [128, DM], F32))
            uf = [p4.enter_context(nc.sbuf_tensor(f"uf{i}", [128, DM], F32))
                  for i in range(2)]
            xf = [p4.enter_context(nc.sbuf_tensor(f"xf{i}", [128, DM], F32))
                  for i in range(2)]
            of = [p4.enter_context(nc.sbuf_tensor(f"of{i}", [128, DM], F32))
                  for i in range(2)]
            P.op("sy", lambda h: h.dma_start(
                gbx[:, :], bass.AP(modsp, 6 * DM, [[0, 128], [1, DM]])), key="bc:gb", dma=True)
            outk = []
            for t in range(NT):
                aft = [f"out:{t - 2}"] if t >= 2 else []
                P.op("sy", (lambda t=t: lambda h: h.dma_start(
                    uf[t % 2][:, :], uacc[:, DM * t: DM * (t + 1)]))(),
                    after=aft + [f"uac:7:{t}:1"], key=f"l4u:{t}", dma=True,
                    stream=f"l4u{t % 2}")
                P.op("sy", (lambda t=t: lambda h: h.dma_start(
                    xf[t % 2][:, :], xsp[:, DM * t: DM * (t + 1)]))(),
                    after=aft, key=f"l4x:{t}", dma=True, stream=f"l4x{t % 2}")
                P.op("dv", (lambda t=t: lambda h: h.tensor_mul(
                    of[t % 2][:, :], uf[t % 2][:, :], gbx[:, :]))(),
                    after=[f"l4u:{t}", "bc:gb"] + ([f"out:{t - 2}"] if t >= 2 else []))
                P.op("dv", (lambda t=t: lambda h: h.tensor_add(
                    of[t % 2][:, :], of[t % 2][:, :], xf[t % 2][:, :]))(),
                    after=[f"l4x:{t}"], key=f"o4:{t}")
                P.op("gp", (lambda t=t: lambda h: h.dma_start(
                    out[128 * t: 128 * (t + 1), :], of[t % 2][:, :]))(),
                    after=[f"o4:{t}"], key=f"out:{t}", dma=True)
                outk.append(f"out:{t}")
            P.op("gp", lambda h: h.drain(), after=outk, key="final")
        if TRUNC is not None:
            if TRUNC in P.keys:
                idx = P.keys[TRUNC]
            else:
                idx = max(P.keys[f"F:{TRUNC}:{e}"] for e in _ENG)
            P.ops = P.ops[:idx + 1]
    P.emit()
    return nc


def make_constants():
    half = 16
    try:
        import jax
        import jax.numpy as jnp
        with jax.default_device(jax.devices("cpu")[0]):
            invf = 1.0 / 10000.0 ** (jnp.arange(half, dtype=jnp.float32) / half)
            fr = jnp.arange(T, dtype=jnp.float32)[:, None] * invf
            cosv = np.asarray(jnp.cos(fr), dtype=np.float32)
            sinv = np.asarray(jnp.sin(fr), dtype=np.float32)
    except Exception:
        invf = np.float32(1.0) / np.float32(10000.0) ** (
            np.arange(half, dtype=np.float32) / np.float32(half))
        fr = (np.arange(T, dtype=np.float32)[:, None] * invf[None, :])
        cosv = np.cos(fr).astype(np.float32)
        sinv = np.sin(fr).astype(np.float32)
    cosT = np.concatenate([cosv, cosv], axis=1)
    sinT = np.concatenate([-sinv, sinv], axis=1)
    cosTl = np.zeros_like(cosT)
    sinTl = np.zeros_like(sinT)
    identf = np.eye(128, dtype=np.float32)
    identb = np.eye(128, dtype=ml_dtypes.bfloat16)
    maskb = np.triu(np.ones((128, 128), dtype=np.float32)).astype(ml_dtypes.bfloat16)
    Lsel = np.zeros((64, 128), dtype=np.float32)
    for hm in range(64):
        for p in range(128):
            if (hm // 2) % 4 == p // 32:
                Lsel[hm, p] = 1.0
    Rsel = np.zeros((64, 16), dtype=np.float32)
    for g in range(16):
        H4, m = g // 2, g % 2
        for hh in range(4):
            Rsel[2 * (4 * H4 + hh) + m, g] = 1.0
    onesb = np.ones((1, 128), dtype=ml_dtypes.bfloat16)
    return dict(identf=identf, identb=identb, maskb=maskb, cosT=cosT, sinT=sinT,
                cosTl=cosTl, sinTl=sinTl,
                Lsel=Lsel, Rsel=Rsel, onesb=onesb,
            fconst=np.array([list(FCONSTS) + [0.0] * (16 - len(FCONSTS))],
                            dtype=np.float32))


def kernel(**inputs):
    nc = build_program()
    consts = make_constants()
    f32 = lambda a: np.ascontiguousarray(np.asarray(a, dtype=np.float32))
    in_maps = []
    for c in range(8):
        m = dict(
            x=f32(inputs["L0_attn_x"][c]),
            cvec=f32(inputs["L0_c"][c:c + 1]),
            Wq=f32(inputs["L0_attn_Wq"]), Wk=f32(inputs["L0_attn_Wk"]),
            Wv=f32(inputs["L0_attn_Wv"]), Wout=f32(inputs["L0_attn_Wout"]),
            Wbeta=f32(inputs["L0_attn_Wbeta"]),
            bbeta=f32(inputs["L0_attn_bbeta"]).reshape(1, 64),
            Wmix=f32(inputs["L0_attn_Wmix"]),
            bmix=f32(inputs["L0_attn_bmix"]).reshape(1, 64),
            boutv=f32(inputs["L0_attn_bout"]).reshape(1, DM),
            ln_g=f32(inputs["L0_attn_ln_g"]).reshape(1, DM),
            ln_b=f32(inputs["L0_attn_ln_b"]).reshape(1, DM),
            rms_w=f32(inputs["L0_ffn_rms_w"]).reshape(1, DM),
            gate_w=f32(inputs["L0_ffn_gate_w"]),
            W1=f32(inputs["L0_ffn_W1"]), b1=f32(inputs["L0_ffn_b1"]),
            W2=f32(inputs["L0_ffn_W2"]), b2=f32(inputs["L0_ffn_b2"]),
            wmod=f32(inputs["L0_wmod"]),
            **consts,
        )
        in_maps.append(m)
    res = run_bass_kernel_spmd(nc, in_maps, core_ids=list(range(8)))
    return np.stack([r["out"] for r in res.results], axis=0)


# version 06 (final)
# speedup vs baseline: 1.0551x; 1.0551x over previous
import numpy as np
import ml_dtypes
import concourse.bass as bass
import concourse.mybir as mybir
from concourse.bass_utils import run_bass_kernel_spmd
from contextlib import ExitStack

F32 = mybir.dt.float32
BF16 = mybir.dt.bfloat16
AF = mybir.ActivationFunctionType
ALU = mybir.AluOpType
AX = mybir.AxisListType

T = 2048
DM = 1024
NH = 32
M = 2
C = 128
NCH = T // C
NT = T // 128
E = 8
FF = 4096
VA = 33 * NH
LN_EPS = 1e-5
ATT_EPS = 1e-6
RMS_EPS = 1e-6
TRUNC = None
FCONSTS = (LN_EPS, RMS_EPS, -0.5, -1.0, 0.85, 0.9995, 0.25, -0.25, 1.0, 0.0,
           ATT_EPS, 2.0)
DEBUG = False

_ENG = ("te", "dv", "ac", "gp", "sy")
_BLK = dict(te="tensor", dv="vector", ac="scalar", gp="gpsimd", sy="sync")


class Prog:

    def __init__(self, nc):
        self.nc = nc
        self.ops = []
        self.keys = {}

    def op(self, eng, fn, after=(), key=None, dma=False, stream=None):
        assert eng in _ENG
        if isinstance(after, str):
            after = (after,)
        for k in after:
            assert k in self.keys, f"forward/unknown dep {k!r}"
        if key is not None:
            assert key not in self.keys, f"dup key {key!r}"
            self.keys[key] = len(self.ops)
        if dma and stream is None:
            assert key is not None
            parts = key.split(":")
            while parts and parts[-1].isdigit():
                parts.pop()
            stream = ":".join(parts) or key
        self.ops.append([eng, fn, tuple(after), key, dma, stream])

    def fence(self, tag):
        last = {}
        for i, o in enumerate(self.ops):
            last[o[0]] = i
        lkeys = []
        for eng, i in last.items():
            key = self.ops[i][3]
            if key is None:
                key = f"_f:{tag}:{eng}"
                self.ops[i][3] = key
                self.keys[key] = i
            lkeys.append(key)
        for eng in _ENG:
            self.op(eng, lambda h: h.drain(), after=tuple(lkeys), key=f"F:{tag}:{eng}")

    def lint(self):
        return

    def emit(self):
        awaited = set(k for o in self.ops for k in o[2])
        stream_idx = {}
        for i, o in enumerate(self.ops):
            if o[4]:
                stream_idx.setdefault(o[5], []).append(i)
        sems = {e: self.nc.alloc_semaphore(name=f"S_{e}") for e in _ENG}
        ssems = {s: self.nc.alloc_semaphore(name=f"D_{s}"[:31])
                 for s in stream_idx}
        cnt = dict.fromkeys(_ENG, 0)
        keyval = {}
        for eng, fn, after, key, dma, stream in self.ops:
            if dma:
                continue
            if key in awaited:
                cnt[eng] += 1
                keyval[key] = (eng, cnt[eng])

        def dma_wait(k, widx):
            oi = self.keys[k]
            s = self.ops[oi][5]
            lst = stream_idx[s]
            npre = sum(1 for i in lst if i < widx)
            pos = lst.index(oi) + 1
            assert npre >= pos, f"dep {k} after waiter?"
            return ssems[s], 16 * npre

        with self.nc.Block() as blk:
            def mkrun(eng):
                def run(h):
                    done = {}
                    for widx, (oeng, fn, after, key, dma, stream) in enumerate(self.ops):
                        if oeng != eng:
                            continue
                        for k in after:
                            if self.ops[self.keys[k]][4]:
                                sem, v = dma_wait(k, widx)
                            else:
                                se, v = keyval[k]
                                if se == eng:
                                    continue
                                sem = sems[se]
                            sid = sem.name if hasattr(sem, "name") else id(sem)
                            if done.get(sid, -1) >= v:
                                continue
                            h.wait_ge(sem, v)
                            done[sid] = v
                        inst = fn(h)
                        if not dma and eng in ("dv", "ac"):
                            fs = 9999
                            try:
                                outs = inst.ins.outs
                                for oo in outs:
                                    ba = getattr(oo, "bass_ap", None)
                                    if ba is not None:
                                        fs = min(fs, ba.free_size())
                            except Exception:
                                pass
                            if fs < 96:
                                inst = h.drain()
                        if dma:
                            inst.then_inc(ssems[stream], 16)
                        elif key in awaited:
                            inst.then_inc(sems[eng], 1)
                return run
            for eng in _ENG:
                getattr(blk, _BLK[eng])(mkrun(eng))
        return cnt


def apx(t, col0, dims, part0=0, nparts=128):
    pitch = int(np.prod(t.shape[1:]))
    return bass.AP(t, part0 * pitch + col0, [[pitch, nparts]] + [list(d) for d in dims])


def _seq_mm(h, psum_ap, lhs_list, rhs_list):
    r = None
    n = len(lhs_list)
    for i, (l, rr) in enumerate(zip(lhs_list, rhs_list)):
        r = h.matmul(psum_ap, l, rr, start=(i == 0), stop=(i == n - 1))
    return r


def build_program():
    nc = bass.Bass("TRN2", target_bir_lowering=False, detect_race_conditions=False)

    def din(name, shape, dtype=F32):
        return nc.dram_tensor(name, list(shape), dtype, kind="ExternalInput")

    x_hbm = din("x", [T, DM])
    cvec = din("cvec", [1, DM])
    Wq = din("Wq", [DM, DM]); Wk = din("Wk", [DM, DM]); Wv = din("Wv", [DM, DM])
    Wout = din("Wout", [DM, DM])
    Wbeta = din("Wbeta", [64, DM]); bbeta = din("bbeta", [1, 64])
    Wmix = din("Wmix", [64, DM]); bmix = din("bmix", [1, 64])
    boutv = din("boutv", [1, DM])
    ln_g = din("ln_g", [1, DM]); ln_b = din("ln_b", [1, DM])
    rms_w = din("rms_w", [1, DM])
    gate_w = din("gate_w", [E, DM])
    W1 = din("W1", [E, DM, FF]); b1 = din("b1", [E, FF])
    W2 = din("W2", [E, FF, DM]); b2 = din("b2", [E, DM])
    wmod = din("wmod", [DM, 6 * DM])
    identf = din("identf", [128, 128])
    identb = din("identb", [128, 128], BF16)
    maskb = din("maskb", [128, 128], BF16)
    cosTd = din("cosT", [T, 32])
    sinTd = din("sinT", [T, 32])
    cosTl = din("cosTl", [T, 32])
    sinTl = din("sinTl", [T, 32])
    Lsel = din("Lsel", [64, 128])
    Rsel = din("Rsel", [64, 16])
    onesb = din("onesb", [1, 128], BF16)
    fconst = din("fconst", [1, 16])

    out = nc.dram_tensor("out", [T, DM], F32, kind="ExternalOutput")
    sk = "ExternalOutput" if DEBUG else "Internal"
    h1sp = nc.dram_tensor("h1sp", [128, NT * DM], F32, kind=sk)
    xsp = nc.dram_tensor("xsp", [128, NT * DM], F32, kind=sk)
    phiqsp = nc.dram_tensor("phiqsp", [128, 8 * T], BF16, kind=sk)
    ktmsp = nc.dram_tensor("ktmsp", [128, NT * DM], BF16, kind=sk)
    vaugsp = nc.dram_tensor("vaugsp", [128, NT * VA], BF16, kind=sk)
    uacc = nc.dram_tensor("uacc", [128, NT * DM], F32, kind=sk)
    modsp = nc.dram_tensor("modsp", [16, DM], F32, kind=sk)
    if DEBUG:
        d_logb = nc.dram_tensor("d_logb", [64, T], F32, kind="ExternalOutput")
        d_mix = nc.dram_tensor("d_mix", [128, NT * 64], F32, kind="ExternalOutput")
        d_xnt = nc.dram_tensor("d_xnt", [128, 8 * T], BF16, kind="ExternalOutput")
        d_msc = nc.dram_tensor("d_msc", [128, NT * 8], F32, kind="ExternalOutput")
        d_h1t = nc.dram_tensor("d_h1t", [128, 8 * T], BF16, kind="ExternalOutput")

    P = Prog(nc)
    with ExitStack() as top:
        xnT = top.enter_context(nc.sbuf_tensor("xnT", [128, 8 * T], BF16))
        msc = top.enter_context(nc.sbuf_tensor("msc", [128, NT * 8], F32))
        sidf = top.enter_context(nc.sbuf_tensor("sidf", [128, 128], F32))
        sidb = top.enter_context(nc.sbuf_tensor("sidb", [128, 128], BF16))
        smask = top.enter_context(nc.sbuf_tensor("smask", [128, 128], BF16))
        sones = top.enter_context(nc.sbuf_tensor("sones", [1, 128], BF16))
        sgateT = top.enter_context(nc.sbuf_tensor("sgateT", [128, 64], BF16))
        sfc = top.enter_context(nc.sbuf_tensor("sfc", [128, 16], F32))
        P.op("sy", lambda h: h.dma_start(
            sfc[:, :], bass.AP(fconst, 0, [[0, 128], [1, 16]])), key="ld:fc", dma=True)
        P.fence("fc")
        for fi, fv in enumerate(FCONSTS):
            nc.const_aps.register(F32, fv, sfc[:, fi:fi + 1])

        with ExitStack() as mid:
            scosT = mid.enter_context(nc.sbuf_tensor("scosT", [128, NT * 32], F32))
            ssinT = mid.enter_context(nc.sbuf_tensor("ssinT", [128, NT * 32], F32))
            sLsel = mid.enter_context(nc.sbuf_tensor("sLsel", [64, 128], F32))
            sRsel = mid.enter_context(nc.sbuf_tensor("sRsel", [64, 16], F32))
            snbb = mid.enter_context(nc.sbuf_tensor("snbb", [64, 1], F32))
            WoT = mid.enter_context(nc.sbuf_tensor("WoT", [128, 8 * DM], BF16))
            WbT = mid.enter_context(nc.sbuf_tensor("WbT", [128, 8 * 64], BF16))
            WmT = mid.enter_context(nc.sbuf_tensor("WmT", [128, 8 * 64], BF16))
            logb = mid.enter_context(nc.sbuf_tensor("logb", [64, T], F32))
            mixa = mid.enter_context(nc.sbuf_tensor("mixa", [128, NT * 64], F32))
            colv = mid.enter_context(nc.sbuf_tensor("colv", [128, 96], F32))
            (CV_CA1, CV_SA, CV_CB1, CV_SB, CV_LGA, CV_LBA, CV_GB, CV_T0,
             CV_GA, CV_LNG, CV_LNB, CV_T1) = [8 * i for i in range(12)]

            def cv(off, w=8):
                return colv[:, off: off + w]

            with ExitStack() as s05:
                WqT = s05.enter_context(nc.sbuf_tensor("WqT", [128, 8 * DM], BF16))
                WkT = s05.enter_context(nc.sbuf_tensor("WkT", [128, 8 * DM], BF16))
                WvT = s05.enter_context(nc.sbuf_tensor("WvT", [128, 8 * DM], BF16))
                h1T = s05.enter_context(nc.sbuf_tensor("h1T", [128, 8 * T], BF16))

                with ExitStack() as p0:
                    wst = p0.enter_context(nc.sbuf_tensor("wst", [128, 8 * DM], BF16))
                    wbst = p0.enter_context(nc.sbuf_tensor("wbst", [128, DM], BF16))
                    wmst = p0.enter_context(nc.sbuf_tensor("wmst", [128, DM], BF16))
                    wgst = p0.enter_context(nc.sbuf_tensor("wgst", [128, DM], BF16))
                    modst = [p0.enter_context(nc.sbuf_tensor(f"modst{i}", [128, DM], BF16))
                             for i in range(2)]
                    tlo = p0.enter_context(nc.sbuf_tensor("tlo", [128, 1024], F32))
                    smodT = p0.enter_context(nc.sbuf_tensor("smodT", [128, 48], F32))
                    scv = p0.enter_context(nc.sbuf_tensor("scv", [128, 8], F32))
                    srow8 = p0.enter_context(nc.sbuf_tensor("srow8", [128, 128], F32))
                    scvb = p0.enter_context(nc.sbuf_tensor("scvb", [128, 8], BF16))
                    sbbv = p0.enter_context(nc.sbuf_tensor("sbbv", [64, 1], F32))
                    sbmc = p0.enter_context(nc.sbuf_tensor("sbmc", [32, 2], F32))
                    pmod = p0.enter_context(nc.psum_tensor("pmod", [128, 512], F32))
                    ptb = [p0.enter_context(nc.psum_tensor(f"p0tb{i}", [128, 128], BF16))
                           for i in range(2)]

                    for nm, dst, src in (
                        ("idf", sidf[:, :], identf[:, :]),
                        ("idb", sidb[:, :], identb[:, :]),
                        ("msk", smask[:, :], maskb[:, :]),
                        ("one", sones[:, :], onesb[:, :]),
                        ("lsl", sLsel[:, :], Lsel[:, :]),
                        ("rsl", sRsel[:, :], Rsel[:, :]),
                        ("cosT", scosT[:, :].rearrange("p (t f) -> p t f", f=32),
                         bass.AP(cosTd, 0, [[32, 128], [32 * 128, NT], [1, 32]])),
                        ("sinT", ssinT[:, :].rearrange("p (t f) -> p t f", f=32),
                         bass.AP(sinTd, 0, [[32, 128], [32 * 128, NT], [1, 32]])),
                        ("cosTl", tlo[:, 0:512].rearrange("p (t f) -> p t f", f=32),
                         bass.AP(cosTl, 0, [[32, 128], [32 * 128, NT], [1, 32]])),
                        ("sinTl", tlo[:, 512:1024].rearrange("p (t f) -> p t f", f=32),
                         bass.AP(sinTl, 0, [[32, 128], [32 * 128, NT], [1, 32]])),
                        ("bb", sbbv[:, :], bass.AP(bbeta, 0, [[1, 64], [1, 1]])),
                        ("bmc", sbmc[:, :], bass.AP(bmix, 0, [[2, 32], [1, 2]])),
                    ):
                        P.op("sy", (lambda d, s: lambda h: h.dma_start(d, s))(dst, src),
                             key=f"ld:{nm}", dma=True)

                    P.op("dv", lambda h: h.tensor_add(scosT[:, :], scosT[:, :],
                                                      tlo[:, 0:512]),
                         after=["ld:cosT", "ld:cosTl"], key="cosfix")
                    P.op("dv", lambda h: h.tensor_add(ssinT[:, :], ssinT[:, :],
                                                      tlo[:, 512:1024]),
                         after=["ld:sinT", "ld:sinTl"], key="sinfix")
                    def w_src(W, nch, wid):
                        return bass.AP(W, 0, [[wid, 128], [128 * wid, nch], [1, wid]])

                    for wi, (W, dstT) in enumerate(((Wq, WqT), (Wk, WkT), (Wv, WvT), (Wout, WoT))):
                        after = [f"wtr:{wi - 1}:63"] if wi else []
                        P.op("gp", (lambda W=W: lambda h: h.dma_start(
                            wst[:, :].rearrange("p (r d) -> p r d", d=DM), w_src(W, 8, DM)))(),
                            after=after, key=f"wld:{wi}", dma=True)
                        for i in range(64):
                            r, dc = i // 8, i % 8
                            pt = ptb[i % 2]
                            aft = [f"wld:{wi}", "ld:idb"]
                            if i >= 2:
                                aft.append(f"wev:{wi}:{i - 2}")
                            elif wi > 0:
                                aft.append(f"wev:{wi - 1}:{62 + i}")
                            P.op("te", (lambda pt=pt, r=r, dc=dc: lambda h: h.transpose(
                                pt[:, :], wst[:, DM * r + 128 * dc: DM * r + 128 * (dc + 1)],
                                sidb[:, :]))(),
                                after=aft, key=f"wtr:{wi}:{i}")
                            P.op("dv", (lambda pt=pt, dstT=dstT, r=r, dc=dc: lambda h: h.tensor_copy(
                                dstT[:, DM * dc + 128 * r: DM * dc + 128 * (r + 1)], pt[:, :]))(),
                                after=[f"wtr:{wi}:{i}"], key=f"wev:{wi}:{i}")
                    P.op("dv", lambda h: h.memset(wbst[:, :], 0.0), key="wz:b")
                    P.op("dv", lambda h: h.memset(wmst[:, :], 0.0), key="wz:m")
                    P.op("dv", lambda h: h.memset(wgst[:, :], 0.0), key="wz:g")
                    P.op("gp", lambda h: h.dma_start(wbst[0:64, :], Wbeta[:, :]), after=["wz:b"], key="wld:b", dma=True)
                    P.op("gp", lambda h: h.dma_start(wmst[0:64, :], Wmix[:, :]), after=["wz:m"], key="wld:m", dma=True)
                    P.op("gp", lambda h: h.dma_start(wgst[0:8, :], gate_w[:, :]), after=["wz:g"], key="wld:g", dma=True)
                    for si, (st, rows, dstT, wd, ldk) in enumerate((
                            (wbst, 64, WbT, 64, "wld:b"), (wmst, 64, WmT, 64, "wld:m"),
                            (wgst, 8, sgateT, 8, "wld:g"))):
                        for dc in range(8):
                            i = si * 8 + dc
                            pt = ptb[i % 2]
                            aft = [ldk, "wev:3:63", "wz:b", "wz:m", "wz:g"]
                            if i >= 2:
                                aft.append(f"swev:{i - 2}")
                            P.op("te", (lambda pt=pt, st=st, dc=dc: lambda h: h.transpose(
                                pt[:, :], st[:, 128 * dc: 128 * (dc + 1)],
                                sidb[:, :]))(),
                                after=aft, key=f"swtr:{i}")
                            P.op("dv", (lambda pt=pt, dstT=dstT, wd=wd, dc=dc, rows=rows:
                                        lambda h: h.tensor_copy(
                                dstT[:, wd * dc: wd * (dc + 1)], pt[0:128, 0:rows]))(),
                                after=[f"swtr:{i}"], key=f"swev:{i}")

                    P.op("dv", lambda h: h.memset(srow8[:, :], 0.0), key="r8:z")
                    prevk = []
                    for nm, dsrc, dst_ap in (
                            ("cv", cvec, scv[:, :]),
                            ("lngc", ln_g, apx(colv, CV_LNG, [[1, 8]])),
                            ("lnbc", ln_b, apx(colv, CV_LNB, [[1, 8]]))):
                        P.op("sy", (lambda dsrc=dsrc: lambda h: h.dma_start(
                            srow8[0:8, :], bass.AP(dsrc, 0, [[128, 8], [1, 128]])))(),
                            after=["r8:z"] + [k for k in prevk if k.startswith("cvt")],
                            key=f"rld:{nm}", dma=True)
                        P.op("te", (lambda: lambda h: h.transpose(
                            pmod[:, 128:256], srow8[:, :], sidf[:, :]))(),
                            after=[f"rld:{nm}", "ld:idf"] +
                                  [k for k in prevk if k.startswith("cve")],
                            key=f"cvt:{nm}")
                        P.op("ac", (lambda dst_ap=dst_ap: lambda h: h.copy(
                            dst_ap, pmod[:, 128:136]))(),
                            after=[f"cvt:{nm}"], key=f"cve:{nm}")
                        prevk = [f"cvt:{nm}", f"cve:{nm}"]
                    P.op("ac", lambda h: h.activation(scv[:, :], scv[:, :], AF.Silu),
                         after=["cve:cv"], key="silc")
                    P.op("dv", lambda h: h.tensor_copy(scvb[:, :], scv[:, :]),
                         after=["silc"], key="silcb")
                    P.op("ac", lambda h: h.mul(snbb[:, :], sbbv[:, :], -1.0),
                         after=["ld:bb"], key="nbb")
                    for jt in range(48):
                        mst = modst[jt % 2]
                        aft = [] if jt < 2 else [f"modmm:{jt - 2}"]
                        P.op("gp", (lambda jt=jt, mst=mst: lambda h: h.dma_start(
                            mst[:, :].rearrange("p (kc j) -> p kc j", j=128),
                            bass.AP(wmod, 128 * jt, [[6 * DM, 128], [128 * 6 * DM, 8], [1, 128]])))(),
                            after=aft, key=f"modld:{jt}", dma=True, stream=f"mod{jt % 2}")
                        P.op("te", (lambda mst=mst: lambda h: _seq_mm(
                            h, pmod[:, 0:1],
                            [mst[:, 128 * kc: 128 * (kc + 1)] for kc in range(8)],
                            [scvb[:, kc:kc + 1] for kc in range(8)]))(),
                            after=[f"modld:{jt}", "silcb"] +
                                  ([f"modev:{jt - 1}"] if jt else ["cve:lnbc"]),
                            key=f"modmm:{jt}")
                        P.op("ac", (lambda jt=jt: lambda h: h.copy(
                            smodT[:, jt:jt + 1], pmod[:, 0:1]))(),
                            after=[f"modmm:{jt}"], key=f"modev:{jt}")

                    def mTn(i):
                        return smodT[:, 8 * i: 8 * (i + 1)]
                    P.op("dv", lambda h: h.tensor_scalar_add(cv(CV_CA1), mTn(1), 1.0),
                         after=["modev:47"], key="v:ca1")
                    P.op("dv", lambda h: h.tensor_copy(cv(CV_SA), mTn(0)), key="v:sa")
                    P.op("dv", lambda h: h.tensor_scalar_add(cv(CV_CB1), mTn(4), 1.0), key="v:cb1")
                    P.op("dv", lambda h: h.tensor_copy(cv(CV_SB), mTn(3)), key="v:sb")
                    for gi, (src_i, dst) in enumerate(((2, CV_GA), (5, CV_GB))):
                        P.op("dv", (lambda src_i=src_i: lambda h: h.tensor_single_scalar(
                            cv(CV_T0), mTn(src_i), 0.0, ALU.is_ge))())
                        P.op("dv", lambda h: h.tensor_scalar(
                            cv(CV_T0), cv(CV_T0), 2.0, -1.0, ALU.mult, ALU.add))
                        P.op("dv", (lambda src_i=src_i: lambda h: h.tensor_scalar_mul(
                            cv(CV_T1), mTn(src_i), -1.0))())
                        P.op("dv", (lambda src_i=src_i: lambda h: h.tensor_tensor(
                            cv(CV_T1), cv(CV_T1), mTn(src_i), ALU.max))())
                        P.op("dv", (lambda gi=gi: lambda h: h.tensor_scalar_max(
                            cv(CV_T1), cv(CV_T1), 0.25))())
                        P.op("dv", (lambda dst=dst: lambda h: h.tensor_mul(
                            cv(dst), cv(CV_T0), cv(CV_T1)))(), key=f"g:sel:{gi}")
                    P.op("dv", lambda h: h.tensor_mul(cv(CV_LGA), cv(CV_LNG), cv(CV_GA)),
                         after=["g:sel:0", "cve:lngc"])
                    P.op("dv", lambda h: h.tensor_mul(cv(CV_LBA), cv(CV_LNB), cv(CV_GA)),
                         after=["cve:lnbc"])
                    P.op("dv", lambda h: h.tensor_tensor(
                        colv[0:32, CV_T0:CV_T0 + 1], sbmc[0:32, 0:1], sbmc[0:32, 1:2],
                        ALU.subtract), after=["ld:bmc"], key="v:dbm")

                    P.op("te", lambda h: h.transpose(
                        pmod[0:96, 128:256], colv[:, 0:96], sidf[:, :]),
                        after=["v:dbm", "modev:47"], key="cvt:sp")
                    P.op("ac", lambda h: h.copy(srow8[0:96, :], pmod[0:96, 128:256]),
                         after=["cvt:sp"], key="cve:sp")
                    P.op("gp", lambda h: h.dma_start(
                        bass.AP(modsp, 0, [[128, 96], [1, 128]]), srow8[0:96, :]),
                        after=["cve:sp"], key="sp:mod1", dma=True)
                    P.fence("p0")

                with ExitStack() as p1:
                    ca1x = p1.enter_context(nc.sbuf_tensor("ca1x", [128, DM], F32))
                    sax = p1.enter_context(nc.sbuf_tensor("sax", [128, DM], F32))
                    dbmx = p1.enter_context(nc.sbuf_tensor("dbmx", [128, 32], F32))
                    x_st = [p1.enter_context(nc.sbuf_tensor(f"x_st{i}", [128, DM], F32))
                            for i in range(2)]
                    h1_st = [p1.enter_context(nc.sbuf_tensor(f"h1_st{i}", [128, DM], F32))
                             for i in range(2)]
                    sstat = p1.enter_context(nc.sbuf_tensor("sstat", [128, 16], F32))
                    smv = p1.enter_context(nc.sbuf_tensor("smv", [128, 8], F32))
                    kraw1 = p1.enter_context(nc.sbuf_tensor("kraw1", [128, DM], F32))
                    krop = p1.enter_context(nc.sbuf_tensor("krop", [128, DM], F32))
                    krop2 = p1.enter_context(nc.sbuf_tensor("krop2", [128, DM], F32))
                    kbf = [p1.enter_context(nc.sbuf_tensor(f"kbf{i}", [128, DM], BF16))
                           for i in range(2)]
                    va_st = [p1.enter_context(nc.sbuf_tensor(f"va_st{i}", [128, VA], BF16))
                             for i in range(2)]
                    qtm1 = p1.enter_context(nc.sbuf_tensor("qtm1", [128, DM], BF16))
                    qfm1 = p1.enter_context(nc.sbuf_tensor("qfm1", [128, DM], BF16))
                    bst = p1.enter_context(nc.sbuf_tensor("bst", [64, 512], F32))
                    bst2 = p1.enter_context(nc.sbuf_tensor("bst2", [64, 512], F32))
                    mst1 = p1.enter_context(nc.sbuf_tensor("mst1", [128, 32], F32))
                    smix64 = p1.enter_context(nc.sbuf_tensor("smix64", [128, 64], F32))
                    mst2 = p1.enter_context(nc.sbuf_tensor("mst2", [128, 32], F32))
                    pp = [p1.enter_context(nc.psum_tensor(f"pp{i}", [128, 512], F32))
                          for i in range(2)]
                    ptf = [p1.enter_context(nc.psum_tensor(f"ptf{i}", [128, 128], F32))
                           for i in range(2)]
                    ptb1 = [p1.enter_context(nc.psum_tensor(f"p1tb{i}", [128, 128], BF16))
                            for i in range(2)]

                    def cvread(nm, dsttile, vi, deps=("sp:mod1",)):
                        P.op("sy", (lambda dsttile=dsttile, vi=vi: lambda h: h.dma_start(
                            dsttile[:, :],
                            bass.AP(modsp, vi * DM, [[0, 128], [1, DM]])))(),
                            after=deps, key=f"bc:{nm}", dma=True)
                    cvread("ca1", ca1x, 0)
                    cvread("sa", sax, 1)
                    P.op("sy", lambda h: h.dma_start(
                        dbmx[:, :], bass.AP(modsp, 7 * DM, [[0, 128], [1, 32]])),
                        after=["sp:mod1"], key="bc:dbm", dma=True)

                    for t in range(NT):
                        xs, hs = x_st[t % 2], h1_st[t % 2]
                        aft = [] if t < 2 else [f"h1:{t - 2}", f"h1tr:{t - 2}:7",
                                                      f"sp:h1:{t - 2}"]
                        P.op("sy", (lambda t=t, xs=xs: lambda h: h.dma_start(
                            xs[:, :], x_hbm[128 * t: 128 * (t + 1), :]))(),
                            after=aft, key=f"ld:x:{t}", dma=True, stream=f"x{t % 2}")
                        P.op("dv", (lambda xs=xs: lambda h: h.bn_stats(
                            sstat[:, 0:6], xs[:, 0:512]))(), after=[f"ld:x:{t}"])
                        P.op("dv", (lambda xs=xs: lambda h: h.bn_stats(
                            sstat[:, 6:12], xs[:, 512:1024]))())
                        P.op("dv", lambda h: h.bn_aggr(smv[:, 0:2], sstat[:, 0:12]),
                             key=f"h1s:{t}")
                        P.op("ac", lambda h: h.activation(smv[:, 2:3], smv[:, 1:2], AF.Ln,
                                                          bias=LN_EPS), after=[f"h1s:{t}"])
                        P.op("ac", lambda h: h.activation(smv[:, 3:4], smv[:, 2:3], AF.Exp,
                                                          scale=-0.5), key=f"h1rs:{t}")
                        P.op("dv", (lambda xs=xs, hs=hs: lambda h: h.tensor_scalar(
                            hs[:, :], xs[:, :], smv[:, 0:1], smv[:, 3:4],
                            ALU.subtract, ALU.mult))(),
                            after=[f"h1rs:{t}"] + ([f"h1tr:{t - 2}:7", f"sp:h1:{t - 2}"]
                                                   if t >= 2 else []))
                        P.op("dv", (lambda hs=hs: lambda h: h.tensor_mul(
                            hs[:, :], hs[:, :], ca1x[:, :]))(), after=["bc:ca1"])
                        P.op("dv", (lambda hs=hs: lambda h: h.tensor_add(
                            hs[:, :], hs[:, :], sax[:, :]))(), after=["bc:sa"], key=f"h1:{t}")
                        P.op("gp", (lambda t=t, hs=hs: lambda h: h.dma_start(
                            h1sp[:, DM * t: DM * (t + 1)], hs[:, :]))(),
                            after=[f"h1:{t}"], key=f"sp:h1:{t}", dma=True)
                        for b in range(8):
                            pt = ptf[b % 2]
                            aft = [f"h1:{t}", "ld:idf"]
                            if t * 8 + b >= 2:
                                pb, pbb = divmod(t * 8 + b - 2, 8)
                                aft.append(f"h1te:{pb}:{pbb}")
                            P.op("te", (lambda pt=pt, hs=hs, b=b: lambda h: h.transpose(
                                pt[:, :], hs[:, 128 * b: 128 * (b + 1)], sidf[:, :]))(),
                                after=aft, key=f"h1tr:{t}:{b}")
                            P.op("ac", (lambda pt=pt, t=t, b=b: lambda h: h.copy(
                                h1T[:, T * b + 128 * t: T * b + 128 * (t + 1)], pt[:, :]))(),
                                after=[f"h1tr:{t}:{b}"], key=f"h1te:{t}:{b}")
                    P.fence("p05")
                    if DEBUG:
                        P.op("gp", lambda h: h.dma_start(d_h1t[:, :], h1T[:, :]),
                             key="dbg:h1t", dma=True)

                    for n in range(4):
                        ppn = pp[n % 2]
                        aft = [] if n < 2 else [f"bt:ev:{n - 2}"]
                        P.op("te", (lambda n=n, ppn=ppn: lambda h: _seq_mm(
                            h, ppn[0:64, 0:512],
                            [WbT[:, 64 * kc: 64 * (kc + 1)] for kc in range(8)],
                            [h1T[:, T * kc + 512 * n: T * kc + 512 * (n + 1)] for kc in range(8)]))(),
                            after=aft, key=f"bt:mm:{n}")
                        P.op("ac", (lambda ppn=ppn: lambda h: h.activation(
                            bst[0:64, 0:512], ppn[0:64, 0:512], AF.Exp,
                            bias=snbb[:, :], scale=-1.0))(),
                            after=[f"bt:mm:{n}", "nbb"] +
                                  ([f"bt:cl:{n - 1}"] if n else []), key=f"bt:ev:{n}")
                        P.op("dv", lambda h: h.tensor_scalar_add(bst2[0:64, :], bst[0:64, :], 1.0),
                             after=[f"bt:ev:{n}"])
                        P.op("dv", lambda h: h.reciprocal(bst[0:64, :], bst2[0:64, :]))
                        P.op("dv", lambda h: h.tensor_scalar(
                            bst2[0:64, :], bst[0:64, :], 0.85, 0.9995, ALU.max, ALU.min),
                            key=f"bt:cl:{n}")
                        P.op("ac", (lambda n=n: lambda h: h.activation(
                            logb[:, 512 * n: 512 * (n + 1)], bst2[0:64, :], AF.Ln))(),
                            after=[f"bt:cl:{n}"], key=f"bt:ln:{n}")
                    if DEBUG:
                        P.op("gp", lambda h: h.dma_start(d_logb[:, :], logb[:, :]),
                             after=[f"bt:ln:{n}" for n in range(4)], key="dbg:logb", dma=True)

                    def proj_mm(t, n, ppn, WT):
                        def f(h):
                            return _seq_mm(
                                h, ppn[:, 0:512],
                                [h1T[:, T * kc + 128 * t: T * kc + 128 * (t + 1)] for kc in range(8)],
                                [WT[:, DM * kc + 512 * n: DM * kc + 512 * (n + 1)] for kc in range(8)])
                        return f

                    for t in range(NT):
                        ppn = pp[0]
                        aft = (["bt:ev:2", "bt:ev:3"] if t == 0
                               else [f"mx:w:{t - 1}", f"qph:{t - 1}"])
                        P.op("te", (lambda t=t, ppn=ppn: lambda h: _seq_mm(
                            h, ppn[:, 0:64],
                            [h1T[:, T * kc + 128 * t: T * kc + 128 * (t + 1)] for kc in range(8)],
                            [WmT[:, 64 * kc: 64 * (kc + 1)] for kc in range(8)]))(),
                            after=aft, key=f"mx:mm:{t}")
                        P.op("ac", (lambda ppn=ppn: lambda h: h.copy(
                            smix64[:, :], ppn[:, 0:64]))(),
                            after=[f"mx:mm:{t}"] + ([f"mx:d:{t - 1}"] if t else []),
                            key=f"mx:ev:{t}")
                        P.op("dv", (lambda: lambda h: h.tensor_tensor(
                            mst1[:, :], apx(smix64, 0, [[2, 32]]), apx(smix64, 1, [[2, 32]]),
                            ALU.subtract))(),
                            after=[f"mx:ev:{t}"] + ([f"mx:e:{t - 1}"] if t else []),
                            key=f"mx:d:{t}")
                        P.op("dv", lambda h: h.tensor_add(mst1[:, :], mst1[:, :], dbmx[:, :]),
                             after=["bc:dbm"], key=f"mx:d2:{t}")
                        P.op("ac", lambda h: h.activation(mst2[:, :], mst1[:, :], AF.Exp,
                                                          scale=-1.0),
                             after=[f"mx:d2:{t}"], key=f"mx:e:{t}")
                        P.op("dv", lambda h: h.tensor_scalar_add(mst1[:, :], mst2[:, :], 1.0),
                             after=[f"mx:e:{t}"])
                        P.op("dv", (lambda t=t: lambda h: h.reciprocal(
                            apx(mixa, 64 * t, [[2, 32]]), mst1[:, :]))())
                        P.op("dv", (lambda t=t: lambda h: h.tensor_scalar(
                            apx(mixa, 64 * t + 1, [[2, 32]]), apx(mixa, 64 * t, [[2, 32]]),
                            -1.0, 1.0, ALU.mult, ALU.add))(), key=f"mx:w:{t}")

                        va = va_st[t % 2]
                        for n in range(2):
                            ppn = pp[1] if n == 0 else pp[0]
                            aft = ([f"mx:d:{t}"] if n == 1 else
                                   ([f"qph:{t - 1}"] if t else []) +
                                   ([f"sp:va:{t - 2}"] if t >= 2 else []))
                            P.op("te", proj_mm(t, n, ppn, WvT), after=aft, key=f"v:mm:{t}:{n}")
                            P.op("ac", (lambda va=va, n=n, ppn=ppn: lambda h: h.activation(
                                apx(va, 33 * 16 * n, [[33, 16], [1, 32]]), ppn[:, 0:512],
                                AF.Copy))(), after=[f"v:mm:{t}:{n}"], key=f"v:ev:{t}:{n}")
                        P.op("dv", (lambda va=va: lambda h: h.memset(
                            apx(va, 32, [[33, 32]]), 1.0))(),
                            after=[f"v:ev:{t}:0", f"v:ev:{t}:1"], key=f"v:on:{t}")
                        P.op("gp", (lambda t=t, va=va: lambda h: h.dma_start(
                            vaugsp[:, VA * t: VA * (t + 1)], va[:, :]))(),
                            after=[f"v:on:{t}"], key=f"sp:va:{t}", dma=True)

                        for which, WT in (("k", WkT), ("q", WqT)):
                            kr = kraw1
                            prev_done = f"qph:{t - 1}" if t >= 1 else None
                            for n in range(2):
                                ppn = pp[n]
                                aft = []
                                if which == "k" and n == 0:
                                    aft = [f"v:ev:{t}:1"]
                                if which == "q" and n == 0:
                                    aft = [f"kph:{t}"]
                                if prev_done:
                                    aft.append(prev_done)
                                P.op("te", proj_mm(t, n, ppn, WT), after=aft,
                                     key=f"{which}:mm:{t}:{n}")
                                P.op("ac", (lambda kr=kr, n=n, ppn=ppn: lambda h: h.copy(
                                    kr[:, 512 * n: 512 * (n + 1)], ppn[:, 0:512]))(),
                                    after=[f"{which}:mm:{t}:{n}"], key=f"{which}:ev:{t}:{n}")
                            cos_ap = apx(scosT, 32 * t, [[0, 32], [1, 32]])
                            sin_lo = apx(ssinT, 32 * t, [[0, 32], [1, 16]])
                            sin_hi = apx(ssinT, 32 * t + 16, [[0, 32], [1, 16]])
                            P.op("dv", (lambda kr=kr, cos_ap=cos_ap: lambda h: h.tensor_tensor(
                                krop[:, :].rearrange("p (a b) -> p a b", b=32),
                                kr[:, :].rearrange("p (a b) -> p a b", b=32), cos_ap, ALU.mult))(),
                                after=[f"{which}:ev:{t}:0", f"{which}:ev:{t}:1", "cosfix"])
                            P.op("dv", (lambda kr=kr, sin_lo=sin_lo: lambda h: h.tensor_tensor(
                                apx(krop2, 0, [[32, 32], [1, 16]]),
                                apx(kr, 16, [[32, 32], [1, 16]]), sin_lo, ALU.mult))(),
                                after=["sinfix"])
                            P.op("dv", (lambda kr=kr, sin_hi=sin_hi: lambda h: h.tensor_tensor(
                                apx(krop2, 16, [[32, 32], [1, 16]]),
                                apx(kr, 0, [[32, 32], [1, 16]]), sin_hi, ALU.mult))())
                            P.op("dv", lambda h: h.tensor_add(krop[:, :], krop[:, :], krop2[:, :]))
                            P.op("dv", lambda h: h.tensor_scalar_min(krop2[:, :], krop[:, :], 0.0),
                                 key=f"{which}mn:{t}")
                            P.op("ac", lambda h: h.activation(krop2[:, :], krop2[:, :], AF.Exp),
                                 after=[f"{which}mn:{t}"], key=f"{which}ex:{t}")
                            P.op("dv", lambda h: h.tensor_scalar_max(krop[:, :], krop[:, :], 0.0))
                            kb = kbf[t % 2] if which == "k" else qtm1
                            bufw = []
                            if which == "k" and t >= 2:
                                bufw = [f"sp:kt:{t - 2}"]
                            if which == "q" and t >= 1:
                                bufw = [f"qtr:{t - 1}:7"]
                            P.op("dv", (lambda kb=kb: lambda h: h.tensor_add(
                                kb[:, :], krop[:, :], krop2[:, :]))(),
                                after=[f"{which}ex:{t}"] + bufw, key=f"{which}ph:{t}")
                            if which == "k":
                                P.op("gp", (lambda t=t, kb=kb: lambda h: h.dma_start(
                                    ktmsp[:, DM * t: DM * (t + 1)], kb[:, :]))(),
                                    after=[f"kph:{t}"], key=f"sp:kt:{t}", dma=True)
                        qb, qf = qtm1, qfm1
                        for b in range(8):
                            pt = ptb1[b % 2]
                            aft = [f"qph:{t}"]
                            if t * 8 + b >= 2:
                                pb, pbb = divmod(t * 8 + b - 2, 8)
                                aft.append(f"qte:{pb}:{pbb}")
                            if t >= 1 and b == 0:
                                aft.append(f"sp:q:{t - 1}")
                            P.op("te", (lambda pt=pt, qb=qb, b=b: lambda h: h.transpose(
                                pt[:, :], qb[:, 128 * b: 128 * (b + 1)], sidb[:, :]))(),
                                after=aft, key=f"qtr:{t}:{b}")
                            P.op("ac", (lambda pt=pt, qf=qf, b=b: lambda h: h.copy(
                                qf[:, 128 * b: 128 * (b + 1)], pt[:, :]))(),
                                after=[f"qtr:{t}:{b}"], key=f"qte:{t}:{b}")
                        P.op("gp", (lambda t=t, qf=qf: lambda h: h.dma_start(
                            bass.AP(phiqsp, 128 * t, [[8 * T, 128], [T, 8], [1, 128]]),
                            qf[:, :].rearrange("p (b j) -> p b j", j=128)))(),
                            after=[f"qte:{t}:{b}" for b in range(8)], key=f"sp:q:{t}", dma=True)
                    P.fence("p1")

            with ExitStack() as p2:
                cb1x = p2.enter_context(nc.sbuf_tensor("cb1x", [128, DM], F32))
                sbvx = p2.enter_context(nc.sbuf_tensor("sbvx", [128, DM], F32))
                lgax = p2.enter_context(nc.sbuf_tensor("lgax", [128, DM], F32))
                lbax = p2.enter_context(nc.sbuf_tensor("lbax", [128, DM], F32))
                rmswx = p2.enter_context(nc.sbuf_tensor("rmswx", [128, DM], F32))
                boutx = p2.enter_context(nc.sbuf_tensor("boutx", [128, DM], F32))
                phq_st = [p2.enter_context(nc.sbuf_tensor(f"phq{i}", [128, DM], BF16))
                          for i in range(2)]
                ktm_st = [p2.enter_context(nc.sbuf_tensor(f"ktm{i}", [128, DM], BF16))
                          for i in range(2)]
                va2_st = [p2.enter_context(nc.sbuf_tensor(f"va2{i}", [128, VA], BF16))
                          for i in range(2)]
                h12_st = [p2.enter_context(nc.sbuf_tensor(f"h12{i}", [128, DM], F32))
                          for i in range(2)]
                x2_st = [p2.enter_context(nc.sbuf_tensor(f"x2{i}", [128, DM], F32))
                         for i in range(2)]
                sPTall = p2.enter_context(nc.sbuf_tensor("sPTall", [128, NCH * 64], F32))
                sIvTall = p2.enter_context(nc.sbuf_tensor("sIvTall", [128, NCH * 64], BF16))
                sPCtall = p2.enter_context(nc.sbuf_tensor("sPCtall", [128, NCH * 16], F32))
                ktil = [p2.enter_context(nc.sbuf_tensor(f"ktil{i}", [128, DM], BF16))
                        for i in range(2)]
                ktilT = [p2.enter_context(nc.sbuf_tensor(f"ktilT{i}", [128, DM], BF16))
                         for i in range(2)]
                sAt = [p2.enter_context(nc.sbuf_tensor(f"sAt{i}", [128, 512], BF16))
                       for i in range(2)]
                oa = p2.enter_context(nc.sbuf_tensor("oa", [128, 16 * 132], F32))
                sden = p2.enter_context(nc.sbuf_tensor("sden", [128, 64], F32))
                stA = p2.enter_context(nc.sbuf_tensor("stA", [128, 64], F32))
                stB = p2.enter_context(nc.sbuf_tensor("stB", [128, 64], F32))
                srd = p2.enter_context(nc.sbuf_tensor("srd", [128, 64], F32))
                sytmp = p2.enter_context(nc.sbuf_tensor("sytmp", [128, 128], F32))
                sytmp2 = p2.enter_context(nc.sbuf_tensor("sytmp2", [128, DM], F32))
                ych = p2.enter_context(nc.sbuf_tensor("ych", [128, DM], F32))
                yTb = p2.enter_context(nc.sbuf_tensor("yTb", [128, DM], BF16))
                zst = p2.enter_context(nc.sbuf_tensor("zst", [128, DM], F32))
                h2st = p2.enter_context(nc.sbuf_tensor("h2st", [128, DM], F32))
                xnst = p2.enter_context(nc.sbuf_tensor("xnst", [128, DM], BF16))
                szstat = p2.enter_context(nc.sbuf_tensor("szstat", [128, 16], F32))
                szv = p2.enter_context(nc.sbuf_tensor("szv", [128, 8], F32))
                sel8 = p2.enter_context(nc.sbuf_tensor("sel8", [128, 8], F32))
                stmp8 = p2.enter_context(nc.sbuf_tensor("stmp8", [128, 4], F32))
                sZ = p2.enter_context(nc.sbuf_tensor("sZ", [128, 16 * 132], F32))
                sZb = p2.enter_context(nc.sbuf_tensor("sZb", [128, 16 * 132], BF16))
                mu = [p2.enter_context(nc.psum_tensor(f"mu{i}", [128, 512], F32))
                      for i in range(4)]
                pb = [p2.enter_context(nc.psum_tensor(f"pb{i}", [128, 272], F32))
                      for i in range(2)]
                ptb2 = [p2.enter_context(nc.psum_tensor(f"p2tb{i}", [128, 128], BF16))
                        for i in range(2)]

                def cvread2(nm, dsttile, vi, deps=("sp:mod1",)):
                    P.op("sy", (lambda dsttile=dsttile, vi=vi: lambda h: h.dma_start(
                        dsttile[:, :],
                        bass.AP(modsp, vi * DM, [[0, 128], [1, DM]])))(),
                        after=deps, key=f"bc:{nm}", dma=True)
                cvread2("cb1", cb1x, 2)
                cvread2("sb", sbvx, 3)
                cvread2("lga", lgax, 4)
                cvread2("lba", lbax, 5)
                P.op("sy", lambda h: h.dma_start(
                    rmswx[:, :], bass.AP(rms_w, 0, [[0, 128], [1, DM]])),
                    key="ld:rmw", dma=True)
                P.op("sy", lambda h: h.dma_start(
                    boutx[:, :], bass.AP(boutv, 0, [[0, 128], [1, DM]])),
                    key="ld:bov", dma=True)
                P.op("dv", lambda h: h.memset(sZ[:, :], 0.0), key="z:init")
                P.op("dv", lambda h: h.memset(sZb[:, :], 0.0), key="z:init:b")

                with ExitStack() as dp:
                    cumA = dp.enter_context(nc.sbuf_tensor("cumA", [64, T], F32))
                    cumB = dp.enter_context(nc.sbuf_tensor("cumB", [64, T], F32))
                    sPall = dp.enter_context(nc.sbuf_tensor("sPall", [128, T], F32))
                    sPC64a = dp.enter_context(nc.sbuf_tensor("sPC64a", [64, 16], F32))
                    sRvall = dp.enter_context(nc.sbuf_tensor("sRvall", [64, 256], F32))
                    for si, d in enumerate([1, 2, 4, 8, 16, 32, 64]):
                        if si == 0:
                            src = logb
                        else:
                            src = cumB if si % 2 == 1 else cumA
                        dst = cumB if si % 2 == 0 else cumA
                        aft = [f"bt:ln:{n}" for n in range(4)] if si == 0 else []
                        P.op("dv", (lambda src=src, dst=dst, d=d: lambda h: h.tensor_copy(
                            apx(dst, 0, [[128, NCH], [1, d]], 0, 64),
                            apx(src, 0, [[128, NCH], [1, d]], 0, 64)))(), after=aft)
                        P.op("dv", (lambda src=src, dst=dst, d=d: lambda h: h.tensor_tensor(
                            apx(dst, d, [[128, NCH], [1, 128 - d]], 0, 64),
                            apx(src, d, [[128, NCH], [1, 128 - d]], 0, 64),
                            apx(src, 0, [[128, NCH], [1, 128 - d]], 0, 64), ALU.add))(),
                            key=("dp:cum" if si == 6 else None))
                    P.op("dv", lambda h: h.memset(sPall[64:128, :], 0.0), key="dp:pz")
                    P.op("ac", lambda h: h.activation(sPall[0:64, :], cumB[0:64, :],
                                                      AF.Exp),
                         after=["dp:cum", "dp:pz"], key="dp:pexp")
                    P.op("ac", lambda h: h.activation(
                        sPC64a[0:64, :], apx(cumB, 127, [[128, NCH]], 0, 64), AF.Exp),
                        after=["dp:cum"], key="dp:pc64")
                    P.op("dv", lambda h: h.tensor_tensor(
                        apx(sRvall, 0, [[16, NCH], [1, 16]], 0, 64),
                        apx(sRsel, 0, [[0, NCH], [1, 16]], 0, 64),
                        apx(sPC64a, 0, [[1, NCH], [0, 16]], 0, 64), ALU.mult),
                        after=["dp:pc64", "ld:rsl"], key="dp:rv")
                    P.op("te", lambda h: h.matmul(pb[0][:, 0:256], sLsel[0:64, :],
                                                  sRvall[0:64, :], start=True, stop=True),
                         after=["dp:rv", "ld:lsl"], key="dp:pcmm")
                    P.op("ac", lambda h: h.copy(sPCtall[:, :], pb[0][:, 0:256]),
                         after=["dp:pcmm"], key="dp:pctev")
                    for k in range(4):
                        for j in range(4):
                            c = 4 * k + j
                            P.op("te", (lambda k=k, j=j, c=c: lambda h: h.transpose(
                                mu[k][:, 128 * j: 128 * (j + 1)],
                                sPall[:, 128 * c: 128 * (c + 1)], sidf[:, :]))(),
                                after=["dp:pexp"], key=f"dp:ptr:{c}")
                    for k in range(4):
                        P.op("ac", (lambda k=k: lambda h: h.copy(
                            sPTall[:, 256 * k: 256 * (k + 1)],
                            apx(mu[k], 0, [[128, 4], [1, 64]])))(),
                            after=[f"dp:ptr:{4 * k + 3}"], key=f"dp:ptev:{k}")
                    P.op("ac", lambda h: h.activation(sPall[0:64, :], cumB[0:64, :],
                                                      AF.Exp, scale=-1.0),
                         after=["dp:ptr:15"], key="dp:ivexp")
                    for k in range(4):
                        for j in range(4):
                            c = 4 * k + j
                            P.op("te", (lambda k=k, j=j, c=c: lambda h: h.transpose(
                                mu[k][:, 128 * j: 128 * (j + 1)],
                                sPall[:, 128 * c: 128 * (c + 1)], sidf[:, :]))(),
                                after=["dp:ivexp", f"dp:ptev:{k}"], key=f"dp:ivtr:{c}")
                    for k in range(4):
                        P.op("ac", (lambda k=k: lambda h: h.copy(
                            sIvTall[:, 256 * k: 256 * (k + 1)],
                            apx(mu[k], 0, [[128, 4], [1, 64]])))(),
                            after=[f"dp:ivtr:{4 * k + 3}"], key=f"dp:ivtev:{k}")

                for c in range(NCH):
                    dep_done = [f"ch:{c - 2}:done"] if c >= 2 else []
                    P.op("sy", (lambda c=c: lambda h: h.dma_start(
                        phq_st[c % 2][:, :].rearrange("p (b j) -> p b j", j=128),
                        bass.AP(phiqsp, 128 * c, [[8 * T, 128], [T, 8], [1, 128]])))(),
                        after=dep_done + [f"sp:q:{c}"], key=f"l2:phq:{c}", dma=True,
                        stream=f"lphq{c % 2}")
                    P.op("sy", (lambda c=c: lambda h: h.dma_start(
                        ktm_st[c % 2][:, :], ktmsp[:, DM * c: DM * (c + 1)]))(),
                        after=dep_done + [f"sp:kt:{c}"], key=f"l2:ktm:{c}", dma=True,
                        stream=f"lktm{c % 2}")
                    P.op("sy", (lambda c=c: lambda h: h.dma_start(
                        va2_st[c % 2][:, :], vaugsp[:, VA * c: VA * (c + 1)]))(),
                        after=dep_done + [f"sp:va:{c}"], key=f"l2:va:{c}", dma=True,
                        stream=f"lva{c % 2}")
                    P.op("sy", (lambda c=c: lambda h: h.dma_start(
                        h12_st[c % 2][:, :], h1sp[:, DM * c: DM * (c + 1)]))(),
                        after=dep_done + [f"sp:h1:{c}"], key=f"l2:h1:{c}", dma=True,
                        stream=f"lh1{c % 2}")
                    P.op("sy", (lambda c=c: lambda h: h.dma_start(
                        x2_st[c % 2][:, :], x_hbm[128 * c: 128 * (c + 1), :]))(),
                        after=dep_done, key=f"l2:x:{c}", dma=True, stream=f"lx{c % 2}")

                    for m in range(2):
                        P.op("dv", (lambda c=c, m=m: lambda h: h.tensor_tensor(
                            ktil[m][:, :].rearrange("p (a b) -> p a b", b=32),
                            ktm_st[c % 2][:, :].rearrange("p (a b) -> p a b", b=32),
                            apx(sIvTall, 64 * c + m, [[2, 32], [0, 32]]), ALU.mult))(),
                            after=[f"dp:ivtev:{c // 4}", f"l2:ktm:{c}"] +
                                  ([f"u:mm:{c - 1}:{14 + m}"] if c else []),
                            key=f"ktl:{c}:{m}")
                        for b in range(8):
                            i = m * 8 + b
                            pt = ptb2[i % 2]
                            aft = [f"ktl:{c}:{m}"]
                            if c > 0 and i < 2:
                                aft.append(f"xnte:{c - 1}:{6 + i}")
                            if i >= 2:
                                aft.append(f"kte:{c}:{(i - 2) // 8}:{(i - 2) % 8}")
                            P.op("te", (lambda pt=pt, m=m, b=b: lambda h: h.transpose(
                                pt[:, :], ktil[m][:, 128 * b: 128 * (b + 1)], sidb[:, :]))(),
                                after=aft, key=f"ktr:{c}:{m}:{b}")
                            P.op("ac", (lambda pt=pt, m=m, b=b: lambda h: h.copy(
                                ktilT[m][:, 128 * b: 128 * (b + 1)], pt[:, :]))(),
                                after=[f"ktr:{c}:{m}:{b}"], key=f"kte:{c}:{m}:{b}")

                    for g in range(16):
                        H4, m = g // 2, g % 2
                        sl = g % 2
                        aft = [f"kte:{c}:{m}:{H4}", f"l2:phq:{c}"]
                        if g >= 1:
                            aft.append(f"sc:ev:{c}:{g - 1}")
                        elif c > 0:
                            aft.extend([f"lgr:{c - 1}", f"lgel:{c - 1}",
                                        f"zdv:{c - 1}:1"])
                        def scmm(c=c, g=g, H4=H4, m=m):
                            def f(h):
                                r = None
                                for hh in range(4):
                                    r = h.matmul(
                                        mu[hh][:, 0:128],
                                        ktilT[m][32 * hh: 32 * (hh + 1), 128 * H4: 128 * (H4 + 1)],
                                        phq_st[c % 2][32 * hh: 32 * (hh + 1), 128 * H4: 128 * (H4 + 1)],
                                        start=True, stop=True, tile_position=(32 * hh, 0))
                                return r
                            return f
                        P.op("te", scmm(), after=aft, key=f"sc:mm:{c}:{g}")
                        def scev(sl=sl):
                            def f(h):
                                r = None
                                for hh in range(4):
                                    r = h.copy(sAt[sl][:, 128 * hh: 128 * (hh + 1)],
                                               mu[hh][:, 0:128])
                                return r
                            return f
                        P.op("ac", scev(), after=[f"sc:mm:{c}:{g}"] +
                             ([f"oa:mm:{c}:{g - 2}"] if g >= 2 else
                              ([f"oa:mm:{c - 1}:{14 + g}"] if c else [])),
                             key=f"sc:ev:{c}:{g}")
                        P.op("dv", (lambda sl=sl: lambda h: h.tensor_tensor(
                            sAt[sl][:, :].rearrange("p (a b) -> p a b", b=128),
                            sAt[sl][:, :].rearrange("p (a b) -> p a b", b=128),
                            apx(smask, 0, [[0, 4], [1, 128]]), ALU.mult))(),
                            after=[f"sc:ev:{c}:{g}", "ld:msk"], key=f"sc:mk:{c}:{g}")
                        aft = [f"sc:mk:{c}:{g}", f"l2:va:{c}"]
                        aft.append(f"zbf:{c - 1}:{g}" if c else "z:init:b")
                        if g >= 2:
                            aft.append(f"oa:ev:{c}:{g - 2}")
                            aft.append(f"zadd:{c}:{g - 2}")
                        elif c:
                            aft.append(f"oa:ev:{c - 1}:{14 + g}")
                        elif g == 0:
                            aft.append("dp:pctev")
                        def oamm(c=c, g=g, H4=H4, m=m, sl=sl):
                            def f(h):
                                r = None
                                for hh in range(4):
                                    h.matmul(
                                        pb[sl][:, 33 * hh: 33 * (hh + 1)],
                                        sAt[sl][:, 128 * hh: 128 * (hh + 1)],
                                        va2_st[c % 2][:, 33 * (4 * H4 + hh): 33 * (4 * H4 + hh) + 33],
                                        start=True, stop=False)
                                    r = h.matmul(
                                        pb[sl][:, 33 * hh: 33 * (hh + 1)],
                                        phq_st[c % 2][32 * hh: 32 * (hh + 1), 128 * H4: 128 * (H4 + 1)],
                                        sZb[32 * hh: 32 * (hh + 1), 132 * g + 33 * hh: 132 * g + 33 * (hh + 1)],
                                        start=False, stop=True, tile_position=(32 * hh, 0))
                                return r
                            return f
                        P.op("te", oamm(), after=aft, key=f"oa:mm:{c}:{g}")
                        aft = [f"ktl:{c}:{m}", f"l2:va:{c}"]
                        if g >= 2:
                            aft.append(f"zadd:{c}:{g - 2}")
                        elif c:
                            aft.append(f"zadd:{c - 1}:{14 + g}")
                        elif g == 0:
                            aft.append("dp:pctev")
                        P.op("te", (lambda c=c, g=g, H4=H4, m=m, sl=sl: lambda h: h.matmul(
                            pb[sl][:, 132:264], ktil[m][:, 128 * H4: 128 * (H4 + 1)],
                            va2_st[c % 2][:, 33 * 4 * H4: 33 * 4 * H4 + 132],
                            start=True, stop=True))(), after=aft, key=f"u:mm:{c}:{g}")
                        P.op("ac", (lambda c=c, g=g, sl=sl: lambda h: h.copy(
                            oa[:, 132 * g: 132 * (g + 1)], pb[sl][:, 0:132]))(),
                            after=[f"oa:mm:{c}:{g}", f"u:mm:{c}:{g}"] +
                                  ([f"y:{c - 1}"] if (c and g == 0) else []),
                            key=f"oa:ev:{c}:{g}")
                        P.op("dv", (lambda g=g, sl=sl: lambda h: h.tensor_tensor(
                            sZ[:, 132 * g: 132 * (g + 1)], sZ[:, 132 * g: 132 * (g + 1)],
                            pb[sl][:, 132:264], ALU.add))(),
                            after=[f"u:mm:{c}:{g}", "dp:pctev", f"oa:ev:{c}:{g}"] +
                                  (["z:init"] if c == 0 else []))
                        P.op("dv", (lambda c=c, g=g: lambda h: h.tensor_scalar(
                            sZ[:, 132 * g: 132 * (g + 1)], sZ[:, 132 * g: 132 * (g + 1)],
                            sPCtall[:, 16 * c + g: 16 * c + g + 1], None, ALU.mult))(),
                            key=f"zadd:{c}:{g}")
                        P.op("ac", (lambda g=g: lambda h: h.copy(
                            sZb[:, 132 * g: 132 * (g + 1)], sZ[:, 132 * g: 132 * (g + 1)]))(),
                            after=[f"zadd:{c}:{g}"], key=f"zbf:{c}:{g}")

                    PTsel = apx(sPTall, 64 * c, [[8, 8], [1, 2], [2, 4]])
                    mixsel = apx(mixa, 64 * c, [[8, 8], [1, 2], [2, 4]])
                    P.op("dv", lambda h: h.tensor_copy(
                        sden[:, :].rearrange("p (a b) -> p a b", b=4),
                        apx(oa, 32, [[132, 16], [33, 4]])),
                        after=[f"oa:ev:{c}:15", f"dp:ptev:{c // 4}"])
                    P.op("dv", (lambda PTsel=PTsel: lambda h: h.tensor_tensor(
                        stA[:, :].rearrange("p (a b c) -> p a b c", b=2, c=4),
                        sden[:, :].rearrange("p (a b c) -> p a b c", b=2, c=4),
                        PTsel, ALU.mult))())
                    P.op("dv", lambda h: h.tensor_scalar_add(stA[:, :], stA[:, :], ATT_EPS))
                    P.op("dv", lambda h: h.reciprocal(stA[:, :], stA[:, :]))
                    P.op("dv", (lambda PTsel=PTsel, mixsel=mixsel: lambda h: h.tensor_tensor(
                        stB[:, :].rearrange("p (a b c) -> p a b c", b=2, c=4),
                        PTsel, mixsel, ALU.mult))())
                    P.op("dv", lambda h: h.tensor_mul(srd[:, :], stA[:, :], stB[:, :]),
                         key=f"rd:{c}")

                    P.op("dv", (lambda c=c: lambda h: h.tensor_tensor(
                        ych[:, :].rearrange("p (a b c) -> p a b c", b=4, c=32),
                        apx(oa, 0, [[264, 8], [33, 4], [1, 32]]),
                        apx(srd, 0, [[8, 8], [1, 4], [0, 32]]), ALU.mult))(),
                        after=([f"ytr:{c - 1}:7"] if c else []))
                    P.op("dv", lambda h: h.tensor_tensor(
                        sytmp2[:, :].rearrange("p (a b c) -> p a b c", b=4, c=32),
                        apx(oa, 132, [[264, 8], [33, 4], [1, 32]]),
                        apx(srd, 4, [[8, 8], [1, 4], [0, 32]]), ALU.mult))
                    P.op("dv", (lambda c=c: lambda h: h.tensor_add(
                        ych[:, :], ych[:, :], sytmp2[:, :]))(), key=f"y:{c}")

                    for b in range(8):
                        pt = mu[2 + b % 2]
                        aft = [f"y:{c}", f"sc:ev:{c}:15"]
                        if b >= 2:
                            aft.append(f"yte:{c}:{b - 2}")
                        P.op("te", (lambda pt=pt, b=b: lambda h: h.transpose(
                            pt[:, 0:128], ych[:, 128 * b: 128 * (b + 1)], sidf[:, :]))(),
                            after=aft, key=f"ytr:{c}:{b}")
                        P.op("ac", (lambda pt=pt, b=b: lambda h: h.copy(
                            yTb[:, 128 * b: 128 * (b + 1)], pt[:, 0:128]))(),
                            after=[f"ytr:{c}:{b}"], key=f"yte:{c}:{b}")
                    for n in range(2):
                        aft = [f"yte:{c}:7", f"sc:ev:{c}:15"]
                        P.op("te", (lambda n=n: lambda h: _seq_mm(
                            h, mu[n][:, 0:512],
                            [yTb[:, 128 * kb: 128 * (kb + 1)] for kb in range(8)],
                            [WoT[:, DM * kb + 512 * n: DM * kb + 512 * (n + 1)] for kb in range(8)]))(),
                            after=aft, key=f"z:mm:{c}:{n}")
                        P.op("dv", (lambda c=c, n=n: lambda h: h.tensor_tensor(
                            zst[:, 512 * n: 512 * (n + 1)], mu[n][:, 0:512],
                            h12_st[c % 2][:, 512 * n: 512 * (n + 1)], ALU.add))(),
                            after=[f"z:mm:{c}:{n}", f"l2:h1:{c}"])
                        P.op("dv", (lambda n=n: lambda h: h.tensor_add(
                            zst[:, 512 * n: 512 * (n + 1)], zst[:, 512 * n: 512 * (n + 1)],
                            boutx[:, 512 * n: 512 * (n + 1)]))(),
                            after=["ld:bov"], key=f"zdv:{c}:{n}")
                    P.op("dv", lambda h: h.bn_stats(szstat[:, 0:6], zst[:, 0:512]))
                    P.op("dv", lambda h: h.bn_stats(szstat[:, 6:12], zst[:, 512:1024]))
                    P.op("dv", lambda h: h.bn_aggr(szv[:, 0:2], szstat[:, 0:12]),
                         key=f"zs:{c}")
                    P.op("ac", lambda h: h.activation(szv[:, 2:3], szv[:, 1:2], AF.Ln,
                                                      bias=LN_EPS), after=[f"zs:{c}"])
                    P.op("ac", lambda h: h.activation(szv[:, 3:4], szv[:, 2:3], AF.Exp,
                                                      scale=-0.5), key=f"zrs:{c}")
                    P.op("dv", lambda h: h.tensor_scalar(
                        h2st[:, :], zst[:, :], szv[:, 0:1], szv[:, 3:4],
                        ALU.subtract, ALU.mult), after=[f"zrs:{c}"])
                    P.op("dv", lambda h: h.tensor_mul(h2st[:, :], h2st[:, :], lgax[:, :]),
                         after=["bc:lga"])
                    P.op("dv", lambda h: h.tensor_add(h2st[:, :], h2st[:, :], lbax[:, :]),
                         after=["bc:lba"])
                    P.op("dv", (lambda c=c: lambda h: h.tensor_add(
                        x2_st[c % 2][:, :], x2_st[c % 2][:, :], h2st[:, :]))(),
                        after=[f"l2:x:{c}"], key=f"xn:{c}")
                    P.op("gp", (lambda c=c: lambda h: h.dma_start(
                        xsp[:, DM * c: DM * (c + 1)], x2_st[c % 2][:, :]))(),
                        after=[f"xn:{c}"], key=f"sp:x:{c}", dma=True)
                    P.op("dv", (lambda c=c: lambda h: h.bn_stats(
                        szstat[:, 0:6], x2_st[c % 2][:, 0:512]))(), after=[f"xn:{c}"])
                    P.op("dv", (lambda c=c: lambda h: h.bn_stats(
                        szstat[:, 6:12], x2_st[c % 2][:, 512:1024]))())
                    P.op("dv", lambda h: h.bn_aggr(szv[:, 0:2], szstat[:, 0:12]),
                         key=f"xs:{c}")
                    P.op("ac", lambda h: h.activation(szv[:, 2:3], szv[:, 1:2], AF.Ln,
                                                      bias=LN_EPS), after=[f"xs:{c}"])
                    P.op("ac", lambda h: h.activation(szv[:, 3:4], szv[:, 2:3], AF.Exp,
                                                      scale=-0.5), key=f"xrs:{c}")
                    P.op("dv", (lambda c=c: lambda h: h.tensor_scalar(
                        h2st[:, :], x2_st[c % 2][:, :], szv[:, 0:1], szv[:, 3:4],
                        ALU.subtract, ALU.mult))(), after=[f"xrs:{c}"])
                    P.op("dv", lambda h: h.tensor_mul(h2st[:, :], h2st[:, :], cb1x[:, :]),
                         after=["bc:cb1"])
                    P.op("dv", lambda h: h.tensor_add(h2st[:, :], h2st[:, :], sbvx[:, :]),
                         after=["bc:sb"], key=f"h2:{c}")
                    P.op("dv", lambda h: h.bn_stats(szstat[:, 0:6], h2st[:, 0:512]))
                    P.op("dv", lambda h: h.bn_stats(szstat[:, 6:12], h2st[:, 512:1024]))
                    P.op("dv", lambda h: h.bn_aggr(szv[:, 4:6], szstat[:, 0:12]),
                         key=f"h2s:{c}")
                    P.op("ac", lambda h: h.activation(szv[:, 6:7], szv[:, 4:5], AF.Square),
                         after=[f"h2s:{c}"], key=f"h2sq:{c}")
                    P.op("dv", lambda h: h.tensor_add(szv[:, 6:7], szv[:, 6:7], szv[:, 5:6]),
                         after=[f"h2sq:{c}"], key=f"ms:{c}")
                    P.op("ac", lambda h: h.activation(szv[:, 6:7], szv[:, 6:7], AF.Ln,
                                                      bias=RMS_EPS), after=[f"ms:{c}"])
                    P.op("ac", lambda h: h.activation(szv[:, 7:8], szv[:, 6:7], AF.Exp,
                                                      scale=-0.5), key=f"h2rs:{c}")
                    P.op("dv", lambda h: h.tensor_scalar(
                        h2st[:, :], h2st[:, :], szv[:, 7:8], None, ALU.mult),
                        after=[f"h2rs:{c}"])
                    P.op("dv", lambda h: h.tensor_mul(xnst[:, :], h2st[:, :], rmswx[:, :]),
                         after=["ld:rmw"] + ([f"xntr:{c - 1}:7"] if c else []),
                         key=f"xnst:{c}")
                    for b in range(8):
                        pt = ptb2[b % 2]
                        aft = [f"xnst:{c}", f"kte:{c}:1:{6 + (b % 2)}"]
                        if b >= 2:
                            aft.append(f"xnte:{c}:{b - 2}")
                        P.op("te", (lambda pt=pt, b=b: lambda h: h.transpose(
                            pt[:, :], xnst[:, 128 * b: 128 * (b + 1)], sidb[:, :]))(),
                            after=aft, key=f"xntr:{c}:{b}")
                        P.op("ac", (lambda pt=pt, c=c, b=b: lambda h: h.copy(
                            xnT[:, T * b + 128 * c: T * b + 128 * (c + 1)], pt[:, :]))(),
                            after=[f"xntr:{c}:{b}"], key=f"xnte:{c}:{b}")
                    P.op("te", (lambda c=c: lambda h: _seq_mm(
                        h, mu[0][:, 0:8],
                        [xnT[:, T * kb + 128 * c: T * kb + 128 * (c + 1)] for kb in range(8)],
                        [sgateT[:, 8 * kb: 8 * (kb + 1)] for kb in range(8)]))(),
                        after=[f"xnte:{c}:7", f"zdv:{c}:0"], key=f"lg:mm:{c}")
                    P.op("dv", lambda h: h.reduce_max(szv[:, 0:1], mu[0][:, 0:8], AX.X,
                                                      negate=True), after=[f"lg:mm:{c}"])
                    P.op("dv", lambda h: h.reduce_max(szv[:, 1:2], mu[0][:, 0:8], AX.X),
                         key=f"lgmx:{c}")
                    P.op("ac", lambda h: h.activation(sel8[:, :], mu[0][:, 0:8], AF.Exp,
                                                      bias=szv[:, 0:1]),
                         after=[f"lgmx:{c}"], key=f"lgel:{c}")
                    P.op("dv", lambda h: h.reduce_sum(szv[:, 3:4], sel8[:, :], AX.X),
                         after=[f"lgel:{c}"])
                    P.op("dv", lambda h: h.reciprocal(szv[:, 4:5], szv[:, 3:4]),
                         key=f"lgsc:{c}")
                    P.op("dv", (lambda c=c: lambda h: h.tensor_scalar(
                        sel8[:, :], mu[0][:, 0:8], szv[:, 1:2], None, ALU.is_ge))(),
                        after=[f"lgel:{c}", f"lgsc:{c}"], key=f"lgr:{c}")
                    P.op("dv", (lambda c=c: lambda h: h.tensor_scalar(
                        msc[:, 8 * c: 8 * (c + 1)], sel8[:, :], szv[:, 4:5], None,
                        ALU.mult))(), key=f"msc:{c}")
                    P.op("dv", lambda h: h.drain(),
                         after=[f"msc:{c}", f"u:mm:{c}:15", f"zbf:{c}:15", f"sp:x:{c}"],
                         key=f"ch:{c}:done")
                if DEBUG:
                    P.op("gp", lambda h: h.dma_start(d_mix[:, :], mixa[:, :]),
                         after=["ch:15:done"], key="dbg:mix", dma=True)
                    P.op("gp", lambda h: h.dma_start(d_msc[:, :], msc[:, :]),
                         after=["ch:15:done"], key="dbg:msc", dma=True)
                    P.op("gp", lambda h: h.dma_start(d_xnt[:, :], xnT[:, :]),
                         after=["ch:15:done"], key="dbg:xnt", dma=True)
                P.fence("p2")

        with ExitStack() as p3:
            W1bf = p3.enter_context(nc.sbuf_tensor("W1bf", [128, 8 * FF], BF16))
            W2bf = p3.enter_context(nc.sbuf_tensor("W2bf", [128, 32 * DM], BF16))
            hTb = p3.enter_context(nc.sbuf_tensor("hTb", [128, 32 * 512], BF16))
            b1c = p3.enter_context(nc.sbuf_tensor("b1c", [128, 32], F32))
            b1row = p3.enter_context(nc.sbuf_tensor("b1row", [128, 128], F32))
            b2bf = p3.enter_context(nc.sbuf_tensor("b2bf", [1, DM], BF16))
            ctb = [p3.enter_context(nc.sbuf_tensor(f"ctb{i}", [128, 512], F32))
                   for i in range(2)]
            pm1 = [p3.enter_context(nc.psum_tensor(f"pm1{i}", [128, 512], F32))
                   for i in range(2)]
            pm2 = [p3.enter_context(nc.psum_tensor(f"pm2{i}", [128, 512], F32))
                   for i in range(2)]

            uac_keys = []
            P.op("dv", lambda h: h.memset(b1row[:, :], 0.0), key="b1r:z")
            for e in range(E):
                aft1 = [f"m1:{e - 1}:3:31"] if e else []
                P.op("gp", (lambda e=e: lambda h: h.dma_start(
                    W1bf[:, :].rearrange("p (r f) -> p r f", f=FF),
                    bass.AP(W1, e * DM * FF, [[FF, 128], [128 * FF, 8], [1, FF]])))(),
                    after=aft1, key=f"w1ld:{e}", dma=True)
                aft2 = [f"m2:{e - 1}:3:3:1"] if e else []
                P.op("gp", (lambda e=e: lambda h: h.dma_start(
                    W2bf[:, :].rearrange("p (r d) -> p r d", d=DM),
                    bass.AP(W2, e * FF * DM, [[DM, 128], [128 * DM, 32], [1, DM]])))(),
                    after=aft2, key=f"w2ld:{e}", dma=True)
                P.op("sy", (lambda e=e: lambda h: h.dma_start(
                    b1row[0:32, :], bass.AP(b1, e * FF, [[128, 32], [1, 128]])))(),
                    after=([f"b1tr:{e - 1}"] if e else ["b1r:z"]), key=f"b1ld:{e}", dma=True)
                P.op("te", (lambda e=e: lambda h: h.transpose(
                    pm1[0][:, 0:128], b1row[:, :], sidf[:, :]))(),
                    after=[f"b1ld:{e}"] + ([f"m1ev:{e - 1}:3:31"] if e else []),
                    key=f"b1tr:{e}")
                P.op("ac", (lambda e=e: lambda h: h.copy(b1c[:, :], pm1[0][:, 0:32]))(),
                    after=[f"b1tr:{e}"], key=f"b1ev:{e}")
                P.op("gp", (lambda e=e: lambda h: h.dma_start(
                    b2bf[0:1, :], bass.AP(b2, e * DM, [[DM, 1], [1, DM]])))(),
                    after=aft2, key=f"b2ld:{e}", dma=True)
                for q in range(4):
                    for mf in range(32):
                        sl = mf % 2
                        aft = [f"w1ld:{e}"]
                        if q == 0 and mf == 0:
                            aft.append(f"b1ev:{e}")
                        if mf >= 2:
                            aft.append(f"m1ev:{e}:{q}:{mf - 2}")
                        elif q > 0:
                            aft.append(f"m1ev:{e}:{q - 1}:{30 + mf}")
                        elif e > 0:
                            aft.append(f"m1ev:{e - 1}:3:{30 + mf}")
                        P.op("te", (lambda e=e, q=q, mf=mf, sl=sl: lambda h: _seq_mm(
                            h, pm1[sl][:, 0:512],
                            [W1bf[:, FF * kc + 128 * mf: FF * kc + 128 * (mf + 1)]
                             for kc in range(8)],
                            [xnT[:, T * kc + 512 * q: T * kc + 512 * (q + 1)]
                             for kc in range(8)]))(),
                            after=aft, key=f"m1:{e}:{q}:{mf}")
                        P.op("ac", (lambda mf=mf, sl=sl: lambda h: h.activation(
                            hTb[:, 512 * mf: 512 * (mf + 1)], pm1[sl][:, 0:512],
                            AF.Silu, bias=b1c[:, mf:mf + 1]))(),
                            after=[f"m1:{e}:{q}:{mf}", f"b1ev:{e}"],
                            key=f"m1ev:{e}:{q}:{mf}")
                    for tt4 in range(4):
                        tau = 4 * q + tt4
                        for n in range(2):
                            idx = tt4 * 2 + n
                            sl = idx % 2
                            aft = [f"m1ev:{e}:{q}:31", f"w2ld:{e}", f"b2ld:{e}"]
                            if idx >= 2:
                                aft.append(f"m2ev:{e}:{q}:{tt4 - 1}:{n}")
                            elif q > 0:
                                aft.append(f"m2ev:{e}:{q - 1}:3:{n}")
                            elif e > 0:
                                aft.append(f"m2ev:{e - 1}:3:3:{n}")
                            def m2f(e=e, q=q, tt4=tt4, n=n, sl=sl):
                                def f(h):
                                    for kc in range(32):
                                        h.matmul(pm2[sl][:, 0:512],
                                                 hTb[:, 512 * kc + 128 * tt4: 512 * kc + 128 * (tt4 + 1)],
                                                 W2bf[:, DM * kc + 512 * n: DM * kc + 512 * (n + 1)],
                                                 start=(kc == 0), stop=False)
                                    return h.matmul(pm2[sl][:, 0:512], sones[0:1, 0:128],
                                                    b2bf[0:1, 512 * n: 512 * (n + 1)],
                                                    start=False, stop=True)
                                return f
                            P.op("te", m2f(), after=aft, key=f"m2:{e}:{q}:{tt4}:{n}")
                            aftv = [f"m2:{e}:{q}:{tt4}:{n}"]
                            if uac_keys[-2:] and len(uac_keys) >= 2:
                                aftv.append(uac_keys[-2])
                            P.op("ac", (lambda e=e, tau=tau, sl=sl: lambda h: h.activation(
                                ctb[sl][:, :], pm2[sl][:, 0:512], AF.Copy,
                                scale=msc[:, 8 * tau + e: 8 * tau + e + 1]))(),
                                after=aftv, key=f"m2ev:{e}:{q}:{tt4}:{n}")
                            uk = f"uac:{e}:{tau}:{n}"
                            P.op("gp", (lambda e=e, tau=tau, n=n, sl=sl: lambda h: h.dma_start(
                                uacc[:, DM * tau + 512 * n: DM * tau + 512 * (n + 1)],
                                ctb[sl][:, :],
                                accum_op=(ALU.bypass if e == 0 else ALU.add)))(),
                                after=[f"m2ev:{e}:{q}:{tt4}:{n}"], key=uk, dma=True)
                            uac_keys.append(uk)
            P.fence("p3")

        with ExitStack() as p4:
            gbx = p4.enter_context(nc.sbuf_tensor("gbx", [128, DM], F32))
            uf = [p4.enter_context(nc.sbuf_tensor(f"uf{i}", [128, DM], F32))
                  for i in range(2)]
            xf = [p4.enter_context(nc.sbuf_tensor(f"xf{i}", [128, DM], F32))
                  for i in range(2)]
            of = [p4.enter_context(nc.sbuf_tensor(f"of{i}", [128, DM], F32))
                  for i in range(2)]
            P.op("sy", lambda h: h.dma_start(
                gbx[:, :], bass.AP(modsp, 6 * DM, [[0, 128], [1, DM]])), key="bc:gb", dma=True)
            outk = []
            for t in range(NT):
                aft = [f"out:{t - 2}"] if t >= 2 else []
                P.op("sy", (lambda t=t: lambda h: h.dma_start(
                    uf[t % 2][:, :], uacc[:, DM * t: DM * (t + 1)]))(),
                    after=aft + [f"uac:7:{t}:1"], key=f"l4u:{t}", dma=True,
                    stream=f"l4u{t % 2}")
                P.op("sy", (lambda t=t: lambda h: h.dma_start(
                    xf[t % 2][:, :], xsp[:, DM * t: DM * (t + 1)]))(),
                    after=aft, key=f"l4x:{t}", dma=True, stream=f"l4x{t % 2}")
                P.op("dv", (lambda t=t: lambda h: h.tensor_mul(
                    of[t % 2][:, :], uf[t % 2][:, :], gbx[:, :]))(),
                    after=[f"l4u:{t}", "bc:gb"] + ([f"out:{t - 2}"] if t >= 2 else []))
                P.op("dv", (lambda t=t: lambda h: h.tensor_add(
                    of[t % 2][:, :], of[t % 2][:, :], xf[t % 2][:, :]))(),
                    after=[f"l4x:{t}"], key=f"o4:{t}")
                P.op("gp", (lambda t=t: lambda h: h.dma_start(
                    out[128 * t: 128 * (t + 1), :], of[t % 2][:, :]))(),
                    after=[f"o4:{t}"], key=f"out:{t}", dma=True)
                outk.append(f"out:{t}")
            P.op("gp", lambda h: h.drain(), after=outk, key="final")
        if TRUNC is not None:
            if TRUNC in P.keys:
                idx = P.keys[TRUNC]
            else:
                idx = max(P.keys[f"F:{TRUNC}:{e}"] for e in _ENG)
            P.ops = P.ops[:idx + 1]
    P.emit()
    return nc


def make_constants():
    half = 16
    try:
        import jax
        import jax.numpy as jnp
        with jax.default_device(jax.devices("cpu")[0]):
            invf = 1.0 / 10000.0 ** (jnp.arange(half, dtype=jnp.float32) / half)
            fr = jnp.arange(T, dtype=jnp.float32)[:, None] * invf
            cosv = np.asarray(jnp.cos(fr), dtype=np.float32)
            sinv = np.asarray(jnp.sin(fr), dtype=np.float32)
    except Exception:
        invf = np.float32(1.0) / np.float32(10000.0) ** (
            np.arange(half, dtype=np.float32) / np.float32(half))
        fr = (np.arange(T, dtype=np.float32)[:, None] * invf[None, :])
        cosv = np.cos(fr).astype(np.float32)
        sinv = np.sin(fr).astype(np.float32)
    cosT = np.concatenate([cosv, cosv], axis=1)
    sinT = np.concatenate([-sinv, sinv], axis=1)
    cosTl = np.zeros_like(cosT)
    sinTl = np.zeros_like(sinT)
    identf = np.eye(128, dtype=np.float32)
    identb = np.eye(128, dtype=ml_dtypes.bfloat16)
    maskb = np.triu(np.ones((128, 128), dtype=np.float32)).astype(ml_dtypes.bfloat16)
    Lsel = np.zeros((64, 128), dtype=np.float32)
    for hm in range(64):
        for p in range(128):
            if (hm // 2) % 4 == p // 32:
                Lsel[hm, p] = 1.0
    Rsel = np.zeros((64, 16), dtype=np.float32)
    for g in range(16):
        H4, m = g // 2, g % 2
        for hh in range(4):
            Rsel[2 * (4 * H4 + hh) + m, g] = 1.0
    onesb = np.ones((1, 128), dtype=ml_dtypes.bfloat16)
    return dict(identf=identf, identb=identb, maskb=maskb, cosT=cosT, sinT=sinT,
                cosTl=cosTl, sinTl=sinTl,
                Lsel=Lsel, Rsel=Rsel, onesb=onesb,
            fconst=np.array([list(FCONSTS) + [0.0] * (16 - len(FCONSTS))],
                            dtype=np.float32))


def kernel(**inputs):
    nc = build_program()
    consts = make_constants()
    f32 = lambda a: np.ascontiguousarray(np.asarray(a, dtype=np.float32))
    in_maps = []
    for c in range(8):
        m = dict(
            x=f32(inputs["L0_attn_x"][c]),
            cvec=f32(inputs["L0_c"][c:c + 1]),
            Wq=f32(inputs["L0_attn_Wq"]), Wk=f32(inputs["L0_attn_Wk"]),
            Wv=f32(inputs["L0_attn_Wv"]), Wout=f32(inputs["L0_attn_Wout"]),
            Wbeta=f32(inputs["L0_attn_Wbeta"]),
            bbeta=f32(inputs["L0_attn_bbeta"]).reshape(1, 64),
            Wmix=f32(inputs["L0_attn_Wmix"]),
            bmix=f32(inputs["L0_attn_bmix"]).reshape(1, 64),
            boutv=f32(inputs["L0_attn_bout"]).reshape(1, DM),
            ln_g=f32(inputs["L0_attn_ln_g"]).reshape(1, DM),
            ln_b=f32(inputs["L0_attn_ln_b"]).reshape(1, DM),
            rms_w=f32(inputs["L0_ffn_rms_w"]).reshape(1, DM),
            gate_w=f32(inputs["L0_ffn_gate_w"]),
            W1=f32(inputs["L0_ffn_W1"]), b1=f32(inputs["L0_ffn_b1"]),
            W2=f32(inputs["L0_ffn_W2"]), b2=f32(inputs["L0_ffn_b2"]),
            wmod=f32(inputs["L0_wmod"]),
            **consts,
        )
        in_maps.append(m)
    res = run_bass_kernel_spmd(nc, in_maps, core_ids=list(range(8)))
    return np.stack([r["out"] for r in res.results], axis=0)


# version 07
# speedup vs baseline: 1.0571x; 1.0019x over previous
import numpy as np
import ml_dtypes
import concourse.bass as bass
import concourse.mybir as mybir
from concourse.bass_utils import run_bass_kernel_spmd
from contextlib import ExitStack

F32 = mybir.dt.float32
BF16 = mybir.dt.bfloat16
AF = mybir.ActivationFunctionType
ALU = mybir.AluOpType
AX = mybir.AxisListType

T = 2048
DM = 1024
NH = 32
M = 2
C = 128
NCH = T // C
NT = T // 128
E = 8
FF = 4096
VA = 33 * NH
LN_EPS = 1e-5
ATT_EPS = 1e-6
RMS_EPS = 1e-6
TRUNC = None
FCONSTS = (LN_EPS, RMS_EPS, -0.5, -1.0, 0.85, 0.9995, 0.25, -0.25, 1.0, 0.0,
           ATT_EPS, 2.0)
DEBUG = False

_ENG = ("te", "dv", "ac", "gp", "sy")
_BLK = dict(te="tensor", dv="vector", ac="scalar", gp="gpsimd", sy="sync")


class Prog:

    def __init__(self, nc):
        self.nc = nc
        self.ops = []
        self.keys = {}

    def op(self, eng, fn, after=(), key=None, dma=False, stream=None):
        assert eng in _ENG
        if isinstance(after, str):
            after = (after,)
        for k in after:
            assert k in self.keys, f"forward/unknown dep {k!r}"
        if key is not None:
            assert key not in self.keys, f"dup key {key!r}"
            self.keys[key] = len(self.ops)
        if dma and stream is None:
            assert key is not None
            parts = key.split(":")
            while parts and parts[-1].isdigit():
                parts.pop()
            stream = ":".join(parts) or key
        self.ops.append([eng, fn, tuple(after), key, dma, stream])

    def fence(self, tag):
        last = {}
        for i, o in enumerate(self.ops):
            last[o[0]] = i
        lkeys = []
        for eng, i in last.items():
            key = self.ops[i][3]
            if key is None:
                key = f"_f:{tag}:{eng}"
                self.ops[i][3] = key
                self.keys[key] = i
            lkeys.append(key)
        for eng in _ENG:
            self.op(eng, lambda h: h.drain(), after=tuple(lkeys), key=f"F:{tag}:{eng}")

    def lint(self):
        return

    def emit(self):
        awaited = set(k for o in self.ops for k in o[2])
        stream_idx = {}
        for i, o in enumerate(self.ops):
            if o[4]:
                stream_idx.setdefault(o[5], []).append(i)
        sems = {e: self.nc.alloc_semaphore(name=f"S_{e}") for e in _ENG}
        ssems = {s: self.nc.alloc_semaphore(name=f"D_{s}"[:31])
                 for s in stream_idx}
        cnt = dict.fromkeys(_ENG, 0)
        keyval = {}
        for eng, fn, after, key, dma, stream in self.ops:
            if dma:
                continue
            if key in awaited:
                cnt[eng] += 1
                keyval[key] = (eng, cnt[eng])

        def dma_wait(k, widx):
            oi = self.keys[k]
            s = self.ops[oi][5]
            lst = stream_idx[s]
            npre = sum(1 for i in lst if i < widx)
            pos = lst.index(oi) + 1
            assert npre >= pos, f"dep {k} after waiter?"
            return ssems[s], 16 * npre

        with self.nc.Block() as blk:
            def mkrun(eng):
                def run(h):
                    done = {}
                    for widx, (oeng, fn, after, key, dma, stream) in enumerate(self.ops):
                        if oeng != eng:
                            continue
                        for k in after:
                            if self.ops[self.keys[k]][4]:
                                sem, v = dma_wait(k, widx)
                            else:
                                se, v = keyval[k]
                                if se == eng:
                                    continue
                                sem = sems[se]
                            sid = sem.name if hasattr(sem, "name") else id(sem)
                            if done.get(sid, -1) >= v:
                                continue
                            h.wait_ge(sem, v)
                            done[sid] = v
                        inst = fn(h)
                        if not dma and eng in ("dv", "ac"):
                            fs = 9999
                            try:
                                outs = inst.ins.outs
                                for oo in outs:
                                    ba = getattr(oo, "bass_ap", None)
                                    if ba is not None:
                                        fs = min(fs, ba.free_size())
                            except Exception:
                                pass
                            if fs < 96:
                                inst = h.drain()
                        if dma:
                            inst.then_inc(ssems[stream], 16)
                        elif key in awaited:
                            inst.then_inc(sems[eng], 1)
                return run
            for eng in _ENG:
                getattr(blk, _BLK[eng])(mkrun(eng))
        return cnt


def apx(t, col0, dims, part0=0, nparts=128):
    pitch = int(np.prod(t.shape[1:]))
    return bass.AP(t, part0 * pitch + col0, [[pitch, nparts]] + [list(d) for d in dims])


def _seq_mm(h, psum_ap, lhs_list, rhs_list):
    r = None
    n = len(lhs_list)
    for i, (l, rr) in enumerate(zip(lhs_list, rhs_list)):
        r = h.matmul(psum_ap, l, rr, start=(i == 0), stop=(i == n - 1))
    return r


def build_program():
    nc = bass.Bass("TRN2", target_bir_lowering=False, detect_race_conditions=False)

    def din(name, shape, dtype=F32):
        return nc.dram_tensor(name, list(shape), dtype, kind="ExternalInput")

    x_hbm = din("x", [T, DM])
    cvec = din("cvec", [1, DM])
    Wq = din("Wq", [DM, DM]); Wk = din("Wk", [DM, DM]); Wv = din("Wv", [DM, DM])
    Wout = din("Wout", [DM, DM])
    Wbeta = din("Wbeta", [64, DM]); bbeta = din("bbeta", [1, 64])
    Wmix = din("Wmix", [64, DM]); bmix = din("bmix", [1, 64])
    boutv = din("boutv", [1, DM])
    ln_g = din("ln_g", [1, DM]); ln_b = din("ln_b", [1, DM])
    rms_w = din("rms_w", [1, DM])
    gate_w = din("gate_w", [E, DM])
    W1 = din("W1", [E, DM, FF]); b1 = din("b1", [E, FF])
    W2 = din("W2", [E, FF, DM]); b2 = din("b2", [E, DM])
    wmod = din("wmod", [DM, 6 * DM])
    identf = din("identf", [128, 128])
    identb = din("identb", [128, 128], BF16)
    maskb = din("maskb", [128, 128], BF16)
    cosTd = din("cosT", [T, 32])
    sinTd = din("sinT", [T, 32])
    cosTl = din("cosTl", [T, 32])
    sinTl = din("sinTl", [T, 32])
    Lsel = din("Lsel", [64, 128])
    Rsel = din("Rsel", [64, 16])
    onesb = din("onesb", [1, 128], BF16)
    fconst = din("fconst", [1, 16])

    out = nc.dram_tensor("out", [T, DM], F32, kind="ExternalOutput")
    sk = "ExternalOutput" if DEBUG else "Internal"
    h1sp = nc.dram_tensor("h1sp", [128, NT * DM], F32, kind=sk)
    xsp = nc.dram_tensor("xsp", [128, NT * DM], F32, kind=sk)
    phiqsp = nc.dram_tensor("phiqsp", [128, 8 * T], BF16, kind=sk)
    ktmsp = nc.dram_tensor("ktmsp", [128, NT * DM], BF16, kind=sk)
    vaugsp = nc.dram_tensor("vaugsp", [128, NT * VA], BF16, kind=sk)
    uacc = nc.dram_tensor("uacc", [128, NT * DM], F32, kind=sk)
    modsp = nc.dram_tensor("modsp", [16, DM], F32, kind=sk)
    if DEBUG:
        d_logb = nc.dram_tensor("d_logb", [64, T], F32, kind="ExternalOutput")
        d_mix = nc.dram_tensor("d_mix", [128, NT * 64], F32, kind="ExternalOutput")
        d_xnt = nc.dram_tensor("d_xnt", [128, 8 * T], BF16, kind="ExternalOutput")
        d_msc = nc.dram_tensor("d_msc", [128, NT * 8], F32, kind="ExternalOutput")
        d_h1t = nc.dram_tensor("d_h1t", [128, 8 * T], BF16, kind="ExternalOutput")

    P = Prog(nc)
    with ExitStack() as top:
        xnT = top.enter_context(nc.sbuf_tensor("xnT", [128, 8 * T], BF16))
        msc = top.enter_context(nc.sbuf_tensor("msc", [128, NT * 8], F32))
        sidf = top.enter_context(nc.sbuf_tensor("sidf", [128, 128], F32))
        sidb = top.enter_context(nc.sbuf_tensor("sidb", [128, 128], BF16))
        smask = top.enter_context(nc.sbuf_tensor("smask", [128, 128], BF16))
        sones = top.enter_context(nc.sbuf_tensor("sones", [1, 128], BF16))
        sgateT = top.enter_context(nc.sbuf_tensor("sgateT", [128, 64], BF16))
        sfc = top.enter_context(nc.sbuf_tensor("sfc", [128, 16], F32))
        P.op("sy", lambda h: h.dma_start(
            sfc[:, :], bass.AP(fconst, 0, [[0, 128], [1, 16]])), key="ld:fc", dma=True)
        P.fence("fc")
        for fi, fv in enumerate(FCONSTS):
            nc.const_aps.register(F32, fv, sfc[:, fi:fi + 1])

        with ExitStack() as mid:
            scosT = mid.enter_context(nc.sbuf_tensor("scosT", [128, NT * 32], F32))
            ssinT = mid.enter_context(nc.sbuf_tensor("ssinT", [128, NT * 32], F32))
            sLsel = mid.enter_context(nc.sbuf_tensor("sLsel", [64, 128], F32))
            sRsel = mid.enter_context(nc.sbuf_tensor("sRsel", [64, 16], F32))
            snbb = mid.enter_context(nc.sbuf_tensor("snbb", [64, 1], F32))
            WoT = mid.enter_context(nc.sbuf_tensor("WoT", [128, 8 * DM], BF16))
            WbT = mid.enter_context(nc.sbuf_tensor("WbT", [128, 8 * 64], BF16))
            WmT = mid.enter_context(nc.sbuf_tensor("WmT", [128, 8 * 64], BF16))
            logb = mid.enter_context(nc.sbuf_tensor("logb", [64, T], F32))
            mixa = mid.enter_context(nc.sbuf_tensor("mixa", [128, NT * 64], F32))
            colv = mid.enter_context(nc.sbuf_tensor("colv", [128, 96], F32))
            (CV_CA1, CV_SA, CV_CB1, CV_SB, CV_LGA, CV_LBA, CV_GB, CV_T0,
             CV_GA, CV_LNG, CV_LNB, CV_T1) = [8 * i for i in range(12)]

            def cv(off, w=8):
                return colv[:, off: off + w]

            with ExitStack() as s05:
                WqT = s05.enter_context(nc.sbuf_tensor("WqT", [128, 8 * DM], BF16))
                WkT = s05.enter_context(nc.sbuf_tensor("WkT", [128, 8 * DM], BF16))
                WvT = s05.enter_context(nc.sbuf_tensor("WvT", [128, 8 * DM], BF16))
                h1T = s05.enter_context(nc.sbuf_tensor("h1T", [128, 8 * T], BF16))

                with ExitStack() as p0:
                    wst = p0.enter_context(nc.sbuf_tensor("wst", [128, 8 * DM], BF16))
                    wbst = p0.enter_context(nc.sbuf_tensor("wbst", [128, DM], BF16))
                    wmst = p0.enter_context(nc.sbuf_tensor("wmst", [128, DM], BF16))
                    wgst = p0.enter_context(nc.sbuf_tensor("wgst", [128, DM], BF16))
                    modst = [p0.enter_context(nc.sbuf_tensor(f"modst{i}", [128, DM], BF16))
                             for i in range(2)]
                    tlo = p0.enter_context(nc.sbuf_tensor("tlo", [128, 1024], F32))
                    smodT = p0.enter_context(nc.sbuf_tensor("smodT", [128, 48], F32))
                    scv = p0.enter_context(nc.sbuf_tensor("scv", [128, 8], F32))
                    srow8 = p0.enter_context(nc.sbuf_tensor("srow8", [128, 128], F32))
                    scvb = p0.enter_context(nc.sbuf_tensor("scvb", [128, 8], BF16))
                    sbbv = p0.enter_context(nc.sbuf_tensor("sbbv", [64, 1], F32))
                    sbmc = p0.enter_context(nc.sbuf_tensor("sbmc", [32, 2], F32))
                    pmod = p0.enter_context(nc.psum_tensor("pmod", [128, 512], F32))
                    ptb = [p0.enter_context(nc.psum_tensor(f"p0tb{i}", [128, 128], BF16))
                           for i in range(2)]

                    for nm, dst, src in (
                        ("idf", sidf[:, :], identf[:, :]),
                        ("idb", sidb[:, :], identb[:, :]),
                        ("msk", smask[:, :], maskb[:, :]),
                        ("one", sones[:, :], onesb[:, :]),
                        ("lsl", sLsel[:, :], Lsel[:, :]),
                        ("rsl", sRsel[:, :], Rsel[:, :]),
                        ("cosT", scosT[:, :].rearrange("p (t f) -> p t f", f=32),
                         bass.AP(cosTd, 0, [[32, 128], [32 * 128, NT], [1, 32]])),
                        ("sinT", ssinT[:, :].rearrange("p (t f) -> p t f", f=32),
                         bass.AP(sinTd, 0, [[32, 128], [32 * 128, NT], [1, 32]])),
                        ("cosTl", tlo[:, 0:512].rearrange("p (t f) -> p t f", f=32),
                         bass.AP(cosTl, 0, [[32, 128], [32 * 128, NT], [1, 32]])),
                        ("sinTl", tlo[:, 512:1024].rearrange("p (t f) -> p t f", f=32),
                         bass.AP(sinTl, 0, [[32, 128], [32 * 128, NT], [1, 32]])),
                        ("bb", sbbv[:, :], bass.AP(bbeta, 0, [[1, 64], [1, 1]])),
                        ("bmc", sbmc[:, :], bass.AP(bmix, 0, [[2, 32], [1, 2]])),
                    ):
                        P.op("sy", (lambda d, s: lambda h: h.dma_start(d, s))(dst, src),
                             key=f"ld:{nm}", dma=True)

                    P.op("dv", lambda h: h.tensor_add(scosT[:, :], scosT[:, :],
                                                      tlo[:, 0:512]),
                         after=["ld:cosT", "ld:cosTl"], key="cosfix")
                    P.op("dv", lambda h: h.tensor_add(ssinT[:, :], ssinT[:, :],
                                                      tlo[:, 512:1024]),
                         after=["ld:sinT", "ld:sinTl"], key="sinfix")
                    def w_src(W, nch, wid):
                        return bass.AP(W, 0, [[wid, 128], [128 * wid, nch], [1, wid]])

                    for wi, (W, dstT) in enumerate(((Wq, WqT), (Wk, WkT), (Wv, WvT), (Wout, WoT))):
                        after = [f"wtr:{wi - 1}:63"] if wi else []
                        P.op("gp", (lambda W=W: lambda h: h.dma_start(
                            wst[:, :].rearrange("p (r d) -> p r d", d=DM), w_src(W, 8, DM)))(),
                            after=after, key=f"wld:{wi}", dma=True)
                        for i in range(64):
                            r, dc = i // 8, i % 8
                            pt = ptb[i % 2]
                            aft = [f"wld:{wi}", "ld:idb"]
                            if i >= 2:
                                aft.append(f"wev:{wi}:{i - 2}")
                            elif wi > 0:
                                aft.append(f"wev:{wi - 1}:{62 + i}")
                            P.op("te", (lambda pt=pt, r=r, dc=dc: lambda h: h.transpose(
                                pt[:, :], wst[:, DM * r + 128 * dc: DM * r + 128 * (dc + 1)],
                                sidb[:, :]))(),
                                after=aft, key=f"wtr:{wi}:{i}")
                            P.op("dv", (lambda pt=pt, dstT=dstT, r=r, dc=dc: lambda h: h.tensor_copy(
                                dstT[:, DM * dc + 128 * r: DM * dc + 128 * (r + 1)], pt[:, :]))(),
                                after=[f"wtr:{wi}:{i}"], key=f"wev:{wi}:{i}")
                    P.op("dv", lambda h: h.memset(wbst[:, :], 0.0), key="wz:b")
                    P.op("dv", lambda h: h.memset(wmst[:, :], 0.0), key="wz:m")
                    P.op("dv", lambda h: h.memset(wgst[:, :], 0.0), key="wz:g")
                    P.op("gp", lambda h: h.dma_start(wbst[0:64, :], Wbeta[:, :]), after=["wz:b"], key="wld:b", dma=True)
                    P.op("gp", lambda h: h.dma_start(wmst[0:64, :], Wmix[:, :]), after=["wz:m"], key="wld:m", dma=True)
                    P.op("gp", lambda h: h.dma_start(wgst[0:8, :], gate_w[:, :]), after=["wz:g"], key="wld:g", dma=True)
                    for si, (st, rows, dstT, wd, ldk) in enumerate((
                            (wbst, 64, WbT, 64, "wld:b"), (wmst, 64, WmT, 64, "wld:m"),
                            (wgst, 8, sgateT, 8, "wld:g"))):
                        for dc in range(8):
                            i = si * 8 + dc
                            pt = ptb[i % 2]
                            aft = [ldk, "wev:3:63", "wz:b", "wz:m", "wz:g"]
                            if i >= 2:
                                aft.append(f"swev:{i - 2}")
                            P.op("te", (lambda pt=pt, st=st, dc=dc: lambda h: h.transpose(
                                pt[:, :], st[:, 128 * dc: 128 * (dc + 1)],
                                sidb[:, :]))(),
                                after=aft, key=f"swtr:{i}")
                            P.op("dv", (lambda pt=pt, dstT=dstT, wd=wd, dc=dc, rows=rows:
                                        lambda h: h.tensor_copy(
                                dstT[:, wd * dc: wd * (dc + 1)], pt[0:128, 0:rows]))(),
                                after=[f"swtr:{i}"], key=f"swev:{i}")

                    P.op("dv", lambda h: h.memset(srow8[:, :], 0.0), key="r8:z")
                    prevk = []
                    for nm, dsrc, dst_ap in (
                            ("cv", cvec, scv[:, :]),
                            ("lngc", ln_g, apx(colv, CV_LNG, [[1, 8]])),
                            ("lnbc", ln_b, apx(colv, CV_LNB, [[1, 8]]))):
                        P.op("sy", (lambda dsrc=dsrc: lambda h: h.dma_start(
                            srow8[0:8, :], bass.AP(dsrc, 0, [[128, 8], [1, 128]])))(),
                            after=["r8:z"] + [k for k in prevk if k.startswith("cvt")],
                            key=f"rld:{nm}", dma=True)
                        P.op("te", (lambda: lambda h: h.transpose(
                            pmod[:, 128:256], srow8[:, :], sidf[:, :]))(),
                            after=[f"rld:{nm}", "ld:idf"] +
                                  [k for k in prevk if k.startswith("cve")],
                            key=f"cvt:{nm}")
                        P.op("ac", (lambda dst_ap=dst_ap: lambda h: h.copy(
                            dst_ap, pmod[:, 128:136]))(),
                            after=[f"cvt:{nm}"], key=f"cve:{nm}")
                        prevk = [f"cvt:{nm}", f"cve:{nm}"]
                    P.op("ac", lambda h: h.activation(scv[:, :], scv[:, :], AF.Silu),
                         after=["cve:cv"], key="silc")
                    P.op("dv", lambda h: h.tensor_copy(scvb[:, :], scv[:, :]),
                         after=["silc"], key="silcb")
                    P.op("ac", lambda h: h.mul(snbb[:, :], sbbv[:, :], -1.0),
                         after=["ld:bb"], key="nbb")
                    for jt in range(48):
                        mst = modst[jt % 2]
                        aft = [] if jt < 2 else [f"modmm:{jt - 2}"]
                        P.op("gp", (lambda jt=jt, mst=mst: lambda h: h.dma_start(
                            mst[:, :].rearrange("p (kc j) -> p kc j", j=128),
                            bass.AP(wmod, 128 * jt, [[6 * DM, 128], [128 * 6 * DM, 8], [1, 128]])))(),
                            after=aft, key=f"modld:{jt}", dma=True, stream=f"mod{jt % 2}")
                        P.op("te", (lambda mst=mst: lambda h: _seq_mm(
                            h, pmod[:, 0:1],
                            [mst[:, 128 * kc: 128 * (kc + 1)] for kc in range(8)],
                            [scvb[:, kc:kc + 1] for kc in range(8)]))(),
                            after=[f"modld:{jt}", "silcb"] +
                                  ([f"modev:{jt - 1}"] if jt else ["cve:lnbc"]),
                            key=f"modmm:{jt}")
                        P.op("ac", (lambda jt=jt: lambda h: h.copy(
                            smodT[:, jt:jt + 1], pmod[:, 0:1]))(),
                            after=[f"modmm:{jt}"], key=f"modev:{jt}")

                    def mTn(i):
                        return smodT[:, 8 * i: 8 * (i + 1)]
                    P.op("dv", lambda h: h.tensor_scalar_add(cv(CV_CA1), mTn(1), 1.0),
                         after=["modev:47"], key="v:ca1")
                    P.op("dv", lambda h: h.tensor_copy(cv(CV_SA), mTn(0)), key="v:sa")
                    P.op("dv", lambda h: h.tensor_scalar_add(cv(CV_CB1), mTn(4), 1.0), key="v:cb1")
                    P.op("dv", lambda h: h.tensor_copy(cv(CV_SB), mTn(3)), key="v:sb")
                    for gi, (src_i, dst) in enumerate(((2, CV_GA), (5, CV_GB))):
                        P.op("dv", (lambda src_i=src_i: lambda h: h.tensor_single_scalar(
                            cv(CV_T0), mTn(src_i), 0.0, ALU.is_ge))())
                        P.op("dv", lambda h: h.tensor_scalar(
                            cv(CV_T0), cv(CV_T0), 2.0, -1.0, ALU.mult, ALU.add))
                        P.op("dv", (lambda src_i=src_i: lambda h: h.tensor_scalar_mul(
                            cv(CV_T1), mTn(src_i), -1.0))())
                        P.op("dv", (lambda src_i=src_i: lambda h: h.tensor_tensor(
                            cv(CV_T1), cv(CV_T1), mTn(src_i), ALU.max))())
                        P.op("dv", (lambda gi=gi: lambda h: h.tensor_scalar_max(
                            cv(CV_T1), cv(CV_T1), 0.25))())
                        P.op("dv", (lambda dst=dst: lambda h: h.tensor_mul(
                            cv(dst), cv(CV_T0), cv(CV_T1)))(), key=f"g:sel:{gi}")
                    P.op("dv", lambda h: h.tensor_mul(cv(CV_LGA), cv(CV_LNG), cv(CV_GA)),
                         after=["g:sel:0", "cve:lngc"])
                    P.op("dv", lambda h: h.tensor_mul(cv(CV_LBA), cv(CV_LNB), cv(CV_GA)),
                         after=["cve:lnbc"])
                    P.op("dv", lambda h: h.tensor_tensor(
                        colv[0:32, CV_T0:CV_T0 + 1], sbmc[0:32, 0:1], sbmc[0:32, 1:2],
                        ALU.subtract), after=["ld:bmc"], key="v:dbm")

                    P.op("te", lambda h: h.transpose(
                        pmod[0:96, 128:256], colv[:, 0:96], sidf[:, :]),
                        after=["v:dbm", "modev:47"], key="cvt:sp")
                    P.op("ac", lambda h: h.copy(srow8[0:96, :], pmod[0:96, 128:256]),
                         after=["cvt:sp"], key="cve:sp")
                    P.op("gp", lambda h: h.dma_start(
                        bass.AP(modsp, 0, [[128, 96], [1, 128]]), srow8[0:96, :]),
                        after=["cve:sp"], key="sp:mod1", dma=True)
                    P.fence("p0")

                with ExitStack() as p1:
                    ca1x = p1.enter_context(nc.sbuf_tensor("ca1x", [128, DM], F32))
                    sax = p1.enter_context(nc.sbuf_tensor("sax", [128, DM], F32))
                    dbmx = p1.enter_context(nc.sbuf_tensor("dbmx", [128, 32], F32))
                    x_st = [p1.enter_context(nc.sbuf_tensor(f"x_st{i}", [128, DM], F32))
                            for i in range(2)]
                    h1_st = [p1.enter_context(nc.sbuf_tensor(f"h1_st{i}", [128, DM], F32))
                             for i in range(2)]
                    sstat = p1.enter_context(nc.sbuf_tensor("sstat", [128, 16], F32))
                    smv = p1.enter_context(nc.sbuf_tensor("smv", [128, 8], F32))
                    kraw1 = p1.enter_context(nc.sbuf_tensor("kraw1", [128, DM], F32))
                    krop = p1.enter_context(nc.sbuf_tensor("krop", [128, DM], F32))
                    krop2 = p1.enter_context(nc.sbuf_tensor("krop2", [128, DM], F32))
                    kbf = [p1.enter_context(nc.sbuf_tensor(f"kbf{i}", [128, DM], BF16))
                           for i in range(2)]
                    va_st = [p1.enter_context(nc.sbuf_tensor(f"va_st{i}", [128, VA], BF16))
                             for i in range(2)]
                    qtm1 = p1.enter_context(nc.sbuf_tensor("qtm1", [128, DM], BF16))
                    qfm1 = p1.enter_context(nc.sbuf_tensor("qfm1", [128, DM], BF16))
                    bst = p1.enter_context(nc.sbuf_tensor("bst", [64, 512], F32))
                    bst2 = p1.enter_context(nc.sbuf_tensor("bst2", [64, 512], F32))
                    mst1 = p1.enter_context(nc.sbuf_tensor("mst1", [128, 32], F32))
                    smix64 = p1.enter_context(nc.sbuf_tensor("smix64", [128, 64], F32))
                    mst2 = p1.enter_context(nc.sbuf_tensor("mst2", [128, 32], F32))
                    pp = [p1.enter_context(nc.psum_tensor(f"pp{i}", [128, 512], F32))
                          for i in range(2)]
                    ptf = [p1.enter_context(nc.psum_tensor(f"ptf{i}", [128, 128], F32))
                           for i in range(2)]
                    ptb1 = [p1.enter_context(nc.psum_tensor(f"p1tb{i}", [128, 128], BF16))
                            for i in range(2)]

                    def cvread(nm, dsttile, vi, deps=("sp:mod1",)):
                        P.op("sy", (lambda dsttile=dsttile, vi=vi: lambda h: h.dma_start(
                            dsttile[:, :],
                            bass.AP(modsp, vi * DM, [[0, 128], [1, DM]])))(),
                            after=deps, key=f"bc:{nm}", dma=True)
                    cvread("ca1", ca1x, 0)
                    cvread("sa", sax, 1)
                    P.op("sy", lambda h: h.dma_start(
                        dbmx[:, :], bass.AP(modsp, 7 * DM, [[0, 128], [1, 32]])),
                        after=["sp:mod1"], key="bc:dbm", dma=True)

                    for t in range(NT):
                        xs, hs = x_st[t % 2], h1_st[t % 2]
                        aft = [] if t < 2 else [f"h1:{t - 2}", f"h1tr:{t - 2}:7",
                                                      f"sp:h1:{t - 2}"]
                        P.op("sy", (lambda t=t, xs=xs: lambda h: h.dma_start(
                            xs[:, :], x_hbm[128 * t: 128 * (t + 1), :]))(),
                            after=aft, key=f"ld:x:{t}", dma=True, stream=f"x{t % 2}")
                        P.op("dv", (lambda xs=xs: lambda h: h.bn_stats(
                            sstat[:, 0:6], xs[:, 0:512]))(), after=[f"ld:x:{t}"])
                        P.op("dv", (lambda xs=xs: lambda h: h.bn_stats(
                            sstat[:, 6:12], xs[:, 512:1024]))())
                        P.op("dv", lambda h: h.bn_aggr(smv[:, 0:2], sstat[:, 0:12]),
                             key=f"h1s:{t}")
                        P.op("ac", lambda h: h.activation(smv[:, 2:3], smv[:, 1:2], AF.Ln,
                                                          bias=LN_EPS), after=[f"h1s:{t}"])
                        P.op("ac", lambda h: h.activation(smv[:, 3:4], smv[:, 2:3], AF.Exp,
                                                          scale=-0.5), key=f"h1rs:{t}")
                        P.op("dv", (lambda xs=xs, hs=hs: lambda h: h.tensor_scalar(
                            hs[:, :], xs[:, :], smv[:, 0:1], smv[:, 3:4],
                            ALU.subtract, ALU.mult))(),
                            after=[f"h1rs:{t}"] + ([f"h1tr:{t - 2}:7", f"sp:h1:{t - 2}"]
                                                   if t >= 2 else []))
                        P.op("dv", (lambda hs=hs: lambda h: h.tensor_mul(
                            hs[:, :], hs[:, :], ca1x[:, :]))(), after=["bc:ca1"])
                        P.op("dv", (lambda hs=hs: lambda h: h.tensor_add(
                            hs[:, :], hs[:, :], sax[:, :]))(), after=["bc:sa"], key=f"h1:{t}")
                        P.op("gp", (lambda t=t, hs=hs: lambda h: h.dma_start(
                            h1sp[:, DM * t: DM * (t + 1)], hs[:, :]))(),
                            after=[f"h1:{t}"], key=f"sp:h1:{t}", dma=True)
                        for b in range(8):
                            pt = ptf[b % 2]
                            aft = [f"h1:{t}", "ld:idf"]
                            if t * 8 + b >= 2:
                                pb, pbb = divmod(t * 8 + b - 2, 8)
                                aft.append(f"h1te:{pb}:{pbb}")
                            P.op("te", (lambda pt=pt, hs=hs, b=b: lambda h: h.transpose(
                                pt[:, :], hs[:, 128 * b: 128 * (b + 1)], sidf[:, :]))(),
                                after=aft, key=f"h1tr:{t}:{b}")
                            P.op("ac", (lambda pt=pt, t=t, b=b: lambda h: h.copy(
                                h1T[:, T * b + 128 * t: T * b + 128 * (t + 1)], pt[:, :]))(),
                                after=[f"h1tr:{t}:{b}"], key=f"h1te:{t}:{b}")
                    P.fence("p05")
                    if DEBUG:
                        P.op("gp", lambda h: h.dma_start(d_h1t[:, :], h1T[:, :]),
                             key="dbg:h1t", dma=True)

                    for n in range(4):
                        ppn = pp[n % 2]
                        aft = [] if n < 2 else [f"bt:ev:{n - 2}"]
                        P.op("te", (lambda n=n, ppn=ppn: lambda h: _seq_mm(
                            h, ppn[0:64, 0:512],
                            [WbT[:, 64 * kc: 64 * (kc + 1)] for kc in range(8)],
                            [h1T[:, T * kc + 512 * n: T * kc + 512 * (n + 1)] for kc in range(8)]))(),
                            after=aft, key=f"bt:mm:{n}")
                        P.op("ac", (lambda ppn=ppn: lambda h: h.activation(
                            bst[0:64, 0:512], ppn[0:64, 0:512], AF.Exp,
                            bias=snbb[:, :], scale=-1.0))(),
                            after=[f"bt:mm:{n}", "nbb"] +
                                  ([f"bt:cl:{n - 1}"] if n else []), key=f"bt:ev:{n}")
                        P.op("dv", lambda h: h.tensor_scalar_add(bst2[0:64, :], bst[0:64, :], 1.0),
                             after=[f"bt:ev:{n}"])
                        P.op("dv", lambda h: h.reciprocal(bst[0:64, :], bst2[0:64, :]))
                        P.op("dv", lambda h: h.tensor_scalar(
                            bst2[0:64, :], bst[0:64, :], 0.85, 0.9995, ALU.max, ALU.min),
                            key=f"bt:cl:{n}")
                        P.op("ac", (lambda n=n: lambda h: h.activation(
                            logb[:, 512 * n: 512 * (n + 1)], bst2[0:64, :], AF.Ln))(),
                            after=[f"bt:cl:{n}"], key=f"bt:ln:{n}")
                    if DEBUG:
                        P.op("gp", lambda h: h.dma_start(d_logb[:, :], logb[:, :]),
                             after=[f"bt:ln:{n}" for n in range(4)], key="dbg:logb", dma=True)

                    def proj_mm(t, n, ppn, WT):
                        def f(h):
                            return _seq_mm(
                                h, ppn[:, 0:512],
                                [h1T[:, T * kc + 128 * t: T * kc + 128 * (t + 1)] for kc in range(8)],
                                [WT[:, DM * kc + 512 * n: DM * kc + 512 * (n + 1)] for kc in range(8)])
                        return f

                    for t in range(NT):
                        ppn = pp[0]
                        aft = (["bt:ev:2", "bt:ev:3"] if t == 0
                               else [f"mx:w:{t - 1}", f"qph:{t - 1}"])
                        P.op("te", (lambda t=t, ppn=ppn: lambda h: _seq_mm(
                            h, ppn[:, 0:64],
                            [h1T[:, T * kc + 128 * t: T * kc + 128 * (t + 1)] for kc in range(8)],
                            [WmT[:, 64 * kc: 64 * (kc + 1)] for kc in range(8)]))(),
                            after=aft, key=f"mx:mm:{t}")
                        P.op("ac", (lambda ppn=ppn: lambda h: h.copy(
                            smix64[:, :], ppn[:, 0:64]))(),
                            after=[f"mx:mm:{t}"] + ([f"mx:d:{t - 1}"] if t else []),
                            key=f"mx:ev:{t}")
                        P.op("dv", (lambda: lambda h: h.tensor_tensor(
                            mst1[:, :], apx(smix64, 0, [[2, 32]]), apx(smix64, 1, [[2, 32]]),
                            ALU.subtract))(),
                            after=[f"mx:ev:{t}"] + ([f"mx:e:{t - 1}"] if t else []),
                            key=f"mx:d:{t}")
                        P.op("dv", lambda h: h.tensor_add(mst1[:, :], mst1[:, :], dbmx[:, :]),
                             after=["bc:dbm"], key=f"mx:d2:{t}")
                        P.op("ac", lambda h: h.activation(mst2[:, :], mst1[:, :], AF.Exp,
                                                          scale=-1.0),
                             after=[f"mx:d2:{t}"], key=f"mx:e:{t}")
                        P.op("dv", lambda h: h.tensor_scalar_add(mst1[:, :], mst2[:, :], 1.0),
                             after=[f"mx:e:{t}"])
                        P.op("dv", (lambda t=t: lambda h: h.reciprocal(
                            apx(mixa, 64 * t, [[2, 32]]), mst1[:, :]))())
                        P.op("dv", (lambda t=t: lambda h: h.tensor_scalar(
                            apx(mixa, 64 * t + 1, [[2, 32]]), apx(mixa, 64 * t, [[2, 32]]),
                            -1.0, 1.0, ALU.mult, ALU.add))(), key=f"mx:w:{t}")

                        va = va_st[t % 2]
                        for n in range(2):
                            ppn = pp[1] if n == 0 else pp[0]
                            aft = ([f"mx:d:{t}"] if n == 1 else
                                   ([f"qph:{t - 1}"] if t else []) +
                                   ([f"sp:va:{t - 2}"] if t >= 2 else []))
                            P.op("te", proj_mm(t, n, ppn, WvT), after=aft, key=f"v:mm:{t}:{n}")
                            P.op("ac", (lambda va=va, n=n, ppn=ppn: lambda h: h.activation(
                                apx(va, 33 * 16 * n, [[33, 16], [1, 32]]), ppn[:, 0:512],
                                AF.Copy))(), after=[f"v:mm:{t}:{n}"], key=f"v:ev:{t}:{n}")
                        P.op("dv", (lambda va=va: lambda h: h.memset(
                            apx(va, 32, [[33, 32]]), 1.0))(),
                            after=[f"v:ev:{t}:0", f"v:ev:{t}:1"], key=f"v:on:{t}")
                        P.op("gp", (lambda t=t, va=va: lambda h: h.dma_start(
                            vaugsp[:, VA * t: VA * (t + 1)], va[:, :]))(),
                            after=[f"v:on:{t}"], key=f"sp:va:{t}", dma=True)

                        for which, WT in (("k", WkT), ("q", WqT)):
                            kr = kraw1
                            prev_done = f"qph:{t - 1}" if t >= 1 else None
                            for n in range(2):
                                ppn = pp[n]
                                aft = []
                                if which == "k" and n == 0:
                                    aft = [f"v:ev:{t}:1"]
                                if which == "q" and n == 0:
                                    aft = [f"kph:{t}"]
                                if prev_done:
                                    aft.append(prev_done)
                                P.op("te", proj_mm(t, n, ppn, WT), after=aft,
                                     key=f"{which}:mm:{t}:{n}")
                                P.op("ac", (lambda kr=kr, n=n, ppn=ppn: lambda h: h.copy(
                                    kr[:, 512 * n: 512 * (n + 1)], ppn[:, 0:512]))(),
                                    after=[f"{which}:mm:{t}:{n}"], key=f"{which}:ev:{t}:{n}")
                            cos_ap = apx(scosT, 32 * t, [[0, 32], [1, 32]])
                            sin_lo = apx(ssinT, 32 * t, [[0, 32], [1, 16]])
                            sin_hi = apx(ssinT, 32 * t + 16, [[0, 32], [1, 16]])
                            P.op("dv", (lambda kr=kr, cos_ap=cos_ap: lambda h: h.tensor_tensor(
                                krop[:, :].rearrange("p (a b) -> p a b", b=32),
                                kr[:, :].rearrange("p (a b) -> p a b", b=32), cos_ap, ALU.mult))(),
                                after=[f"{which}:ev:{t}:0", f"{which}:ev:{t}:1", "cosfix"])
                            P.op("dv", (lambda kr=kr, sin_lo=sin_lo: lambda h: h.tensor_tensor(
                                apx(krop2, 0, [[32, 32], [1, 16]]),
                                apx(kr, 16, [[32, 32], [1, 16]]), sin_lo, ALU.mult))(),
                                after=["sinfix"])
                            P.op("dv", (lambda kr=kr, sin_hi=sin_hi: lambda h: h.tensor_tensor(
                                apx(krop2, 16, [[32, 32], [1, 16]]),
                                apx(kr, 0, [[32, 32], [1, 16]]), sin_hi, ALU.mult))())
                            P.op("dv", lambda h: h.tensor_add(krop[:, :], krop[:, :], krop2[:, :]))
                            P.op("dv", lambda h: h.tensor_scalar_min(krop2[:, :], krop[:, :], 0.0),
                                 key=f"{which}mn:{t}")
                            P.op("ac", lambda h: h.activation(krop2[:, :], krop2[:, :], AF.Exp),
                                 after=[f"{which}mn:{t}"], key=f"{which}ex:{t}")
                            P.op("dv", lambda h: h.tensor_scalar_max(krop[:, :], krop[:, :], 0.0))
                            kb = kbf[t % 2] if which == "k" else qtm1
                            bufw = []
                            if which == "k" and t >= 2:
                                bufw = [f"sp:kt:{t - 2}"]
                            if which == "q" and t >= 1:
                                bufw = [f"qtr:{t - 1}:7"]
                            P.op("dv", (lambda kb=kb: lambda h: h.tensor_add(
                                kb[:, :], krop[:, :], krop2[:, :]))(),
                                after=[f"{which}ex:{t}"] + bufw, key=f"{which}ph:{t}")
                            if which == "k":
                                P.op("gp", (lambda t=t, kb=kb: lambda h: h.dma_start(
                                    ktmsp[:, DM * t: DM * (t + 1)], kb[:, :]))(),
                                    after=[f"kph:{t}"], key=f"sp:kt:{t}", dma=True)
                        qb, qf = qtm1, qfm1
                        for b in range(8):
                            pt = ptb1[b % 2]
                            aft = [f"qph:{t}"]
                            if t * 8 + b >= 2:
                                pb, pbb = divmod(t * 8 + b - 2, 8)
                                aft.append(f"qte:{pb}:{pbb}")
                            if t >= 1 and b == 0:
                                aft.append(f"sp:q:{t - 1}")
                            P.op("te", (lambda pt=pt, qb=qb, b=b: lambda h: h.transpose(
                                pt[:, :], qb[:, 128 * b: 128 * (b + 1)], sidb[:, :]))(),
                                after=aft, key=f"qtr:{t}:{b}")
                            P.op("ac", (lambda pt=pt, qf=qf, b=b: lambda h: h.copy(
                                qf[:, 128 * b: 128 * (b + 1)], pt[:, :]))(),
                                after=[f"qtr:{t}:{b}"], key=f"qte:{t}:{b}")
                        P.op("gp", (lambda t=t, qf=qf: lambda h: h.dma_start(
                            bass.AP(phiqsp, 128 * t, [[8 * T, 128], [T, 8], [1, 128]]),
                            qf[:, :].rearrange("p (b j) -> p b j", j=128)))(),
                            after=[f"qte:{t}:{b}" for b in range(8)], key=f"sp:q:{t}", dma=True)
                    P.fence("p1")

            with ExitStack() as p2:
                cb1x = p2.enter_context(nc.sbuf_tensor("cb1x", [128, DM], F32))
                sbvx = p2.enter_context(nc.sbuf_tensor("sbvx", [128, DM], F32))
                lgax = p2.enter_context(nc.sbuf_tensor("lgax", [128, DM], F32))
                lbax = p2.enter_context(nc.sbuf_tensor("lbax", [128, DM], F32))
                rmswx = p2.enter_context(nc.sbuf_tensor("rmswx", [128, DM], F32))
                boutx = p2.enter_context(nc.sbuf_tensor("boutx", [128, DM], F32))
                phq_st = [p2.enter_context(nc.sbuf_tensor(f"phq{i}", [128, DM], BF16))
                          for i in range(2)]
                ktm_st = [p2.enter_context(nc.sbuf_tensor(f"ktm{i}", [128, DM], BF16))
                          for i in range(2)]
                va2_st = [p2.enter_context(nc.sbuf_tensor(f"va2{i}", [128, VA], BF16))
                          for i in range(2)]
                h12_st = [p2.enter_context(nc.sbuf_tensor(f"h12{i}", [128, DM], F32))
                          for i in range(2)]
                x2_st = [p2.enter_context(nc.sbuf_tensor(f"x2{i}", [128, DM], F32))
                         for i in range(2)]
                sPTall = p2.enter_context(nc.sbuf_tensor("sPTall", [128, NCH * 64], F32))
                sIvTall = p2.enter_context(nc.sbuf_tensor("sIvTall", [128, NCH * 64], BF16))
                sPCtall = p2.enter_context(nc.sbuf_tensor("sPCtall", [128, NCH * 16], F32))
                ktil = [p2.enter_context(nc.sbuf_tensor(f"ktil{i}", [128, DM], BF16))
                        for i in range(2)]
                ktilT = [p2.enter_context(nc.sbuf_tensor(f"ktilT{i}", [128, DM], BF16))
                         for i in range(2)]
                sAt = [p2.enter_context(nc.sbuf_tensor(f"sAt{i}", [128, 512], BF16))
                       for i in range(2)]
                oa = p2.enter_context(nc.sbuf_tensor("oa", [128, 16 * 132], F32))
                sden = p2.enter_context(nc.sbuf_tensor("sden", [128, 64], F32))
                stA = p2.enter_context(nc.sbuf_tensor("stA", [128, 64], F32))
                stB = p2.enter_context(nc.sbuf_tensor("stB", [128, 64], F32))
                srd = p2.enter_context(nc.sbuf_tensor("srd", [128, 64], F32))
                sytmp = p2.enter_context(nc.sbuf_tensor("sytmp", [128, 128], F32))
                sytmp2 = p2.enter_context(nc.sbuf_tensor("sytmp2", [128, DM], F32))
                ych = p2.enter_context(nc.sbuf_tensor("ych", [128, DM], F32))
                yTb = p2.enter_context(nc.sbuf_tensor("yTb", [128, DM], BF16))
                zst = p2.enter_context(nc.sbuf_tensor("zst", [128, DM], F32))
                h2st = p2.enter_context(nc.sbuf_tensor("h2st", [128, DM], F32))
                xnst = p2.enter_context(nc.sbuf_tensor("xnst", [128, DM], BF16))
                szstat = p2.enter_context(nc.sbuf_tensor("szstat", [128, 16], F32))
                szv = p2.enter_context(nc.sbuf_tensor("szv", [128, 8], F32))
                sel8 = p2.enter_context(nc.sbuf_tensor("sel8", [128, 8], F32))
                slg = p2.enter_context(nc.sbuf_tensor("slg", [128, 8], F32))
                stmp8 = p2.enter_context(nc.sbuf_tensor("stmp8", [128, 4], F32))
                sZ = p2.enter_context(nc.sbuf_tensor("sZ", [128, 16 * 132], F32))
                sZb = p2.enter_context(nc.sbuf_tensor("sZb", [128, 16 * 132], BF16))
                mu = [p2.enter_context(nc.psum_tensor(f"mu{i}", [128, 512], F32))
                      for i in range(4)]
                pb = [p2.enter_context(nc.psum_tensor(f"pb{i}", [128, 272], F32))
                      for i in range(2)]
                ptb2 = [p2.enter_context(nc.psum_tensor(f"p2tb{i}", [128, 128], BF16))
                        for i in range(2)]

                def cvread2(nm, dsttile, vi, deps=("sp:mod1",)):
                    P.op("sy", (lambda dsttile=dsttile, vi=vi: lambda h: h.dma_start(
                        dsttile[:, :],
                        bass.AP(modsp, vi * DM, [[0, 128], [1, DM]])))(),
                        after=deps, key=f"bc:{nm}", dma=True)
                cvread2("cb1", cb1x, 2)
                cvread2("sb", sbvx, 3)
                cvread2("lga", lgax, 4)
                cvread2("lba", lbax, 5)
                P.op("sy", lambda h: h.dma_start(
                    rmswx[:, :], bass.AP(rms_w, 0, [[0, 128], [1, DM]])),
                    key="ld:rmw", dma=True)
                P.op("sy", lambda h: h.dma_start(
                    boutx[:, :], bass.AP(boutv, 0, [[0, 128], [1, DM]])),
                    key="ld:bov", dma=True)
                P.op("dv", lambda h: h.memset(sZ[:, :], 0.0), key="z:init")
                P.op("dv", lambda h: h.memset(sZb[:, :], 0.0), key="z:init:b")

                with ExitStack() as dp:
                    cumA = dp.enter_context(nc.sbuf_tensor("cumA", [64, T], F32))
                    cumB = dp.enter_context(nc.sbuf_tensor("cumB", [64, T], F32))
                    sPall = dp.enter_context(nc.sbuf_tensor("sPall", [128, T], F32))
                    sPC64a = dp.enter_context(nc.sbuf_tensor("sPC64a", [64, 16], F32))
                    sRvall = dp.enter_context(nc.sbuf_tensor("sRvall", [64, 256], F32))
                    for si, d in enumerate([1, 2, 4, 8, 16, 32, 64]):
                        if si == 0:
                            src = logb
                        else:
                            src = cumB if si % 2 == 1 else cumA
                        dst = cumB if si % 2 == 0 else cumA
                        aft = [f"bt:ln:{n}" for n in range(4)] if si == 0 else []
                        P.op("dv", (lambda src=src, dst=dst, d=d: lambda h: h.tensor_copy(
                            apx(dst, 0, [[128, NCH], [1, d]], 0, 64),
                            apx(src, 0, [[128, NCH], [1, d]], 0, 64)))(), after=aft)
                        P.op("dv", (lambda src=src, dst=dst, d=d: lambda h: h.tensor_tensor(
                            apx(dst, d, [[128, NCH], [1, 128 - d]], 0, 64),
                            apx(src, d, [[128, NCH], [1, 128 - d]], 0, 64),
                            apx(src, 0, [[128, NCH], [1, 128 - d]], 0, 64), ALU.add))(),
                            key=("dp:cum" if si == 6 else None))
                    P.op("dv", lambda h: h.memset(sPall[64:128, :], 0.0), key="dp:pz")
                    P.op("ac", lambda h: h.activation(sPall[0:64, :], cumB[0:64, :],
                                                      AF.Exp),
                         after=["dp:cum", "dp:pz"], key="dp:pexp")
                    P.op("ac", lambda h: h.activation(
                        sPC64a[0:64, :], apx(cumB, 127, [[128, NCH]], 0, 64), AF.Exp),
                        after=["dp:cum"], key="dp:pc64")
                    P.op("dv", lambda h: h.tensor_tensor(
                        apx(sRvall, 0, [[16, NCH], [1, 16]], 0, 64),
                        apx(sRsel, 0, [[0, NCH], [1, 16]], 0, 64),
                        apx(sPC64a, 0, [[1, NCH], [0, 16]], 0, 64), ALU.mult),
                        after=["dp:pc64", "ld:rsl"], key="dp:rv")
                    P.op("te", lambda h: h.matmul(pb[0][:, 0:256], sLsel[0:64, :],
                                                  sRvall[0:64, :], start=True, stop=True),
                         after=["dp:rv", "ld:lsl"], key="dp:pcmm")
                    P.op("ac", lambda h: h.copy(sPCtall[:, :], pb[0][:, 0:256]),
                         after=["dp:pcmm"], key="dp:pctev")
                    for k in range(4):
                        for j in range(4):
                            c = 4 * k + j
                            P.op("te", (lambda k=k, j=j, c=c: lambda h: h.transpose(
                                mu[k][:, 128 * j: 128 * (j + 1)],
                                sPall[:, 128 * c: 128 * (c + 1)], sidf[:, :]))(),
                                after=["dp:pexp"], key=f"dp:ptr:{c}")
                    for k in range(4):
                        P.op("ac", (lambda k=k: lambda h: h.copy(
                            sPTall[:, 256 * k: 256 * (k + 1)],
                            apx(mu[k], 0, [[128, 4], [1, 64]])))(),
                            after=[f"dp:ptr:{4 * k + 3}"], key=f"dp:ptev:{k}")
                    P.op("ac", lambda h: h.activation(sPall[0:64, :], cumB[0:64, :],
                                                      AF.Exp, scale=-1.0),
                         after=["dp:ptr:15"], key="dp:ivexp")
                    for k in range(4):
                        for j in range(4):
                            c = 4 * k + j
                            P.op("te", (lambda k=k, j=j, c=c: lambda h: h.transpose(
                                mu[k][:, 128 * j: 128 * (j + 1)],
                                sPall[:, 128 * c: 128 * (c + 1)], sidf[:, :]))(),
                                after=["dp:ivexp", f"dp:ptev:{k}"], key=f"dp:ivtr:{c}")
                    for k in range(4):
                        P.op("ac", (lambda k=k: lambda h: h.copy(
                            sIvTall[:, 256 * k: 256 * (k + 1)],
                            apx(mu[k], 0, [[128, 4], [1, 64]])))(),
                            after=[f"dp:ivtr:{4 * k + 3}"], key=f"dp:ivtev:{k}")

                for c in range(NCH):
                    dep_done = [f"ch:{c - 2}:done"] if c >= 2 else []
                    P.op("sy", (lambda c=c: lambda h: h.dma_start(
                        phq_st[c % 2][:, :].rearrange("p (b j) -> p b j", j=128),
                        bass.AP(phiqsp, 128 * c, [[8 * T, 128], [T, 8], [1, 128]])))(),
                        after=dep_done + [f"sp:q:{c}"], key=f"l2:phq:{c}", dma=True,
                        stream=f"lphq{c % 2}")
                    P.op("sy", (lambda c=c: lambda h: h.dma_start(
                        ktm_st[c % 2][:, :], ktmsp[:, DM * c: DM * (c + 1)]))(),
                        after=dep_done + [f"sp:kt:{c}"], key=f"l2:ktm:{c}", dma=True,
                        stream=f"lktm{c % 2}")
                    P.op("sy", (lambda c=c: lambda h: h.dma_start(
                        va2_st[c % 2][:, :], vaugsp[:, VA * c: VA * (c + 1)]))(),
                        after=dep_done + [f"sp:va:{c}"], key=f"l2:va:{c}", dma=True,
                        stream=f"lva{c % 2}")
                    P.op("sy", (lambda c=c: lambda h: h.dma_start(
                        h12_st[c % 2][:, :], h1sp[:, DM * c: DM * (c + 1)]))(),
                        after=dep_done + [f"sp:h1:{c}"], key=f"l2:h1:{c}", dma=True,
                        stream=f"lh1{c % 2}")
                    P.op("sy", (lambda c=c: lambda h: h.dma_start(
                        x2_st[c % 2][:, :], x_hbm[128 * c: 128 * (c + 1), :]))(),
                        after=dep_done, key=f"l2:x:{c}", dma=True, stream=f"lx{c % 2}")

                    for m in range(2):
                        P.op("dv", (lambda c=c, m=m: lambda h: h.tensor_tensor(
                            ktil[m][:, :].rearrange("p (a b) -> p a b", b=32),
                            ktm_st[c % 2][:, :].rearrange("p (a b) -> p a b", b=32),
                            apx(sIvTall, 64 * c + m, [[2, 32], [0, 32]]), ALU.mult))(),
                            after=[f"dp:ivtev:{c // 4}", f"l2:ktm:{c}"] +
                                  ([f"u:mm:{c - 1}:{14 + m}"] if c else []),
                            key=f"ktl:{c}:{m}")
                        for b in range(8):
                            i = m * 8 + b
                            pt = ptb2[i % 2]
                            aft = [f"ktl:{c}:{m}"]
                            if c > 0 and i < 2:
                                aft.append(f"xnte:{c - 1}:{6 + i}")
                            if i >= 2:
                                aft.append(f"kte:{c}:{(i - 2) // 8}:{(i - 2) % 8}")
                            P.op("te", (lambda pt=pt, m=m, b=b: lambda h: h.transpose(
                                pt[:, :], ktil[m][:, 128 * b: 128 * (b + 1)], sidb[:, :]))(),
                                after=aft, key=f"ktr:{c}:{m}:{b}")
                            P.op("ac", (lambda pt=pt, m=m, b=b: lambda h: h.copy(
                                ktilT[m][:, 128 * b: 128 * (b + 1)], pt[:, :]))(),
                                after=[f"ktr:{c}:{m}:{b}"], key=f"kte:{c}:{m}:{b}")

                    for g in range(16):
                        H4, m = g // 2, g % 2
                        sl = g % 2
                        aft = [f"kte:{c}:{m}:{H4}", f"l2:phq:{c}"]
                        if g >= 1:
                            aft.append(f"sc:ev:{c}:{g - 1}")
                        elif c > 0:
                            aft.extend([f"lgcp:{c - 1}", f"zdv:{c - 1}:1"])
                        def scmm(c=c, g=g, H4=H4, m=m):
                            def f(h):
                                r = None
                                for hh in range(4):
                                    r = h.matmul(
                                        mu[hh][:, 0:128],
                                        ktilT[m][32 * hh: 32 * (hh + 1), 128 * H4: 128 * (H4 + 1)],
                                        phq_st[c % 2][32 * hh: 32 * (hh + 1), 128 * H4: 128 * (H4 + 1)],
                                        start=True, stop=True, tile_position=(32 * hh, 0))
                                return r
                            return f
                        P.op("te", scmm(), after=aft, key=f"sc:mm:{c}:{g}")
                        def scev(sl=sl):
                            def f(h):
                                r = None
                                for hh in range(4):
                                    r = h.copy(sAt[sl][:, 128 * hh: 128 * (hh + 1)],
                                               mu[hh][:, 0:128])
                                return r
                            return f
                        P.op("ac", scev(), after=[f"sc:mm:{c}:{g}"] +
                             ([f"oa:mm:{c}:{g - 2}"] if g >= 2 else
                              ([f"oa:mm:{c - 1}:{14 + g}"] if c else [])),
                             key=f"sc:ev:{c}:{g}")
                        P.op("dv", (lambda sl=sl: lambda h: h.tensor_tensor(
                            sAt[sl][:, :].rearrange("p (a b) -> p a b", b=128),
                            sAt[sl][:, :].rearrange("p (a b) -> p a b", b=128),
                            apx(smask, 0, [[0, 4], [1, 128]]), ALU.mult))(),
                            after=[f"sc:ev:{c}:{g}", "ld:msk"], key=f"sc:mk:{c}:{g}")
                        aft = [f"sc:mk:{c}:{g}", f"l2:va:{c}"]
                        aft.append(f"zbf:{c - 1}:{g}" if c else "z:init:b")
                        if g >= 2:
                            aft.append(f"oa:ev:{c}:{g - 2}")
                            aft.append(f"zadd:{c}:{g - 2}")
                        elif c:
                            aft.append(f"oa:ev:{c - 1}:{14 + g}")
                        elif g == 0:
                            aft.append("dp:pctev")
                        def oamm(c=c, g=g, H4=H4, m=m, sl=sl):
                            def f(h):
                                r = None
                                for hh in range(4):
                                    h.matmul(
                                        pb[sl][:, 33 * hh: 33 * (hh + 1)],
                                        sAt[sl][:, 128 * hh: 128 * (hh + 1)],
                                        va2_st[c % 2][:, 33 * (4 * H4 + hh): 33 * (4 * H4 + hh) + 33],
                                        start=True, stop=False)
                                    r = h.matmul(
                                        pb[sl][:, 33 * hh: 33 * (hh + 1)],
                                        phq_st[c % 2][32 * hh: 32 * (hh + 1), 128 * H4: 128 * (H4 + 1)],
                                        sZb[32 * hh: 32 * (hh + 1), 132 * g + 33 * hh: 132 * g + 33 * (hh + 1)],
                                        start=False, stop=True, tile_position=(32 * hh, 0))
                                return r
                            return f
                        P.op("te", oamm(), after=aft, key=f"oa:mm:{c}:{g}")
                        aft = [f"ktl:{c}:{m}", f"l2:va:{c}"]
                        if g >= 2:
                            aft.append(f"zadd:{c}:{g - 2}")
                        elif c:
                            aft.append(f"zadd:{c - 1}:{14 + g}")
                        elif g == 0:
                            aft.append("dp:pctev")
                        P.op("te", (lambda c=c, g=g, H4=H4, m=m, sl=sl: lambda h: h.matmul(
                            pb[sl][:, 132:264], ktil[m][:, 128 * H4: 128 * (H4 + 1)],
                            va2_st[c % 2][:, 33 * 4 * H4: 33 * 4 * H4 + 132],
                            start=True, stop=True))(), after=aft, key=f"u:mm:{c}:{g}")
                        P.op("ac", (lambda c=c, g=g, sl=sl: lambda h: h.copy(
                            oa[:, 132 * g: 132 * (g + 1)], pb[sl][:, 0:132]))(),
                            after=[f"oa:mm:{c}:{g}", f"u:mm:{c}:{g}"] +
                                  ([f"y:{c - 1}"] if (c and g == 0) else []),
                            key=f"oa:ev:{c}:{g}")
                        P.op("dv", (lambda g=g, sl=sl: lambda h: h.tensor_tensor(
                            sZ[:, 132 * g: 132 * (g + 1)], sZ[:, 132 * g: 132 * (g + 1)],
                            pb[sl][:, 132:264], ALU.add))(),
                            after=[f"u:mm:{c}:{g}", "dp:pctev", f"oa:ev:{c}:{g}"] +
                                  (["z:init"] if c == 0 else []))
                        P.op("dv", (lambda c=c, g=g: lambda h: h.tensor_scalar(
                            sZ[:, 132 * g: 132 * (g + 1)], sZ[:, 132 * g: 132 * (g + 1)],
                            sPCtall[:, 16 * c + g: 16 * c + g + 1], None, ALU.mult))(),
                            key=f"zadd:{c}:{g}")
                        P.op("ac", (lambda g=g: lambda h: h.copy(
                            sZb[:, 132 * g: 132 * (g + 1)], sZ[:, 132 * g: 132 * (g + 1)]))(),
                            after=[f"zadd:{c}:{g}"], key=f"zbf:{c}:{g}")

                    PTsel = apx(sPTall, 64 * c, [[8, 8], [1, 2], [2, 4]])
                    mixsel = apx(mixa, 64 * c, [[8, 8], [1, 2], [2, 4]])
                    P.op("dv", lambda h: h.tensor_copy(
                        sden[:, :].rearrange("p (a b) -> p a b", b=4),
                        apx(oa, 32, [[132, 16], [33, 4]])),
                        after=[f"oa:ev:{c}:15", f"dp:ptev:{c // 4}"])
                    P.op("dv", (lambda PTsel=PTsel: lambda h: h.tensor_tensor(
                        stA[:, :].rearrange("p (a b c) -> p a b c", b=2, c=4),
                        sden[:, :].rearrange("p (a b c) -> p a b c", b=2, c=4),
                        PTsel, ALU.mult))())
                    P.op("dv", lambda h: h.tensor_scalar_add(stA[:, :], stA[:, :], ATT_EPS))
                    P.op("dv", lambda h: h.reciprocal(stA[:, :], stA[:, :]))
                    P.op("dv", (lambda PTsel=PTsel, mixsel=mixsel: lambda h: h.tensor_tensor(
                        stB[:, :].rearrange("p (a b c) -> p a b c", b=2, c=4),
                        PTsel, mixsel, ALU.mult))())
                    P.op("dv", lambda h: h.tensor_mul(srd[:, :], stA[:, :], stB[:, :]),
                         key=f"rd:{c}")

                    P.op("dv", (lambda c=c: lambda h: h.tensor_tensor(
                        ych[:, :].rearrange("p (a b c) -> p a b c", b=4, c=32),
                        apx(oa, 0, [[264, 8], [33, 4], [1, 32]]),
                        apx(srd, 0, [[8, 8], [1, 4], [0, 32]]), ALU.mult))(),
                        after=([f"ytr:{c - 1}:7"] if c else []))
                    P.op("dv", lambda h: h.tensor_tensor(
                        sytmp2[:, :].rearrange("p (a b c) -> p a b c", b=4, c=32),
                        apx(oa, 132, [[264, 8], [33, 4], [1, 32]]),
                        apx(srd, 4, [[8, 8], [1, 4], [0, 32]]), ALU.mult))
                    P.op("dv", (lambda c=c: lambda h: h.tensor_add(
                        ych[:, :], ych[:, :], sytmp2[:, :]))(), key=f"y:{c}")

                    for b in range(8):
                        pt = mu[2 + b % 2]
                        aft = [f"y:{c}", f"sc:ev:{c}:15"]
                        if b >= 2:
                            aft.append(f"yte:{c}:{b - 2}")
                        P.op("te", (lambda pt=pt, b=b: lambda h: h.transpose(
                            pt[:, 0:128], ych[:, 128 * b: 128 * (b + 1)], sidf[:, :]))(),
                            after=aft, key=f"ytr:{c}:{b}")
                        P.op("ac", (lambda pt=pt, b=b: lambda h: h.copy(
                            yTb[:, 128 * b: 128 * (b + 1)], pt[:, 0:128]))(),
                            after=[f"ytr:{c}:{b}"], key=f"yte:{c}:{b}")
                    for n in range(2):
                        aft = [f"yte:{c}:7", f"sc:ev:{c}:15"]
                        P.op("te", (lambda n=n: lambda h: _seq_mm(
                            h, mu[n][:, 0:512],
                            [yTb[:, 128 * kb: 128 * (kb + 1)] for kb in range(8)],
                            [WoT[:, DM * kb + 512 * n: DM * kb + 512 * (n + 1)] for kb in range(8)]))(),
                            after=aft, key=f"z:mm:{c}:{n}")
                        P.op("dv", (lambda c=c, n=n: lambda h: h.tensor_tensor(
                            zst[:, 512 * n: 512 * (n + 1)], mu[n][:, 0:512],
                            h12_st[c % 2][:, 512 * n: 512 * (n + 1)], ALU.add))(),
                            after=[f"z:mm:{c}:{n}", f"l2:h1:{c}"])
                        P.op("dv", (lambda n=n: lambda h: h.tensor_add(
                            zst[:, 512 * n: 512 * (n + 1)], zst[:, 512 * n: 512 * (n + 1)],
                            boutx[:, 512 * n: 512 * (n + 1)]))(),
                            after=["ld:bov"], key=f"zdv:{c}:{n}")
                    P.op("dv", lambda h: h.bn_stats(szstat[:, 0:6], zst[:, 0:512]))
                    P.op("dv", lambda h: h.bn_stats(szstat[:, 6:12], zst[:, 512:1024]))
                    P.op("dv", lambda h: h.bn_aggr(szv[:, 0:2], szstat[:, 0:12]),
                         key=f"zs:{c}")
                    P.op("ac", lambda h: h.activation(szv[:, 2:3], szv[:, 1:2], AF.Ln,
                                                      bias=LN_EPS), after=[f"zs:{c}"])
                    P.op("ac", lambda h: h.activation(szv[:, 3:4], szv[:, 2:3], AF.Exp,
                                                      scale=-0.5), key=f"zrs:{c}")
                    P.op("dv", lambda h: h.tensor_scalar(
                        h2st[:, :], zst[:, :], szv[:, 0:1], szv[:, 3:4],
                        ALU.subtract, ALU.mult), after=[f"zrs:{c}"])
                    P.op("dv", lambda h: h.tensor_mul(h2st[:, :], h2st[:, :], lgax[:, :]),
                         after=["bc:lga"])
                    P.op("dv", lambda h: h.tensor_add(h2st[:, :], h2st[:, :], lbax[:, :]),
                         after=["bc:lba"])
                    P.op("dv", (lambda c=c: lambda h: h.tensor_add(
                        x2_st[c % 2][:, :], x2_st[c % 2][:, :], h2st[:, :]))(),
                        after=[f"l2:x:{c}"], key=f"xn:{c}")
                    P.op("gp", (lambda c=c: lambda h: h.dma_start(
                        xsp[:, DM * c: DM * (c + 1)], x2_st[c % 2][:, :]))(),
                        after=[f"xn:{c}"], key=f"sp:x:{c}", dma=True)
                    P.op("dv", (lambda c=c: lambda h: h.bn_stats(
                        szstat[:, 0:6], x2_st[c % 2][:, 0:512]))(), after=[f"xn:{c}"])
                    P.op("dv", (lambda c=c: lambda h: h.bn_stats(
                        szstat[:, 6:12], x2_st[c % 2][:, 512:1024]))())
                    P.op("dv", lambda h: h.bn_aggr(szv[:, 0:2], szstat[:, 0:12]),
                         key=f"xs:{c}")
                    P.op("ac", lambda h: h.activation(szv[:, 2:3], szv[:, 1:2], AF.Ln,
                                                      bias=LN_EPS), after=[f"xs:{c}"])
                    P.op("ac", lambda h: h.activation(szv[:, 3:4], szv[:, 2:3], AF.Exp,
                                                      scale=-0.5), key=f"xrs:{c}")
                    P.op("dv", (lambda c=c: lambda h: h.tensor_scalar(
                        h2st[:, :], x2_st[c % 2][:, :], szv[:, 0:1], szv[:, 3:4],
                        ALU.subtract, ALU.mult))(), after=[f"xrs:{c}"])
                    P.op("dv", lambda h: h.tensor_mul(h2st[:, :], h2st[:, :], cb1x[:, :]),
                         after=["bc:cb1"])
                    P.op("dv", lambda h: h.tensor_add(h2st[:, :], h2st[:, :], sbvx[:, :]),
                         after=["bc:sb"], key=f"h2:{c}")
                    P.op("dv", lambda h: h.bn_stats(szstat[:, 0:6], h2st[:, 0:512]))
                    P.op("dv", lambda h: h.bn_stats(szstat[:, 6:12], h2st[:, 512:1024]))
                    P.op("dv", lambda h: h.bn_aggr(szv[:, 4:6], szstat[:, 0:12]),
                         key=f"h2s:{c}")
                    P.op("ac", lambda h: h.activation(szv[:, 6:7], szv[:, 4:5], AF.Square),
                         after=[f"h2s:{c}"], key=f"h2sq:{c}")
                    P.op("dv", lambda h: h.tensor_add(szv[:, 6:7], szv[:, 6:7], szv[:, 5:6]),
                         after=[f"h2sq:{c}"], key=f"ms:{c}")
                    P.op("ac", lambda h: h.activation(szv[:, 6:7], szv[:, 6:7], AF.Ln,
                                                      bias=RMS_EPS), after=[f"ms:{c}"])
                    P.op("ac", lambda h: h.activation(szv[:, 7:8], szv[:, 6:7], AF.Exp,
                                                      scale=-0.5), key=f"h2rs:{c}")
                    P.op("dv", lambda h: h.tensor_scalar(
                        h2st[:, :], h2st[:, :], szv[:, 7:8], None, ALU.mult),
                        after=[f"h2rs:{c}"])
                    P.op("dv", lambda h: h.tensor_mul(xnst[:, :], h2st[:, :], rmswx[:, :]),
                         after=["ld:rmw"] + ([f"xntr:{c - 1}:7"] if c else []),
                         key=f"xnst:{c}")
                    for b in range(8):
                        pt = ptb2[b % 2]
                        aft = [f"xnst:{c}", f"kte:{c}:1:{6 + (b % 2)}"]
                        if b >= 2:
                            aft.append(f"xnte:{c}:{b - 2}")
                        P.op("te", (lambda pt=pt, b=b: lambda h: h.transpose(
                            pt[:, :], xnst[:, 128 * b: 128 * (b + 1)], sidb[:, :]))(),
                            after=aft, key=f"xntr:{c}:{b}")
                        P.op("ac", (lambda pt=pt, c=c, b=b: lambda h: h.copy(
                            xnT[:, T * b + 128 * c: T * b + 128 * (c + 1)], pt[:, :]))(),
                            after=[f"xntr:{c}:{b}"], key=f"xnte:{c}:{b}")
                    P.op("te", (lambda c=c: lambda h: _seq_mm(
                        h, mu[0][:, 0:8],
                        [xnT[:, T * kb + 128 * c: T * kb + 128 * (c + 1)] for kb in range(8)],
                        [sgateT[:, 8 * kb: 8 * (kb + 1)] for kb in range(8)]))(),
                        after=[f"xnte:{c}:7", f"zdv:{c}:0"], key=f"lg:mm:{c}")
                    P.op("ac", (lambda c=c: lambda h: h.copy(slg[:, :], mu[0][:, 0:8]))(),
                         after=[f"lg:mm:{c}"], key=f"lgcp:{c}")
                    P.op("dv", lambda h: h.reduce_max(szv[:, 0:1], slg[:, :], AX.X,
                                                      negate=True), after=[f"lgcp:{c}"])
                    P.op("dv", lambda h: h.reduce_max(szv[:, 1:2], slg[:, :], AX.X),
                         key=f"lgmx:{c}")
                    P.op("ac", lambda h: h.activation(sel8[:, :], slg[:, :], AF.Exp,
                                                      bias=szv[:, 0:1]),
                         after=[f"lgmx:{c}"], key=f"lgel:{c}")
                    P.op("dv", lambda h: h.reduce_sum(szv[:, 3:4], sel8[:, :], AX.X),
                         after=[f"lgel:{c}"])
                    P.op("dv", lambda h: h.reciprocal(szv[:, 4:5], szv[:, 3:4]),
                         key=f"lgsc:{c}")
                    P.op("dv", (lambda c=c: lambda h: h.tensor_scalar(
                        sel8[:, :], slg[:, :], szv[:, 1:2], None, ALU.is_ge))(),
                        after=[f"lgel:{c}", f"lgsc:{c}"], key=f"lgr:{c}")
                    P.op("dv", (lambda c=c: lambda h: h.tensor_scalar(
                        msc[:, 8 * c: 8 * (c + 1)], sel8[:, :], szv[:, 4:5], None,
                        ALU.mult))(), key=f"msc:{c}")
                    P.op("dv", lambda h: h.drain(),
                         after=[f"msc:{c}", f"u:mm:{c}:15", f"zbf:{c}:15", f"sp:x:{c}"],
                         key=f"ch:{c}:done")
                if DEBUG:
                    P.op("gp", lambda h: h.dma_start(d_mix[:, :], mixa[:, :]),
                         after=["ch:15:done"], key="dbg:mix", dma=True)
                    P.op("gp", lambda h: h.dma_start(d_msc[:, :], msc[:, :]),
                         after=["ch:15:done"], key="dbg:msc", dma=True)
                    P.op("gp", lambda h: h.dma_start(d_xnt[:, :], xnT[:, :]),
                         after=["ch:15:done"], key="dbg:xnt", dma=True)
                P.fence("p2")

        with ExitStack() as p3:
            W1bf = p3.enter_context(nc.sbuf_tensor("W1bf", [128, 8 * FF], BF16))
            W2bf = p3.enter_context(nc.sbuf_tensor("W2bf", [128, 32 * DM], BF16))
            hTb = p3.enter_context(nc.sbuf_tensor("hTb", [128, 32 * 512], BF16))
            b1c = p3.enter_context(nc.sbuf_tensor("b1c", [128, 32], F32))
            b1row = p3.enter_context(nc.sbuf_tensor("b1row", [128, 128], F32))
            b2bf = p3.enter_context(nc.sbuf_tensor("b2bf", [1, DM], BF16))
            ctb = [p3.enter_context(nc.sbuf_tensor(f"ctb{i}", [128, 512], F32))
                   for i in range(2)]
            pm1 = [p3.enter_context(nc.psum_tensor(f"pm1{i}", [128, 512], F32))
                   for i in range(2)]
            pm2 = [p3.enter_context(nc.psum_tensor(f"pm2{i}", [128, 512], F32))
                   for i in range(2)]

            uac_keys = []
            P.op("dv", lambda h: h.memset(b1row[:, :], 0.0), key="b1r:z")
            for e in range(E):
                aft1 = [f"m1:{e - 1}:3:31"] if e else []
                P.op("gp", (lambda e=e: lambda h: h.dma_start(
                    W1bf[:, :].rearrange("p (r f) -> p r f", f=FF),
                    bass.AP(W1, e * DM * FF, [[FF, 128], [128 * FF, 8], [1, FF]])))(),
                    after=aft1, key=f"w1ld:{e}", dma=True)
                aft2 = [f"m2:{e - 1}:3:3:1"] if e else []
                P.op("gp", (lambda e=e: lambda h: h.dma_start(
                    W2bf[:, :].rearrange("p (r d) -> p r d", d=DM),
                    bass.AP(W2, e * FF * DM, [[DM, 128], [128 * DM, 32], [1, DM]])))(),
                    after=aft2, key=f"w2ld:{e}", dma=True)
                P.op("sy", (lambda e=e: lambda h: h.dma_start(
                    b1row[0:32, :], bass.AP(b1, e * FF, [[128, 32], [1, 128]])))(),
                    after=([f"b1tr:{e - 1}"] if e else ["b1r:z"]), key=f"b1ld:{e}", dma=True)
                P.op("te", (lambda e=e: lambda h: h.transpose(
                    pm1[0][:, 0:128], b1row[:, :], sidf[:, :]))(),
                    after=[f"b1ld:{e}"] + ([f"m1ev:{e - 1}:3:31"] if e else []),
                    key=f"b1tr:{e}")
                P.op("ac", (lambda e=e: lambda h: h.copy(b1c[:, :], pm1[0][:, 0:32]))(),
                    after=[f"b1tr:{e}"], key=f"b1ev:{e}")
                P.op("gp", (lambda e=e: lambda h: h.dma_start(
                    b2bf[0:1, :], bass.AP(b2, e * DM, [[DM, 1], [1, DM]])))(),
                    after=aft2, key=f"b2ld:{e}", dma=True)
                for q in range(4):
                    for mf in range(32):
                        sl = mf % 2
                        aft = [f"w1ld:{e}"]
                        if q == 0 and mf == 0:
                            aft.append(f"b1ev:{e}")
                        if mf >= 2:
                            aft.append(f"m1ev:{e}:{q}:{mf - 2}")
                        elif q > 0:
                            aft.append(f"m1ev:{e}:{q - 1}:{30 + mf}")
                        elif e > 0:
                            aft.append(f"m1ev:{e - 1}:3:{30 + mf}")
                        P.op("te", (lambda e=e, q=q, mf=mf, sl=sl: lambda h: _seq_mm(
                            h, pm1[sl][:, 0:512],
                            [W1bf[:, FF * kc + 128 * mf: FF * kc + 128 * (mf + 1)]
                             for kc in range(8)],
                            [xnT[:, T * kc + 512 * q: T * kc + 512 * (q + 1)]
                             for kc in range(8)]))(),
                            after=aft, key=f"m1:{e}:{q}:{mf}")
                        P.op("ac", (lambda mf=mf, sl=sl: lambda h: h.activation(
                            hTb[:, 512 * mf: 512 * (mf + 1)], pm1[sl][:, 0:512],
                            AF.Silu, bias=b1c[:, mf:mf + 1]))(),
                            after=[f"m1:{e}:{q}:{mf}", f"b1ev:{e}"],
                            key=f"m1ev:{e}:{q}:{mf}")
                    for tt4 in range(4):
                        tau = 4 * q + tt4
                        for n in range(2):
                            idx = tt4 * 2 + n
                            sl = idx % 2
                            aft = [f"m1ev:{e}:{q}:31", f"w2ld:{e}", f"b2ld:{e}"]
                            if idx >= 2:
                                aft.append(f"m2ev:{e}:{q}:{tt4 - 1}:{n}")
                            elif q > 0:
                                aft.append(f"m2ev:{e}:{q - 1}:3:{n}")
                            elif e > 0:
                                aft.append(f"m2ev:{e - 1}:3:3:{n}")
                            def m2f(e=e, q=q, tt4=tt4, n=n, sl=sl):
                                def f(h):
                                    for kc in range(32):
                                        h.matmul(pm2[sl][:, 0:512],
                                                 hTb[:, 512 * kc + 128 * tt4: 512 * kc + 128 * (tt4 + 1)],
                                                 W2bf[:, DM * kc + 512 * n: DM * kc + 512 * (n + 1)],
                                                 start=(kc == 0), stop=False)
                                    return h.matmul(pm2[sl][:, 0:512], sones[0:1, 0:128],
                                                    b2bf[0:1, 512 * n: 512 * (n + 1)],
                                                    start=False, stop=True)
                                return f
                            P.op("te", m2f(), after=aft, key=f"m2:{e}:{q}:{tt4}:{n}")
                            aftv = [f"m2:{e}:{q}:{tt4}:{n}"]
                            if uac_keys[-2:] and len(uac_keys) >= 2:
                                aftv.append(uac_keys[-2])
                            P.op("ac", (lambda e=e, tau=tau, sl=sl: lambda h: h.activation(
                                ctb[sl][:, :], pm2[sl][:, 0:512], AF.Copy,
                                scale=msc[:, 8 * tau + e: 8 * tau + e + 1]))(),
                                after=aftv, key=f"m2ev:{e}:{q}:{tt4}:{n}")
                            uk = f"uac:{e}:{tau}:{n}"
                            P.op("gp", (lambda e=e, tau=tau, n=n, sl=sl: lambda h: h.dma_start(
                                uacc[:, DM * tau + 512 * n: DM * tau + 512 * (n + 1)],
                                ctb[sl][:, :],
                                accum_op=(ALU.bypass if e == 0 else ALU.add)))(),
                                after=[f"m2ev:{e}:{q}:{tt4}:{n}"], key=uk, dma=True)
                            uac_keys.append(uk)
            P.fence("p3")

        with ExitStack() as p4:
            gbx = p4.enter_context(nc.sbuf_tensor("gbx", [128, DM], F32))
            uf = [p4.enter_context(nc.sbuf_tensor(f"uf{i}", [128, DM], F32))
                  for i in range(2)]
            xf = [p4.enter_context(nc.sbuf_tensor(f"xf{i}", [128, DM], F32))
                  for i in range(2)]
            of = [p4.enter_context(nc.sbuf_tensor(f"of{i}", [128, DM], F32))
                  for i in range(2)]
            P.op("sy", lambda h: h.dma_start(
                gbx[:, :], bass.AP(modsp, 6 * DM, [[0, 128], [1, DM]])), key="bc:gb", dma=True)
            outk = []
            for t in range(NT):
                aft = [f"out:{t - 2}"] if t >= 2 else []
                P.op("sy", (lambda t=t: lambda h: h.dma_start(
                    uf[t % 2][:, :], uacc[:, DM * t: DM * (t + 1)]))(),
                    after=aft + [f"uac:7:{t}:1"], key=f"l4u:{t}", dma=True,
                    stream=f"l4u{t % 2}")
                P.op("sy", (lambda t=t: lambda h: h.dma_start(
                    xf[t % 2][:, :], xsp[:, DM * t: DM * (t + 1)]))(),
                    after=aft, key=f"l4x:{t}", dma=True, stream=f"l4x{t % 2}")
                P.op("dv", (lambda t=t: lambda h: h.tensor_mul(
                    of[t % 2][:, :], uf[t % 2][:, :], gbx[:, :]))(),
                    after=[f"l4u:{t}", "bc:gb"] + ([f"out:{t - 2}"] if t >= 2 else []))
                P.op("dv", (lambda t=t: lambda h: h.tensor_add(
                    of[t % 2][:, :], of[t % 2][:, :], xf[t % 2][:, :]))(),
                    after=[f"l4x:{t}"], key=f"o4:{t}")
                P.op("gp", (lambda t=t: lambda h: h.dma_start(
                    out[128 * t: 128 * (t + 1), :], of[t % 2][:, :]))(),
                    after=[f"o4:{t}"], key=f"out:{t}", dma=True)
                outk.append(f"out:{t}")
            P.op("gp", lambda h: h.drain(), after=outk, key="final")
        if TRUNC is not None:
            if TRUNC in P.keys:
                idx = P.keys[TRUNC]
            else:
                idx = max(P.keys[f"F:{TRUNC}:{e}"] for e in _ENG)
            P.ops = P.ops[:idx + 1]
    P.emit()
    return nc


def make_constants():
    half = 16
    try:
        import jax
        import jax.numpy as jnp
        with jax.default_device(jax.devices("cpu")[0]):
            invf = 1.0 / 10000.0 ** (jnp.arange(half, dtype=jnp.float32) / half)
            fr = jnp.arange(T, dtype=jnp.float32)[:, None] * invf
            cosv = np.asarray(jnp.cos(fr), dtype=np.float32)
            sinv = np.asarray(jnp.sin(fr), dtype=np.float32)
    except Exception:
        invf = np.float32(1.0) / np.float32(10000.0) ** (
            np.arange(half, dtype=np.float32) / np.float32(half))
        fr = (np.arange(T, dtype=np.float32)[:, None] * invf[None, :])
        cosv = np.cos(fr).astype(np.float32)
        sinv = np.sin(fr).astype(np.float32)
    cosT = np.concatenate([cosv, cosv], axis=1)
    sinT = np.concatenate([-sinv, sinv], axis=1)
    cosTl = np.zeros_like(cosT)
    sinTl = np.zeros_like(sinT)
    identf = np.eye(128, dtype=np.float32)
    identb = np.eye(128, dtype=ml_dtypes.bfloat16)
    maskb = np.triu(np.ones((128, 128), dtype=np.float32)).astype(ml_dtypes.bfloat16)
    Lsel = np.zeros((64, 128), dtype=np.float32)
    for hm in range(64):
        for p in range(128):
            if (hm // 2) % 4 == p // 32:
                Lsel[hm, p] = 1.0
    Rsel = np.zeros((64, 16), dtype=np.float32)
    for g in range(16):
        H4, m = g // 2, g % 2
        for hh in range(4):
            Rsel[2 * (4 * H4 + hh) + m, g] = 1.0
    onesb = np.ones((1, 128), dtype=ml_dtypes.bfloat16)
    return dict(identf=identf, identb=identb, maskb=maskb, cosT=cosT, sinT=sinT,
                cosTl=cosTl, sinTl=sinTl,
                Lsel=Lsel, Rsel=Rsel, onesb=onesb,
            fconst=np.array([list(FCONSTS) + [0.0] * (16 - len(FCONSTS))],
                            dtype=np.float32))


def kernel(**inputs):
    nc = build_program()
    consts = make_constants()
    f32 = lambda a: np.ascontiguousarray(np.asarray(a, dtype=np.float32))
    in_maps = []
    for c in range(8):
        m = dict(
            x=f32(inputs["L0_attn_x"][c]),
            cvec=f32(inputs["L0_c"][c:c + 1]),
            Wq=f32(inputs["L0_attn_Wq"]), Wk=f32(inputs["L0_attn_Wk"]),
            Wv=f32(inputs["L0_attn_Wv"]), Wout=f32(inputs["L0_attn_Wout"]),
            Wbeta=f32(inputs["L0_attn_Wbeta"]),
            bbeta=f32(inputs["L0_attn_bbeta"]).reshape(1, 64),
            Wmix=f32(inputs["L0_attn_Wmix"]),
            bmix=f32(inputs["L0_attn_bmix"]).reshape(1, 64),
            boutv=f32(inputs["L0_attn_bout"]).reshape(1, DM),
            ln_g=f32(inputs["L0_attn_ln_g"]).reshape(1, DM),
            ln_b=f32(inputs["L0_attn_ln_b"]).reshape(1, DM),
            rms_w=f32(inputs["L0_ffn_rms_w"]).reshape(1, DM),
            gate_w=f32(inputs["L0_ffn_gate_w"]),
            W1=f32(inputs["L0_ffn_W1"]), b1=f32(inputs["L0_ffn_b1"]),
            W2=f32(inputs["L0_ffn_W2"]), b2=f32(inputs["L0_ffn_b2"]),
            wmod=f32(inputs["L0_wmod"]),
            **consts,
        )
        in_maps.append(m)
    res = run_bass_kernel_spmd(nc, in_maps, core_ids=list(range(8)))
    return np.stack([r["out"] for r in res.results], axis=0)


# version 08
# speedup vs baseline: 1.0663x; 1.0087x over previous
import numpy as np
import ml_dtypes
import concourse.bass as bass
import concourse.mybir as mybir
from concourse.bass_utils import run_bass_kernel_spmd
from contextlib import ExitStack

F32 = mybir.dt.float32
BF16 = mybir.dt.bfloat16
AF = mybir.ActivationFunctionType
ALU = mybir.AluOpType
AX = mybir.AxisListType

T = 2048
DM = 1024
NH = 32
M = 2
C = 128
NCH = T // C
NT = T // 128
E = 8
FF = 4096
VA = 33 * NH
LN_EPS = 1e-5
ATT_EPS = 1e-6
RMS_EPS = 1e-6
TRUNC = None
FCONSTS = (LN_EPS, RMS_EPS, -0.5, -1.0, 0.85, 0.9995, 0.25, -0.25, 1.0, 0.0,
           ATT_EPS, 2.0)
DEBUG = False

_ENG = ("te", "dv", "ac", "gp", "sy")
_BLK = dict(te="tensor", dv="vector", ac="scalar", gp="gpsimd", sy="sync")


class Prog:

    def __init__(self, nc):
        self.nc = nc
        self.ops = []
        self.keys = {}

    def op(self, eng, fn, after=(), key=None, dma=False, stream=None):
        assert eng in _ENG
        if isinstance(after, str):
            after = (after,)
        for k in after:
            assert k in self.keys, f"forward/unknown dep {k!r}"
        if key is not None:
            assert key not in self.keys, f"dup key {key!r}"
            self.keys[key] = len(self.ops)
        if dma and stream is None:
            assert key is not None
            parts = key.split(":")
            while parts and parts[-1].isdigit():
                parts.pop()
            stream = ":".join(parts) or key
        self.ops.append([eng, fn, tuple(after), key, dma, stream])

    def fence(self, tag):
        last = {}
        for i, o in enumerate(self.ops):
            last[o[0]] = i
        lkeys = []
        for eng, i in last.items():
            key = self.ops[i][3]
            if key is None:
                key = f"_f:{tag}:{eng}"
                self.ops[i][3] = key
                self.keys[key] = i
            lkeys.append(key)
        for eng in _ENG:
            self.op(eng, lambda h: h.drain(), after=tuple(lkeys), key=f"F:{tag}:{eng}")

    def lint(self):
        return

    def emit(self):
        awaited = set(k for o in self.ops for k in o[2])
        stream_idx = {}
        for i, o in enumerate(self.ops):
            if o[4]:
                stream_idx.setdefault(o[5], []).append(i)
        sems = {e: self.nc.alloc_semaphore(name=f"S_{e}") for e in _ENG}
        ssems = {s: self.nc.alloc_semaphore(name=f"D_{s}"[:31])
                 for s in stream_idx}
        cnt = dict.fromkeys(_ENG, 0)
        keyval = {}
        for eng, fn, after, key, dma, stream in self.ops:
            if dma:
                continue
            if key in awaited:
                cnt[eng] += 1
                keyval[key] = (eng, cnt[eng])

        def dma_wait(k, widx):
            oi = self.keys[k]
            s = self.ops[oi][5]
            lst = stream_idx[s]
            npre = sum(1 for i in lst if i < widx)
            pos = lst.index(oi) + 1
            assert npre >= pos, f"dep {k} after waiter?"
            return ssems[s], 16 * npre

        with self.nc.Block() as blk:
            def mkrun(eng):
                def run(h):
                    done = {}
                    for widx, (oeng, fn, after, key, dma, stream) in enumerate(self.ops):
                        if oeng != eng:
                            continue
                        for k in after:
                            if self.ops[self.keys[k]][4]:
                                sem, v = dma_wait(k, widx)
                            else:
                                se, v = keyval[k]
                                if se == eng:
                                    continue
                                sem = sems[se]
                            sid = sem.name if hasattr(sem, "name") else id(sem)
                            if done.get(sid, -1) >= v:
                                continue
                            h.wait_ge(sem, v)
                            done[sid] = v
                        inst = fn(h)
                        if not dma and eng == "dv":
                            fs = 9999
                            try:
                                outs = inst.ins.outs
                                for oo in outs:
                                    ba = getattr(oo, "bass_ap", None)
                                    if ba is not None:
                                        fs = min(fs, ba.free_size())
                            except Exception:
                                pass
                            if fs < 96:
                                inst = h.drain()
                        if dma:
                            inst.then_inc(ssems[stream], 16)
                        elif key in awaited:
                            inst.then_inc(sems[eng], 1)
                return run
            for eng in _ENG:
                getattr(blk, _BLK[eng])(mkrun(eng))
        return cnt


def apx(t, col0, dims, part0=0, nparts=128):
    pitch = int(np.prod(t.shape[1:]))
    return bass.AP(t, part0 * pitch + col0, [[pitch, nparts]] + [list(d) for d in dims])


def _seq_mm(h, psum_ap, lhs_list, rhs_list):
    r = None
    n = len(lhs_list)
    for i, (l, rr) in enumerate(zip(lhs_list, rhs_list)):
        r = h.matmul(psum_ap, l, rr, start=(i == 0), stop=(i == n - 1))
    return r


def build_program():
    nc = bass.Bass("TRN2", target_bir_lowering=False, detect_race_conditions=False)

    def din(name, shape, dtype=F32):
        return nc.dram_tensor(name, list(shape), dtype, kind="ExternalInput")

    x_hbm = din("x", [T, DM])
    cvec = din("cvec", [1, DM])
    Wq = din("Wq", [DM, DM]); Wk = din("Wk", [DM, DM]); Wv = din("Wv", [DM, DM])
    Wout = din("Wout", [DM, DM])
    Wbeta = din("Wbeta", [64, DM]); bbeta = din("bbeta", [1, 64])
    Wmix = din("Wmix", [64, DM]); bmix = din("bmix", [1, 64])
    boutv = din("boutv", [1, DM])
    ln_g = din("ln_g", [1, DM]); ln_b = din("ln_b", [1, DM])
    rms_w = din("rms_w", [1, DM])
    gate_w = din("gate_w", [E, DM])
    W1 = din("W1", [E, DM, FF]); b1 = din("b1", [E, FF])
    W2 = din("W2", [E, FF, DM]); b2 = din("b2", [E, DM])
    wmod = din("wmod", [DM, 6 * DM])
    identf = din("identf", [128, 128])
    identb = din("identb", [128, 128], BF16)
    maskb = din("maskb", [128, 128], BF16)
    cosTd = din("cosT", [T, 32])
    sinTd = din("sinT", [T, 32])
    cosTl = din("cosTl", [T, 32])
    sinTl = din("sinTl", [T, 32])
    Lsel = din("Lsel", [64, 128])
    Rsel = din("Rsel", [64, 16])
    onesb = din("onesb", [1, 128], BF16)
    fconst = din("fconst", [1, 16])

    out = nc.dram_tensor("out", [T, DM], F32, kind="ExternalOutput")
    sk = "ExternalOutput" if DEBUG else "Internal"
    h1sp = nc.dram_tensor("h1sp", [128, NT * DM], F32, kind=sk)
    xsp = nc.dram_tensor("xsp", [128, NT * DM], F32, kind=sk)
    phiqsp = nc.dram_tensor("phiqsp", [128, 8 * T], BF16, kind=sk)
    ktmsp = nc.dram_tensor("ktmsp", [128, NT * DM], BF16, kind=sk)
    vaugsp = nc.dram_tensor("vaugsp", [128, NT * VA], BF16, kind=sk)
    uacc = nc.dram_tensor("uacc", [128, NT * DM], F32, kind=sk)
    modsp = nc.dram_tensor("modsp", [16, DM], F32, kind=sk)
    if DEBUG:
        d_logb = nc.dram_tensor("d_logb", [64, T], F32, kind="ExternalOutput")
        d_mix = nc.dram_tensor("d_mix", [128, NT * 64], F32, kind="ExternalOutput")
        d_xnt = nc.dram_tensor("d_xnt", [128, 8 * T], BF16, kind="ExternalOutput")
        d_msc = nc.dram_tensor("d_msc", [128, NT * 8], F32, kind="ExternalOutput")
        d_h1t = nc.dram_tensor("d_h1t", [128, 8 * T], BF16, kind="ExternalOutput")

    P = Prog(nc)
    with ExitStack() as top:
        xnT = top.enter_context(nc.sbuf_tensor("xnT", [128, 8 * T], BF16))
        msc = top.enter_context(nc.sbuf_tensor("msc", [128, NT * 8], F32))
        sidf = top.enter_context(nc.sbuf_tensor("sidf", [128, 128], F32))
        sidb = top.enter_context(nc.sbuf_tensor("sidb", [128, 128], BF16))
        smask = top.enter_context(nc.sbuf_tensor("smask", [128, 128], BF16))
        sones = top.enter_context(nc.sbuf_tensor("sones", [1, 128], BF16))
        sgateT = top.enter_context(nc.sbuf_tensor("sgateT", [128, 64], BF16))
        sfc = top.enter_context(nc.sbuf_tensor("sfc", [128, 16], F32))
        P.op("sy", lambda h: h.dma_start(
            sfc[:, :], bass.AP(fconst, 0, [[0, 128], [1, 16]])), key="ld:fc", dma=True)
        P.fence("fc")
        for fi, fv in enumerate(FCONSTS):
            nc.const_aps.register(F32, fv, sfc[:, fi:fi + 1])

        with ExitStack() as mid:
            scosT = mid.enter_context(nc.sbuf_tensor("scosT", [128, NT * 32], F32))
            ssinT = mid.enter_context(nc.sbuf_tensor("ssinT", [128, NT * 32], F32))
            sLsel = mid.enter_context(nc.sbuf_tensor("sLsel", [64, 128], F32))
            sRsel = mid.enter_context(nc.sbuf_tensor("sRsel", [64, 16], F32))
            snbb = mid.enter_context(nc.sbuf_tensor("snbb", [64, 1], F32))
            WoT = mid.enter_context(nc.sbuf_tensor("WoT", [128, 8 * DM], BF16))
            WbT = mid.enter_context(nc.sbuf_tensor("WbT", [128, 8 * 64], BF16))
            WmT = mid.enter_context(nc.sbuf_tensor("WmT", [128, 8 * 64], BF16))
            logb = mid.enter_context(nc.sbuf_tensor("logb", [64, T], F32))
            mixa = mid.enter_context(nc.sbuf_tensor("mixa", [128, NT * 64], F32))
            colv = mid.enter_context(nc.sbuf_tensor("colv", [128, 96], F32))
            (CV_CA1, CV_SA, CV_CB1, CV_SB, CV_LGA, CV_LBA, CV_GB, CV_T0,
             CV_GA, CV_LNG, CV_LNB, CV_T1) = [8 * i for i in range(12)]

            def cv(off, w=8):
                return colv[:, off: off + w]

            with ExitStack() as s05:
                WqT = s05.enter_context(nc.sbuf_tensor("WqT", [128, 8 * DM], BF16))
                WkT = s05.enter_context(nc.sbuf_tensor("WkT", [128, 8 * DM], BF16))
                WvT = s05.enter_context(nc.sbuf_tensor("WvT", [128, 8 * DM], BF16))
                h1T = s05.enter_context(nc.sbuf_tensor("h1T", [128, 8 * T], BF16))

                with ExitStack() as p0:
                    wst = p0.enter_context(nc.sbuf_tensor("wst", [128, 8 * DM], BF16))
                    wbst = p0.enter_context(nc.sbuf_tensor("wbst", [128, DM], BF16))
                    wmst = p0.enter_context(nc.sbuf_tensor("wmst", [128, DM], BF16))
                    wgst = p0.enter_context(nc.sbuf_tensor("wgst", [128, DM], BF16))
                    modst = [p0.enter_context(nc.sbuf_tensor(f"modst{i}", [128, DM], BF16))
                             for i in range(2)]
                    tlo = p0.enter_context(nc.sbuf_tensor("tlo", [128, 1024], F32))
                    smodT = p0.enter_context(nc.sbuf_tensor("smodT", [128, 48], F32))
                    scv = p0.enter_context(nc.sbuf_tensor("scv", [128, 8], F32))
                    srow8 = p0.enter_context(nc.sbuf_tensor("srow8", [128, 128], F32))
                    scvb = p0.enter_context(nc.sbuf_tensor("scvb", [128, 8], BF16))
                    sbbv = p0.enter_context(nc.sbuf_tensor("sbbv", [64, 1], F32))
                    sbmc = p0.enter_context(nc.sbuf_tensor("sbmc", [32, 2], F32))
                    pmod = p0.enter_context(nc.psum_tensor("pmod", [128, 512], F32))
                    ptb = [p0.enter_context(nc.psum_tensor(f"p0tb{i}", [128, 128], BF16))
                           for i in range(2)]

                    for nm, dst, src in (
                        ("idf", sidf[:, :], identf[:, :]),
                        ("idb", sidb[:, :], identb[:, :]),
                        ("msk", smask[:, :], maskb[:, :]),
                        ("one", sones[:, :], onesb[:, :]),
                        ("lsl", sLsel[:, :], Lsel[:, :]),
                        ("rsl", sRsel[:, :], Rsel[:, :]),
                        ("cosT", scosT[:, :].rearrange("p (t f) -> p t f", f=32),
                         bass.AP(cosTd, 0, [[32, 128], [32 * 128, NT], [1, 32]])),
                        ("sinT", ssinT[:, :].rearrange("p (t f) -> p t f", f=32),
                         bass.AP(sinTd, 0, [[32, 128], [32 * 128, NT], [1, 32]])),
                        ("cosTl", tlo[:, 0:512].rearrange("p (t f) -> p t f", f=32),
                         bass.AP(cosTl, 0, [[32, 128], [32 * 128, NT], [1, 32]])),
                        ("sinTl", tlo[:, 512:1024].rearrange("p (t f) -> p t f", f=32),
                         bass.AP(sinTl, 0, [[32, 128], [32 * 128, NT], [1, 32]])),
                        ("bb", sbbv[:, :], bass.AP(bbeta, 0, [[1, 64], [1, 1]])),
                        ("bmc", sbmc[:, :], bass.AP(bmix, 0, [[2, 32], [1, 2]])),
                    ):
                        P.op("sy", (lambda d, s: lambda h: h.dma_start(d, s))(dst, src),
                             key=f"ld:{nm}", dma=True)

                    P.op("dv", lambda h: h.tensor_add(scosT[:, :], scosT[:, :],
                                                      tlo[:, 0:512]),
                         after=["ld:cosT", "ld:cosTl"], key="cosfix")
                    P.op("dv", lambda h: h.tensor_add(ssinT[:, :], ssinT[:, :],
                                                      tlo[:, 512:1024]),
                         after=["ld:sinT", "ld:sinTl"], key="sinfix")
                    def w_src(W, nch, wid):
                        return bass.AP(W, 0, [[wid, 128], [128 * wid, nch], [1, wid]])

                    for wi, (W, dstT) in enumerate(((Wq, WqT), (Wk, WkT), (Wv, WvT), (Wout, WoT))):
                        after = [f"wtr:{wi - 1}:63"] if wi else []
                        P.op("gp", (lambda W=W: lambda h: h.dma_start(
                            wst[:, :].rearrange("p (r d) -> p r d", d=DM), w_src(W, 8, DM)))(),
                            after=after, key=f"wld:{wi}", dma=True)
                        for i in range(64):
                            r, dc = i // 8, i % 8
                            pt = ptb[i % 2]
                            aft = [f"wld:{wi}", "ld:idb"]
                            if i >= 2:
                                aft.append(f"wev:{wi}:{i - 2}")
                            elif wi > 0:
                                aft.append(f"wev:{wi - 1}:{62 + i}")
                            P.op("te", (lambda pt=pt, r=r, dc=dc: lambda h: h.transpose(
                                pt[:, :], wst[:, DM * r + 128 * dc: DM * r + 128 * (dc + 1)],
                                sidb[:, :]))(),
                                after=aft, key=f"wtr:{wi}:{i}")
                            P.op("dv", (lambda pt=pt, dstT=dstT, r=r, dc=dc: lambda h: h.tensor_copy(
                                dstT[:, DM * dc + 128 * r: DM * dc + 128 * (r + 1)], pt[:, :]))(),
                                after=[f"wtr:{wi}:{i}"], key=f"wev:{wi}:{i}")
                    P.op("dv", lambda h: h.memset(wbst[:, :], 0.0), key="wz:b")
                    P.op("dv", lambda h: h.memset(wmst[:, :], 0.0), key="wz:m")
                    P.op("dv", lambda h: h.memset(wgst[:, :], 0.0), key="wz:g")
                    P.op("gp", lambda h: h.dma_start(wbst[0:64, :], Wbeta[:, :]), after=["wz:b"], key="wld:b", dma=True)
                    P.op("gp", lambda h: h.dma_start(wmst[0:64, :], Wmix[:, :]), after=["wz:m"], key="wld:m", dma=True)
                    P.op("gp", lambda h: h.dma_start(wgst[0:8, :], gate_w[:, :]), after=["wz:g"], key="wld:g", dma=True)
                    for si, (st, rows, dstT, wd, ldk) in enumerate((
                            (wbst, 64, WbT, 64, "wld:b"), (wmst, 64, WmT, 64, "wld:m"),
                            (wgst, 8, sgateT, 8, "wld:g"))):
                        for dc in range(8):
                            i = si * 8 + dc
                            pt = ptb[i % 2]
                            aft = [ldk, "wev:3:63", "wz:b", "wz:m", "wz:g"]
                            if i >= 2:
                                aft.append(f"swev:{i - 2}")
                            P.op("te", (lambda pt=pt, st=st, dc=dc: lambda h: h.transpose(
                                pt[:, :], st[:, 128 * dc: 128 * (dc + 1)],
                                sidb[:, :]))(),
                                after=aft, key=f"swtr:{i}")
                            P.op("dv", (lambda pt=pt, dstT=dstT, wd=wd, dc=dc, rows=rows:
                                        lambda h: h.tensor_copy(
                                dstT[:, wd * dc: wd * (dc + 1)], pt[0:128, 0:rows]))(),
                                after=[f"swtr:{i}"], key=f"swev:{i}")

                    P.op("dv", lambda h: h.memset(srow8[:, :], 0.0), key="r8:z")
                    prevk = []
                    for nm, dsrc, dst_ap in (
                            ("cv", cvec, scv[:, :]),
                            ("lngc", ln_g, apx(colv, CV_LNG, [[1, 8]])),
                            ("lnbc", ln_b, apx(colv, CV_LNB, [[1, 8]]))):
                        P.op("sy", (lambda dsrc=dsrc: lambda h: h.dma_start(
                            srow8[0:8, :], bass.AP(dsrc, 0, [[128, 8], [1, 128]])))(),
                            after=["r8:z"] + [k for k in prevk if k.startswith("cvt")],
                            key=f"rld:{nm}", dma=True)
                        P.op("te", (lambda: lambda h: h.transpose(
                            pmod[:, 128:256], srow8[:, :], sidf[:, :]))(),
                            after=[f"rld:{nm}", "ld:idf"] +
                                  [k for k in prevk if k.startswith("cve")],
                            key=f"cvt:{nm}")
                        P.op("ac", (lambda dst_ap=dst_ap: lambda h: h.copy(
                            dst_ap, pmod[:, 128:136]))(),
                            after=[f"cvt:{nm}"], key=f"cve:{nm}")
                        prevk = [f"cvt:{nm}", f"cve:{nm}"]
                    P.op("ac", lambda h: h.activation(scv[:, :], scv[:, :], AF.Silu),
                         after=["cve:cv"], key="silc")
                    P.op("dv", lambda h: h.tensor_copy(scvb[:, :], scv[:, :]),
                         after=["silc"], key="silcb")
                    P.op("ac", lambda h: h.mul(snbb[:, :], sbbv[:, :], -1.0),
                         after=["ld:bb"], key="nbb")
                    for jt in range(48):
                        mst = modst[jt % 2]
                        aft = [] if jt < 2 else [f"modmm:{jt - 2}"]
                        P.op("gp", (lambda jt=jt, mst=mst: lambda h: h.dma_start(
                            mst[:, :].rearrange("p (kc j) -> p kc j", j=128),
                            bass.AP(wmod, 128 * jt, [[6 * DM, 128], [128 * 6 * DM, 8], [1, 128]])))(),
                            after=aft, key=f"modld:{jt}", dma=True, stream=f"mod{jt % 2}")
                        P.op("te", (lambda mst=mst: lambda h: _seq_mm(
                            h, pmod[:, 0:1],
                            [mst[:, 128 * kc: 128 * (kc + 1)] for kc in range(8)],
                            [scvb[:, kc:kc + 1] for kc in range(8)]))(),
                            after=[f"modld:{jt}", "silcb"] +
                                  ([f"modev:{jt - 1}"] if jt else ["cve:lnbc"]),
                            key=f"modmm:{jt}")
                        P.op("ac", (lambda jt=jt: lambda h: h.copy(
                            smodT[:, jt:jt + 1], pmod[:, 0:1]))(),
                            after=[f"modmm:{jt}"], key=f"modev:{jt}")

                    def mTn(i):
                        return smodT[:, 8 * i: 8 * (i + 1)]
                    P.op("dv", lambda h: h.tensor_scalar_add(cv(CV_CA1), mTn(1), 1.0),
                         after=["modev:47"], key="v:ca1")
                    P.op("dv", lambda h: h.tensor_copy(cv(CV_SA), mTn(0)), key="v:sa")
                    P.op("dv", lambda h: h.tensor_scalar_add(cv(CV_CB1), mTn(4), 1.0), key="v:cb1")
                    P.op("dv", lambda h: h.tensor_copy(cv(CV_SB), mTn(3)), key="v:sb")
                    for gi, (src_i, dst) in enumerate(((2, CV_GA), (5, CV_GB))):
                        P.op("dv", (lambda src_i=src_i: lambda h: h.tensor_single_scalar(
                            cv(CV_T0), mTn(src_i), 0.0, ALU.is_ge))())
                        P.op("dv", lambda h: h.tensor_scalar(
                            cv(CV_T0), cv(CV_T0), 2.0, -1.0, ALU.mult, ALU.add))
                        P.op("dv", (lambda src_i=src_i: lambda h: h.tensor_scalar_mul(
                            cv(CV_T1), mTn(src_i), -1.0))())
                        P.op("dv", (lambda src_i=src_i: lambda h: h.tensor_tensor(
                            cv(CV_T1), cv(CV_T1), mTn(src_i), ALU.max))())
                        P.op("dv", (lambda gi=gi: lambda h: h.tensor_scalar_max(
                            cv(CV_T1), cv(CV_T1), 0.25))())
                        P.op("dv", (lambda dst=dst: lambda h: h.tensor_mul(
                            cv(dst), cv(CV_T0), cv(CV_T1)))(), key=f"g:sel:{gi}")
                    P.op("dv", lambda h: h.tensor_mul(cv(CV_LGA), cv(CV_LNG), cv(CV_GA)),
                         after=["g:sel:0", "cve:lngc"])
                    P.op("dv", lambda h: h.tensor_mul(cv(CV_LBA), cv(CV_LNB), cv(CV_GA)),
                         after=["cve:lnbc"])
                    P.op("dv", lambda h: h.tensor_tensor(
                        colv[0:32, CV_T0:CV_T0 + 1], sbmc[0:32, 0:1], sbmc[0:32, 1:2],
                        ALU.subtract), after=["ld:bmc"], key="v:dbm")

                    P.op("te", lambda h: h.transpose(
                        pmod[0:96, 128:256], colv[:, 0:96], sidf[:, :]),
                        after=["v:dbm", "modev:47"], key="cvt:sp")
                    P.op("ac", lambda h: h.copy(srow8[0:96, :], pmod[0:96, 128:256]),
                         after=["cvt:sp"], key="cve:sp")
                    P.op("gp", lambda h: h.dma_start(
                        bass.AP(modsp, 0, [[128, 96], [1, 128]]), srow8[0:96, :]),
                        after=["cve:sp"], key="sp:mod1", dma=True)
                    P.fence("p0")

                with ExitStack() as p1:
                    ca1x = p1.enter_context(nc.sbuf_tensor("ca1x", [128, DM], F32))
                    sax = p1.enter_context(nc.sbuf_tensor("sax", [128, DM], F32))
                    dbmx = p1.enter_context(nc.sbuf_tensor("dbmx", [128, 32], F32))
                    x_st = [p1.enter_context(nc.sbuf_tensor(f"x_st{i}", [128, DM], F32))
                            for i in range(2)]
                    h1_st = [p1.enter_context(nc.sbuf_tensor(f"h1_st{i}", [128, DM], F32))
                             for i in range(2)]
                    sstat = p1.enter_context(nc.sbuf_tensor("sstat", [128, 16], F32))
                    smv = p1.enter_context(nc.sbuf_tensor("smv", [128, 8], F32))
                    kraw1 = p1.enter_context(nc.sbuf_tensor("kraw1", [128, DM], F32))
                    krop = p1.enter_context(nc.sbuf_tensor("krop", [128, DM], F32))
                    krop2 = p1.enter_context(nc.sbuf_tensor("krop2", [128, DM], F32))
                    kbf = [p1.enter_context(nc.sbuf_tensor(f"kbf{i}", [128, DM], BF16))
                           for i in range(2)]
                    va_st = [p1.enter_context(nc.sbuf_tensor(f"va_st{i}", [128, VA], BF16))
                             for i in range(2)]
                    qtm1 = p1.enter_context(nc.sbuf_tensor("qtm1", [128, DM], BF16))
                    qfm1 = p1.enter_context(nc.sbuf_tensor("qfm1", [128, DM], BF16))
                    bst = p1.enter_context(nc.sbuf_tensor("bst", [64, 512], F32))
                    bst2 = p1.enter_context(nc.sbuf_tensor("bst2", [64, 512], F32))
                    mst1 = p1.enter_context(nc.sbuf_tensor("mst1", [128, 32], F32))
                    smix64 = p1.enter_context(nc.sbuf_tensor("smix64", [128, 64], F32))
                    mst2 = p1.enter_context(nc.sbuf_tensor("mst2", [128, 32], F32))
                    pp = [p1.enter_context(nc.psum_tensor(f"pp{i}", [128, 512], F32))
                          for i in range(2)]
                    ptf = [p1.enter_context(nc.psum_tensor(f"ptf{i}", [128, 128], F32))
                           for i in range(2)]
                    ptb1 = [p1.enter_context(nc.psum_tensor(f"p1tb{i}", [128, 128], BF16))
                            for i in range(2)]

                    def cvread(nm, dsttile, vi, deps=("sp:mod1",)):
                        P.op("sy", (lambda dsttile=dsttile, vi=vi: lambda h: h.dma_start(
                            dsttile[:, :],
                            bass.AP(modsp, vi * DM, [[0, 128], [1, DM]])))(),
                            after=deps, key=f"bc:{nm}", dma=True)
                    cvread("ca1", ca1x, 0)
                    cvread("sa", sax, 1)
                    P.op("sy", lambda h: h.dma_start(
                        dbmx[:, :], bass.AP(modsp, 7 * DM, [[0, 128], [1, 32]])),
                        after=["sp:mod1"], key="bc:dbm", dma=True)

                    for t in range(NT):
                        xs, hs = x_st[t % 2], h1_st[t % 2]
                        aft = [] if t < 2 else [f"h1:{t - 2}", f"h1tr:{t - 2}:7",
                                                      f"sp:h1:{t - 2}"]
                        P.op("sy", (lambda t=t, xs=xs: lambda h: h.dma_start(
                            xs[:, :], x_hbm[128 * t: 128 * (t + 1), :]))(),
                            after=aft, key=f"ld:x:{t}", dma=True, stream=f"x{t % 2}")
                        P.op("dv", (lambda xs=xs: lambda h: h.bn_stats(
                            sstat[:, 0:6], xs[:, 0:512]))(), after=[f"ld:x:{t}"])
                        P.op("dv", (lambda xs=xs: lambda h: h.bn_stats(
                            sstat[:, 6:12], xs[:, 512:1024]))())
                        P.op("dv", lambda h: h.bn_aggr(smv[:, 0:2], sstat[:, 0:12]),
                             key=f"h1s:{t}")
                        P.op("ac", lambda h: h.activation(smv[:, 2:3], smv[:, 1:2], AF.Ln,
                                                          bias=LN_EPS), after=[f"h1s:{t}"])
                        P.op("ac", lambda h: h.activation(smv[:, 3:4], smv[:, 2:3], AF.Exp,
                                                          scale=-0.5), key=f"h1rs:{t}")
                        P.op("dv", (lambda xs=xs, hs=hs: lambda h: h.tensor_scalar(
                            hs[:, :], xs[:, :], smv[:, 0:1], smv[:, 3:4],
                            ALU.subtract, ALU.mult))(),
                            after=[f"h1rs:{t}"] + ([f"h1tr:{t - 2}:7", f"sp:h1:{t - 2}"]
                                                   if t >= 2 else []))
                        P.op("dv", (lambda hs=hs: lambda h: h.tensor_mul(
                            hs[:, :], hs[:, :], ca1x[:, :]))(), after=["bc:ca1"])
                        P.op("dv", (lambda hs=hs: lambda h: h.tensor_add(
                            hs[:, :], hs[:, :], sax[:, :]))(), after=["bc:sa"], key=f"h1:{t}")
                        P.op("gp", (lambda t=t, hs=hs: lambda h: h.dma_start(
                            h1sp[:, DM * t: DM * (t + 1)], hs[:, :]))(),
                            after=[f"h1:{t}"], key=f"sp:h1:{t}", dma=True)
                        for b in range(8):
                            pt = ptf[b % 2]
                            aft = [f"h1:{t}", "ld:idf"]
                            if t * 8 + b >= 2:
                                pb, pbb = divmod(t * 8 + b - 2, 8)
                                aft.append(f"h1te:{pb}:{pbb}")
                            P.op("te", (lambda pt=pt, hs=hs, b=b: lambda h: h.transpose(
                                pt[:, :], hs[:, 128 * b: 128 * (b + 1)], sidf[:, :]))(),
                                after=aft, key=f"h1tr:{t}:{b}")
                            P.op("ac", (lambda pt=pt, t=t, b=b: lambda h: h.copy(
                                h1T[:, T * b + 128 * t: T * b + 128 * (t + 1)], pt[:, :]))(),
                                after=[f"h1tr:{t}:{b}"], key=f"h1te:{t}:{b}")
                    P.fence("p05")
                    if DEBUG:
                        P.op("gp", lambda h: h.dma_start(d_h1t[:, :], h1T[:, :]),
                             key="dbg:h1t", dma=True)

                    for n in range(4):
                        ppn = pp[n % 2]
                        aft = [] if n < 2 else [f"bt:ev:{n - 2}"]
                        P.op("te", (lambda n=n, ppn=ppn: lambda h: _seq_mm(
                            h, ppn[0:64, 0:512],
                            [WbT[:, 64 * kc: 64 * (kc + 1)] for kc in range(8)],
                            [h1T[:, T * kc + 512 * n: T * kc + 512 * (n + 1)] for kc in range(8)]))(),
                            after=aft, key=f"bt:mm:{n}")
                        P.op("ac", (lambda ppn=ppn: lambda h: h.activation(
                            bst[0:64, 0:512], ppn[0:64, 0:512], AF.Exp,
                            bias=snbb[:, :], scale=-1.0))(),
                            after=[f"bt:mm:{n}", "nbb"] +
                                  ([f"bt:cl:{n - 1}"] if n else []), key=f"bt:ev:{n}")
                        P.op("dv", lambda h: h.tensor_scalar_add(bst2[0:64, :], bst[0:64, :], 1.0),
                             after=[f"bt:ev:{n}"])
                        P.op("dv", lambda h: h.reciprocal(bst[0:64, :], bst2[0:64, :]))
                        P.op("dv", lambda h: h.tensor_scalar(
                            bst2[0:64, :], bst[0:64, :], 0.85, 0.9995, ALU.max, ALU.min),
                            key=f"bt:cl:{n}")
                        P.op("ac", (lambda n=n: lambda h: h.activation(
                            logb[:, 512 * n: 512 * (n + 1)], bst2[0:64, :], AF.Ln))(),
                            after=[f"bt:cl:{n}"], key=f"bt:ln:{n}")
                    if DEBUG:
                        P.op("gp", lambda h: h.dma_start(d_logb[:, :], logb[:, :]),
                             after=[f"bt:ln:{n}" for n in range(4)], key="dbg:logb", dma=True)

                    def proj_mm(t, n, ppn, WT):
                        def f(h):
                            return _seq_mm(
                                h, ppn[:, 0:512],
                                [h1T[:, T * kc + 128 * t: T * kc + 128 * (t + 1)] for kc in range(8)],
                                [WT[:, DM * kc + 512 * n: DM * kc + 512 * (n + 1)] for kc in range(8)])
                        return f

                    for t in range(NT):
                        ppn = pp[0]
                        aft = (["bt:ev:2", "bt:ev:3"] if t == 0
                               else [f"mx:w:{t - 1}", f"qph:{t - 1}"])
                        P.op("te", (lambda t=t, ppn=ppn: lambda h: _seq_mm(
                            h, ppn[:, 0:64],
                            [h1T[:, T * kc + 128 * t: T * kc + 128 * (t + 1)] for kc in range(8)],
                            [WmT[:, 64 * kc: 64 * (kc + 1)] for kc in range(8)]))(),
                            after=aft, key=f"mx:mm:{t}")
                        P.op("ac", (lambda ppn=ppn: lambda h: h.copy(
                            smix64[:, :], ppn[:, 0:64]))(),
                            after=[f"mx:mm:{t}"] + ([f"mx:d:{t - 1}"] if t else []),
                            key=f"mx:ev:{t}")
                        P.op("dv", (lambda: lambda h: h.tensor_tensor(
                            mst1[:, :], apx(smix64, 0, [[2, 32]]), apx(smix64, 1, [[2, 32]]),
                            ALU.subtract))(),
                            after=[f"mx:ev:{t}"] + ([f"mx:e:{t - 1}"] if t else []),
                            key=f"mx:d:{t}")
                        P.op("dv", lambda h: h.tensor_add(mst1[:, :], mst1[:, :], dbmx[:, :]),
                             after=["bc:dbm"], key=f"mx:d2:{t}")
                        P.op("ac", lambda h: h.activation(mst2[:, :], mst1[:, :], AF.Exp,
                                                          scale=-1.0),
                             after=[f"mx:d2:{t}"], key=f"mx:e:{t}")
                        P.op("dv", lambda h: h.tensor_scalar_add(mst1[:, :], mst2[:, :], 1.0),
                             after=[f"mx:e:{t}"])
                        P.op("dv", (lambda t=t: lambda h: h.reciprocal(
                            apx(mixa, 64 * t, [[2, 32]]), mst1[:, :]))())
                        P.op("dv", (lambda t=t: lambda h: h.tensor_scalar(
                            apx(mixa, 64 * t + 1, [[2, 32]]), apx(mixa, 64 * t, [[2, 32]]),
                            -1.0, 1.0, ALU.mult, ALU.add))(), key=f"mx:w:{t}")

                        va = va_st[t % 2]
                        for n in range(2):
                            ppn = pp[1] if n == 0 else pp[0]
                            aft = ([f"mx:d:{t}"] if n == 1 else
                                   ([f"qph:{t - 1}"] if t else []) +
                                   ([f"sp:va:{t - 2}"] if t >= 2 else []))
                            P.op("te", proj_mm(t, n, ppn, WvT), after=aft, key=f"v:mm:{t}:{n}")
                            P.op("ac", (lambda va=va, n=n, ppn=ppn: lambda h: h.activation(
                                apx(va, 33 * 16 * n, [[33, 16], [1, 32]]), ppn[:, 0:512],
                                AF.Copy))(), after=[f"v:mm:{t}:{n}"], key=f"v:ev:{t}:{n}")
                        P.op("dv", (lambda va=va: lambda h: h.memset(
                            apx(va, 32, [[33, 32]]), 1.0))(),
                            after=[f"v:ev:{t}:0", f"v:ev:{t}:1"], key=f"v:on:{t}")
                        P.op("gp", (lambda t=t, va=va: lambda h: h.dma_start(
                            vaugsp[:, VA * t: VA * (t + 1)], va[:, :]))(),
                            after=[f"v:on:{t}"], key=f"sp:va:{t}", dma=True)

                        for which, WT in (("k", WkT), ("q", WqT)):
                            kr = kraw1
                            prev_done = f"qph:{t - 1}" if t >= 1 else None
                            for n in range(2):
                                ppn = pp[n]
                                aft = []
                                if which == "k" and n == 0:
                                    aft = [f"v:ev:{t}:1"]
                                if which == "q" and n == 0:
                                    aft = [f"kph:{t}"]
                                if prev_done:
                                    aft.append(prev_done)
                                P.op("te", proj_mm(t, n, ppn, WT), after=aft,
                                     key=f"{which}:mm:{t}:{n}")
                                P.op("ac", (lambda kr=kr, n=n, ppn=ppn: lambda h: h.copy(
                                    kr[:, 512 * n: 512 * (n + 1)], ppn[:, 0:512]))(),
                                    after=[f"{which}:mm:{t}:{n}"], key=f"{which}:ev:{t}:{n}")
                            cos_ap = apx(scosT, 32 * t, [[0, 32], [1, 32]])
                            sin_lo = apx(ssinT, 32 * t, [[0, 32], [1, 16]])
                            sin_hi = apx(ssinT, 32 * t + 16, [[0, 32], [1, 16]])
                            P.op("dv", (lambda kr=kr, cos_ap=cos_ap: lambda h: h.tensor_tensor(
                                krop[:, :].rearrange("p (a b) -> p a b", b=32),
                                kr[:, :].rearrange("p (a b) -> p a b", b=32), cos_ap, ALU.mult))(),
                                after=[f"{which}:ev:{t}:0", f"{which}:ev:{t}:1", "cosfix"])
                            P.op("dv", (lambda kr=kr, sin_lo=sin_lo: lambda h: h.tensor_tensor(
                                apx(krop2, 0, [[32, 32], [1, 16]]),
                                apx(kr, 16, [[32, 32], [1, 16]]), sin_lo, ALU.mult))(),
                                after=["sinfix"])
                            P.op("dv", (lambda kr=kr, sin_hi=sin_hi: lambda h: h.tensor_tensor(
                                apx(krop2, 16, [[32, 32], [1, 16]]),
                                apx(kr, 0, [[32, 32], [1, 16]]), sin_hi, ALU.mult))())
                            P.op("dv", lambda h: h.tensor_add(krop[:, :], krop[:, :], krop2[:, :]))
                            P.op("dv", lambda h: h.tensor_scalar_min(krop2[:, :], krop[:, :], 0.0),
                                 key=f"{which}mn:{t}")
                            P.op("ac", lambda h: h.activation(krop2[:, :], krop2[:, :], AF.Exp),
                                 after=[f"{which}mn:{t}"], key=f"{which}ex:{t}")
                            P.op("dv", lambda h: h.tensor_scalar_max(krop[:, :], krop[:, :], 0.0))
                            kb = kbf[t % 2] if which == "k" else qtm1
                            bufw = []
                            if which == "k" and t >= 2:
                                bufw = [f"sp:kt:{t - 2}"]
                            if which == "q" and t >= 1:
                                bufw = [f"qtr:{t - 1}:7"]
                            P.op("dv", (lambda kb=kb: lambda h: h.tensor_add(
                                kb[:, :], krop[:, :], krop2[:, :]))(),
                                after=[f"{which}ex:{t}"] + bufw, key=f"{which}ph:{t}")
                            if which == "k":
                                P.op("gp", (lambda t=t, kb=kb: lambda h: h.dma_start(
                                    ktmsp[:, DM * t: DM * (t + 1)], kb[:, :]))(),
                                    after=[f"kph:{t}"], key=f"sp:kt:{t}", dma=True)
                        qb, qf = qtm1, qfm1
                        for b in range(8):
                            pt = ptb1[b % 2]
                            aft = [f"qph:{t}"]
                            if t * 8 + b >= 2:
                                pb, pbb = divmod(t * 8 + b - 2, 8)
                                aft.append(f"qte:{pb}:{pbb}")
                            if t >= 1 and b == 0:
                                aft.append(f"sp:q:{t - 1}")
                            P.op("te", (lambda pt=pt, qb=qb, b=b: lambda h: h.transpose(
                                pt[:, :], qb[:, 128 * b: 128 * (b + 1)], sidb[:, :]))(),
                                after=aft, key=f"qtr:{t}:{b}")
                            P.op("ac", (lambda pt=pt, qf=qf, b=b: lambda h: h.copy(
                                qf[:, 128 * b: 128 * (b + 1)], pt[:, :]))(),
                                after=[f"qtr:{t}:{b}"], key=f"qte:{t}:{b}")
                        P.op("gp", (lambda t=t, qf=qf: lambda h: h.dma_start(
                            bass.AP(phiqsp, 128 * t, [[8 * T, 128], [T, 8], [1, 128]]),
                            qf[:, :].rearrange("p (b j) -> p b j", j=128)))(),
                            after=[f"qte:{t}:{b}" for b in range(8)], key=f"sp:q:{t}", dma=True)
                    P.fence("p1")

            with ExitStack() as p2:
                cb1x = p2.enter_context(nc.sbuf_tensor("cb1x", [128, DM], F32))
                sbvx = p2.enter_context(nc.sbuf_tensor("sbvx", [128, DM], F32))
                lgax = p2.enter_context(nc.sbuf_tensor("lgax", [128, DM], F32))
                lbax = p2.enter_context(nc.sbuf_tensor("lbax", [128, DM], F32))
                rmswx = p2.enter_context(nc.sbuf_tensor("rmswx", [128, DM], F32))
                boutx = p2.enter_context(nc.sbuf_tensor("boutx", [128, DM], F32))
                phq_st = [p2.enter_context(nc.sbuf_tensor(f"phq{i}", [128, DM], BF16))
                          for i in range(2)]
                ktm_st = [p2.enter_context(nc.sbuf_tensor(f"ktm{i}", [128, DM], BF16))
                          for i in range(2)]
                va2_st = [p2.enter_context(nc.sbuf_tensor(f"va2{i}", [128, VA], BF16))
                          for i in range(2)]
                h12_st = [p2.enter_context(nc.sbuf_tensor(f"h12{i}", [128, DM], F32))
                          for i in range(2)]
                x2_st = [p2.enter_context(nc.sbuf_tensor(f"x2{i}", [128, DM], F32))
                         for i in range(2)]
                sPTall = p2.enter_context(nc.sbuf_tensor("sPTall", [128, NCH * 64], F32))
                sIvTall = p2.enter_context(nc.sbuf_tensor("sIvTall", [128, NCH * 64], BF16))
                sPCtall = p2.enter_context(nc.sbuf_tensor("sPCtall", [128, NCH * 16], F32))
                ktil = [p2.enter_context(nc.sbuf_tensor(f"ktil{i}", [128, DM], BF16))
                        for i in range(2)]
                ktilT = [p2.enter_context(nc.sbuf_tensor(f"ktilT{i}", [128, DM], BF16))
                         for i in range(2)]
                sAt = [p2.enter_context(nc.sbuf_tensor(f"sAt{i}", [128, 512], BF16))
                       for i in range(2)]
                oa = p2.enter_context(nc.sbuf_tensor("oa", [128, 16 * 132], F32))
                sden = p2.enter_context(nc.sbuf_tensor("sden", [128, 64], F32))
                stA = p2.enter_context(nc.sbuf_tensor("stA", [128, 64], F32))
                stB = p2.enter_context(nc.sbuf_tensor("stB", [128, 64], F32))
                srd = p2.enter_context(nc.sbuf_tensor("srd", [128, 64], F32))
                sytmp = p2.enter_context(nc.sbuf_tensor("sytmp", [128, 128], F32))
                sytmp2 = p2.enter_context(nc.sbuf_tensor("sytmp2", [128, DM], F32))
                ych = p2.enter_context(nc.sbuf_tensor("ych", [128, DM], F32))
                yTb = p2.enter_context(nc.sbuf_tensor("yTb", [128, DM], BF16))
                zst = p2.enter_context(nc.sbuf_tensor("zst", [128, DM], F32))
                h2st = p2.enter_context(nc.sbuf_tensor("h2st", [128, DM], F32))
                xnst = p2.enter_context(nc.sbuf_tensor("xnst", [128, DM], BF16))
                szstat = p2.enter_context(nc.sbuf_tensor("szstat", [128, 16], F32))
                szv = p2.enter_context(nc.sbuf_tensor("szv", [128, 8], F32))
                sel8 = p2.enter_context(nc.sbuf_tensor("sel8", [128, 8], F32))
                stmp8 = p2.enter_context(nc.sbuf_tensor("stmp8", [128, 4], F32))
                sZ = p2.enter_context(nc.sbuf_tensor("sZ", [128, 16 * 132], F32))
                sZb = p2.enter_context(nc.sbuf_tensor("sZb", [128, 16 * 132], BF16))
                mu = [p2.enter_context(nc.psum_tensor(f"mu{i}", [128, 512], F32))
                      for i in range(4)]
                pb = [p2.enter_context(nc.psum_tensor(f"pb{i}", [128, 272], F32))
                      for i in range(2)]
                ptb2 = [p2.enter_context(nc.psum_tensor(f"p2tb{i}", [128, 128], BF16))
                        for i in range(2)]

                def cvread2(nm, dsttile, vi, deps=("sp:mod1",)):
                    P.op("sy", (lambda dsttile=dsttile, vi=vi: lambda h: h.dma_start(
                        dsttile[:, :],
                        bass.AP(modsp, vi * DM, [[0, 128], [1, DM]])))(),
                        after=deps, key=f"bc:{nm}", dma=True)
                cvread2("cb1", cb1x, 2)
                cvread2("sb", sbvx, 3)
                cvread2("lga", lgax, 4)
                cvread2("lba", lbax, 5)
                P.op("sy", lambda h: h.dma_start(
                    rmswx[:, :], bass.AP(rms_w, 0, [[0, 128], [1, DM]])),
                    key="ld:rmw", dma=True)
                P.op("sy", lambda h: h.dma_start(
                    boutx[:, :], bass.AP(boutv, 0, [[0, 128], [1, DM]])),
                    key="ld:bov", dma=True)
                P.op("dv", lambda h: h.memset(sZ[:, :], 0.0), key="z:init")
                P.op("dv", lambda h: h.memset(sZb[:, :], 0.0), key="z:init:b")

                with ExitStack() as dp:
                    cumA = dp.enter_context(nc.sbuf_tensor("cumA", [64, T], F32))
                    cumB = dp.enter_context(nc.sbuf_tensor("cumB", [64, T], F32))
                    sPall = dp.enter_context(nc.sbuf_tensor("sPall", [128, T], F32))
                    sPC64a = dp.enter_context(nc.sbuf_tensor("sPC64a", [64, 16], F32))
                    sRvall = dp.enter_context(nc.sbuf_tensor("sRvall", [64, 256], F32))
                    for si, d in enumerate([1, 2, 4, 8, 16, 32, 64]):
                        if si == 0:
                            src = logb
                        else:
                            src = cumB if si % 2 == 1 else cumA
                        dst = cumB if si % 2 == 0 else cumA
                        aft = [f"bt:ln:{n}" for n in range(4)] if si == 0 else []
                        P.op("dv", (lambda src=src, dst=dst, d=d: lambda h: h.tensor_copy(
                            apx(dst, 0, [[128, NCH], [1, d]], 0, 64),
                            apx(src, 0, [[128, NCH], [1, d]], 0, 64)))(), after=aft)
                        P.op("dv", (lambda src=src, dst=dst, d=d: lambda h: h.tensor_tensor(
                            apx(dst, d, [[128, NCH], [1, 128 - d]], 0, 64),
                            apx(src, d, [[128, NCH], [1, 128 - d]], 0, 64),
                            apx(src, 0, [[128, NCH], [1, 128 - d]], 0, 64), ALU.add))(),
                            key=("dp:cum" if si == 6 else None))
                    P.op("dv", lambda h: h.memset(sPall[64:128, :], 0.0), key="dp:pz")
                    P.op("ac", lambda h: h.activation(sPall[0:64, :], cumB[0:64, :],
                                                      AF.Exp),
                         after=["dp:cum", "dp:pz"], key="dp:pexp")
                    P.op("ac", lambda h: h.activation(
                        sPC64a[0:64, :], apx(cumB, 127, [[128, NCH]], 0, 64), AF.Exp),
                        after=["dp:cum"], key="dp:pc64")
                    P.op("dv", lambda h: h.tensor_tensor(
                        apx(sRvall, 0, [[16, NCH], [1, 16]], 0, 64),
                        apx(sRsel, 0, [[0, NCH], [1, 16]], 0, 64),
                        apx(sPC64a, 0, [[1, NCH], [0, 16]], 0, 64), ALU.mult),
                        after=["dp:pc64", "ld:rsl"], key="dp:rv")
                    P.op("te", lambda h: h.matmul(pb[0][:, 0:256], sLsel[0:64, :],
                                                  sRvall[0:64, :], start=True, stop=True),
                         after=["dp:rv", "ld:lsl"], key="dp:pcmm")
                    P.op("ac", lambda h: h.copy(sPCtall[:, :], pb[0][:, 0:256]),
                         after=["dp:pcmm"], key="dp:pctev")
                    for k in range(4):
                        for j in range(4):
                            c = 4 * k + j
                            P.op("te", (lambda k=k, j=j, c=c: lambda h: h.transpose(
                                mu[k][:, 128 * j: 128 * (j + 1)],
                                sPall[:, 128 * c: 128 * (c + 1)], sidf[:, :]))(),
                                after=["dp:pexp"], key=f"dp:ptr:{c}")
                    for k in range(4):
                        P.op("ac", (lambda k=k: lambda h: h.copy(
                            sPTall[:, 256 * k: 256 * (k + 1)],
                            apx(mu[k], 0, [[128, 4], [1, 64]])))(),
                            after=[f"dp:ptr:{4 * k + 3}"], key=f"dp:ptev:{k}")
                    P.op("ac", lambda h: h.activation(sPall[0:64, :], cumB[0:64, :],
                                                      AF.Exp, scale=-1.0),
                         after=["dp:ptr:15"], key="dp:ivexp")
                    for k in range(4):
                        for j in range(4):
                            c = 4 * k + j
                            P.op("te", (lambda k=k, j=j, c=c: lambda h: h.transpose(
                                mu[k][:, 128 * j: 128 * (j + 1)],
                                sPall[:, 128 * c: 128 * (c + 1)], sidf[:, :]))(),
                                after=["dp:ivexp", f"dp:ptev:{k}"], key=f"dp:ivtr:{c}")
                    for k in range(4):
                        P.op("ac", (lambda k=k: lambda h: h.copy(
                            sIvTall[:, 256 * k: 256 * (k + 1)],
                            apx(mu[k], 0, [[128, 4], [1, 64]])))(),
                            after=[f"dp:ivtr:{4 * k + 3}"], key=f"dp:ivtev:{k}")

                for c in range(NCH):
                    dep_done = [f"ch:{c - 2}:done"] if c >= 2 else []
                    P.op("sy", (lambda c=c: lambda h: h.dma_start(
                        phq_st[c % 2][:, :].rearrange("p (b j) -> p b j", j=128),
                        bass.AP(phiqsp, 128 * c, [[8 * T, 128], [T, 8], [1, 128]])))(),
                        after=dep_done + [f"sp:q:{c}"], key=f"l2:phq:{c}", dma=True,
                        stream=f"lphq{c % 2}")
                    P.op("sy", (lambda c=c: lambda h: h.dma_start(
                        ktm_st[c % 2][:, :], ktmsp[:, DM * c: DM * (c + 1)]))(),
                        after=dep_done + [f"sp:kt:{c}"], key=f"l2:ktm:{c}", dma=True,
                        stream=f"lktm{c % 2}")
                    P.op("sy", (lambda c=c: lambda h: h.dma_start(
                        va2_st[c % 2][:, :], vaugsp[:, VA * c: VA * (c + 1)]))(),
                        after=dep_done + [f"sp:va:{c}"], key=f"l2:va:{c}", dma=True,
                        stream=f"lva{c % 2}")
                    P.op("sy", (lambda c=c: lambda h: h.dma_start(
                        h12_st[c % 2][:, :], h1sp[:, DM * c: DM * (c + 1)]))(),
                        after=dep_done + [f"sp:h1:{c}"], key=f"l2:h1:{c}", dma=True,
                        stream=f"lh1{c % 2}")
                    P.op("sy", (lambda c=c: lambda h: h.dma_start(
                        x2_st[c % 2][:, :], x_hbm[128 * c: 128 * (c + 1), :]))(),
                        after=dep_done, key=f"l2:x:{c}", dma=True, stream=f"lx{c % 2}")

                    for m in range(2):
                        P.op("dv", (lambda c=c, m=m: lambda h: h.tensor_tensor(
                            ktil[m][:, :].rearrange("p (a b) -> p a b", b=32),
                            ktm_st[c % 2][:, :].rearrange("p (a b) -> p a b", b=32),
                            apx(sIvTall, 64 * c + m, [[2, 32], [0, 32]]), ALU.mult))(),
                            after=[f"dp:ivtev:{c // 4}", f"l2:ktm:{c}"] +
                                  ([f"u:mm:{c - 1}:{14 + m}"] if c else []),
                            key=f"ktl:{c}:{m}")
                        for b in range(8):
                            i = m * 8 + b
                            pt = ptb2[i % 2]
                            aft = [f"ktl:{c}:{m}"]
                            if c > 0 and i < 2:
                                aft.append(f"xnte:{c - 1}:{6 + i}")
                            if i >= 2:
                                aft.append(f"kte:{c}:{(i - 2) // 8}:{(i - 2) % 8}")
                            P.op("te", (lambda pt=pt, m=m, b=b: lambda h: h.transpose(
                                pt[:, :], ktil[m][:, 128 * b: 128 * (b + 1)], sidb[:, :]))(),
                                after=aft, key=f"ktr:{c}:{m}:{b}")
                            P.op("ac", (lambda pt=pt, m=m, b=b: lambda h: h.copy(
                                ktilT[m][:, 128 * b: 128 * (b + 1)], pt[:, :]))(),
                                after=[f"ktr:{c}:{m}:{b}"], key=f"kte:{c}:{m}:{b}")

                    for g in range(16):
                        H4, m = g // 2, g % 2
                        sl = g % 2
                        aft = [f"kte:{c}:{m}:{H4}", f"l2:phq:{c}"]
                        if g >= 1:
                            aft.append(f"sc:ev:{c}:{g - 1}")
                        elif c > 0:
                            aft.extend([f"lgr:{c - 1}", f"lgel:{c - 1}",
                                        f"zdv:{c - 1}:1"])
                        def scmm(c=c, g=g, H4=H4, m=m):
                            def f(h):
                                r = None
                                for hh in range(4):
                                    r = h.matmul(
                                        mu[hh][:, 0:128],
                                        ktilT[m][32 * hh: 32 * (hh + 1), 128 * H4: 128 * (H4 + 1)],
                                        phq_st[c % 2][32 * hh: 32 * (hh + 1), 128 * H4: 128 * (H4 + 1)],
                                        start=True, stop=True, tile_position=(32 * hh, 0))
                                return r
                            return f
                        P.op("te", scmm(), after=aft, key=f"sc:mm:{c}:{g}")
                        def scev(sl=sl):
                            def f(h):
                                r = None
                                for hh in range(4):
                                    r = h.copy(sAt[sl][:, 128 * hh: 128 * (hh + 1)],
                                               mu[hh][:, 0:128])
                                return r
                            return f
                        P.op("ac", scev(), after=[f"sc:mm:{c}:{g}"] +
                             ([f"oa:mm:{c}:{g - 2}"] if g >= 2 else
                              ([f"oa:mm:{c - 1}:{14 + g}"] if c else [])),
                             key=f"sc:ev:{c}:{g}")
                        P.op("dv", (lambda sl=sl: lambda h: h.tensor_tensor(
                            sAt[sl][:, :].rearrange("p (a b) -> p a b", b=128),
                            sAt[sl][:, :].rearrange("p (a b) -> p a b", b=128),
                            apx(smask, 0, [[0, 4], [1, 128]]), ALU.mult))(),
                            after=[f"sc:ev:{c}:{g}", "ld:msk"], key=f"sc:mk:{c}:{g}")
                        aft = [f"sc:mk:{c}:{g}", f"l2:va:{c}"]
                        aft.append(f"zbf:{c - 1}:{g}" if c else "z:init:b")
                        if g >= 2:
                            aft.append(f"oa:ev:{c}:{g - 2}")
                            aft.append(f"zadd:{c}:{g - 2}")
                        elif c:
                            aft.append(f"oa:ev:{c - 1}:{14 + g}")
                        elif g == 0:
                            aft.append("dp:pctev")
                        def oamm(c=c, g=g, H4=H4, m=m, sl=sl):
                            def f(h):
                                r = None
                                for hh in range(4):
                                    h.matmul(
                                        pb[sl][:, 33 * hh: 33 * (hh + 1)],
                                        sAt[sl][:, 128 * hh: 128 * (hh + 1)],
                                        va2_st[c % 2][:, 33 * (4 * H4 + hh): 33 * (4 * H4 + hh) + 33],
                                        start=True, stop=False)
                                    r = h.matmul(
                                        pb[sl][:, 33 * hh: 33 * (hh + 1)],
                                        phq_st[c % 2][32 * hh: 32 * (hh + 1), 128 * H4: 128 * (H4 + 1)],
                                        sZb[32 * hh: 32 * (hh + 1), 132 * g + 33 * hh: 132 * g + 33 * (hh + 1)],
                                        start=False, stop=True, tile_position=(32 * hh, 0))
                                return r
                            return f
                        P.op("te", oamm(), after=aft, key=f"oa:mm:{c}:{g}")
                        aft = [f"ktl:{c}:{m}", f"l2:va:{c}"]
                        if g >= 2:
                            aft.append(f"zadd:{c}:{g - 2}")
                        elif c:
                            aft.append(f"zadd:{c - 1}:{14 + g}")
                        elif g == 0:
                            aft.append("dp:pctev")
                        P.op("te", (lambda c=c, g=g, H4=H4, m=m, sl=sl: lambda h: h.matmul(
                            pb[sl][:, 132:264], ktil[m][:, 128 * H4: 128 * (H4 + 1)],
                            va2_st[c % 2][:, 33 * 4 * H4: 33 * 4 * H4 + 132],
                            start=True, stop=True))(), after=aft, key=f"u:mm:{c}:{g}")
                        P.op("ac", (lambda c=c, g=g, sl=sl: lambda h: h.copy(
                            oa[:, 132 * g: 132 * (g + 1)], pb[sl][:, 0:132]))(),
                            after=[f"oa:mm:{c}:{g}", f"u:mm:{c}:{g}"] +
                                  ([f"y:{c - 1}"] if (c and g == 0) else []),
                            key=f"oa:ev:{c}:{g}")
                        P.op("dv", (lambda g=g, sl=sl: lambda h: h.tensor_tensor(
                            sZ[:, 132 * g: 132 * (g + 1)], sZ[:, 132 * g: 132 * (g + 1)],
                            pb[sl][:, 132:264], ALU.add))(),
                            after=[f"u:mm:{c}:{g}", "dp:pctev", f"oa:ev:{c}:{g}"] +
                                  (["z:init"] if c == 0 else []))
                        P.op("dv", (lambda c=c, g=g: lambda h: h.tensor_scalar(
                            sZ[:, 132 * g: 132 * (g + 1)], sZ[:, 132 * g: 132 * (g + 1)],
                            sPCtall[:, 16 * c + g: 16 * c + g + 1], None, ALU.mult))(),
                            key=f"zadd:{c}:{g}")
                        P.op("ac", (lambda g=g: lambda h: h.copy(
                            sZb[:, 132 * g: 132 * (g + 1)], sZ[:, 132 * g: 132 * (g + 1)]))(),
                            after=[f"zadd:{c}:{g}"], key=f"zbf:{c}:{g}")

                    PTsel = apx(sPTall, 64 * c, [[8, 8], [1, 2], [2, 4]])
                    mixsel = apx(mixa, 64 * c, [[8, 8], [1, 2], [2, 4]])
                    P.op("dv", lambda h: h.tensor_copy(
                        sden[:, :].rearrange("p (a b) -> p a b", b=4),
                        apx(oa, 32, [[132, 16], [33, 4]])),
                        after=[f"oa:ev:{c}:15", f"dp:ptev:{c // 4}"])
                    P.op("dv", (lambda PTsel=PTsel: lambda h: h.tensor_tensor(
                        stA[:, :].rearrange("p (a b c) -> p a b c", b=2, c=4),
                        sden[:, :].rearrange("p (a b c) -> p a b c", b=2, c=4),
                        PTsel, ALU.mult))())
                    P.op("dv", lambda h: h.tensor_scalar_add(stA[:, :], stA[:, :], ATT_EPS))
                    P.op("dv", lambda h: h.reciprocal(stA[:, :], stA[:, :]))
                    P.op("dv", (lambda PTsel=PTsel, mixsel=mixsel: lambda h: h.tensor_tensor(
                        stB[:, :].rearrange("p (a b c) -> p a b c", b=2, c=4),
                        PTsel, mixsel, ALU.mult))())
                    P.op("dv", lambda h: h.tensor_mul(srd[:, :], stA[:, :], stB[:, :]),
                         key=f"rd:{c}")

                    P.op("dv", (lambda c=c: lambda h: h.tensor_tensor(
                        ych[:, :].rearrange("p (a b c) -> p a b c", b=4, c=32),
                        apx(oa, 0, [[264, 8], [33, 4], [1, 32]]),
                        apx(srd, 0, [[8, 8], [1, 4], [0, 32]]), ALU.mult))(),
                        after=([f"ytr:{c - 1}:7"] if c else []))
                    P.op("dv", lambda h: h.tensor_tensor(
                        sytmp2[:, :].rearrange("p (a b c) -> p a b c", b=4, c=32),
                        apx(oa, 132, [[264, 8], [33, 4], [1, 32]]),
                        apx(srd, 4, [[8, 8], [1, 4], [0, 32]]), ALU.mult))
                    P.op("dv", (lambda c=c: lambda h: h.tensor_add(
                        ych[:, :], ych[:, :], sytmp2[:, :]))(), key=f"y:{c}")

                    for b in range(8):
                        pt = mu[2 + b % 2]
                        aft = [f"y:{c}", f"sc:ev:{c}:15"]
                        if b >= 2:
                            aft.append(f"yte:{c}:{b - 2}")
                        P.op("te", (lambda pt=pt, b=b: lambda h: h.transpose(
                            pt[:, 0:128], ych[:, 128 * b: 128 * (b + 1)], sidf[:, :]))(),
                            after=aft, key=f"ytr:{c}:{b}")
                        P.op("ac", (lambda pt=pt, b=b: lambda h: h.copy(
                            yTb[:, 128 * b: 128 * (b + 1)], pt[:, 0:128]))(),
                            after=[f"ytr:{c}:{b}"], key=f"yte:{c}:{b}")
                    for n in range(2):
                        aft = [f"yte:{c}:7", f"sc:ev:{c}:15"]
                        P.op("te", (lambda n=n: lambda h: _seq_mm(
                            h, mu[n][:, 0:512],
                            [yTb[:, 128 * kb: 128 * (kb + 1)] for kb in range(8)],
                            [WoT[:, DM * kb + 512 * n: DM * kb + 512 * (n + 1)] for kb in range(8)]))(),
                            after=aft, key=f"z:mm:{c}:{n}")
                        P.op("dv", (lambda c=c, n=n: lambda h: h.tensor_tensor(
                            zst[:, 512 * n: 512 * (n + 1)], mu[n][:, 0:512],
                            h12_st[c % 2][:, 512 * n: 512 * (n + 1)], ALU.add))(),
                            after=[f"z:mm:{c}:{n}", f"l2:h1:{c}"])
                        P.op("dv", (lambda n=n: lambda h: h.tensor_add(
                            zst[:, 512 * n: 512 * (n + 1)], zst[:, 512 * n: 512 * (n + 1)],
                            boutx[:, 512 * n: 512 * (n + 1)]))(),
                            after=["ld:bov"], key=f"zdv:{c}:{n}")
                    P.op("dv", lambda h: h.bn_stats(szstat[:, 0:6], zst[:, 0:512]))
                    P.op("dv", lambda h: h.bn_stats(szstat[:, 6:12], zst[:, 512:1024]))
                    P.op("dv", lambda h: h.bn_aggr(szv[:, 0:2], szstat[:, 0:12]),
                         key=f"zs:{c}")
                    P.op("ac", lambda h: h.activation(szv[:, 2:3], szv[:, 1:2], AF.Ln,
                                                      bias=LN_EPS), after=[f"zs:{c}"])
                    P.op("ac", lambda h: h.activation(szv[:, 3:4], szv[:, 2:3], AF.Exp,
                                                      scale=-0.5), key=f"zrs:{c}")
                    P.op("dv", lambda h: h.tensor_scalar(
                        h2st[:, :], zst[:, :], szv[:, 0:1], szv[:, 3:4],
                        ALU.subtract, ALU.mult), after=[f"zrs:{c}"])
                    P.op("dv", lambda h: h.tensor_mul(h2st[:, :], h2st[:, :], lgax[:, :]),
                         after=["bc:lga"])
                    P.op("dv", lambda h: h.tensor_add(h2st[:, :], h2st[:, :], lbax[:, :]),
                         after=["bc:lba"])
                    P.op("dv", (lambda c=c: lambda h: h.tensor_add(
                        x2_st[c % 2][:, :], x2_st[c % 2][:, :], h2st[:, :]))(),
                        after=[f"l2:x:{c}"], key=f"xn:{c}")
                    P.op("gp", (lambda c=c: lambda h: h.dma_start(
                        xsp[:, DM * c: DM * (c + 1)], x2_st[c % 2][:, :]))(),
                        after=[f"xn:{c}"], key=f"sp:x:{c}", dma=True)
                    P.op("dv", (lambda c=c: lambda h: h.bn_stats(
                        szstat[:, 0:6], x2_st[c % 2][:, 0:512]))(), after=[f"xn:{c}"])
                    P.op("dv", (lambda c=c: lambda h: h.bn_stats(
                        szstat[:, 6:12], x2_st[c % 2][:, 512:1024]))())
                    P.op("dv", lambda h: h.bn_aggr(szv[:, 0:2], szstat[:, 0:12]),
                         key=f"xs:{c}")
                    P.op("ac", lambda h: h.activation(szv[:, 2:3], szv[:, 1:2], AF.Ln,
                                                      bias=LN_EPS), after=[f"xs:{c}"])
                    P.op("ac", lambda h: h.activation(szv[:, 3:4], szv[:, 2:3], AF.Exp,
                                                      scale=-0.5), key=f"xrs:{c}")
                    P.op("dv", (lambda c=c: lambda h: h.tensor_scalar(
                        h2st[:, :], x2_st[c % 2][:, :], szv[:, 0:1], szv[:, 3:4],
                        ALU.subtract, ALU.mult))(), after=[f"xrs:{c}"])
                    P.op("dv", lambda h: h.tensor_mul(h2st[:, :], h2st[:, :], cb1x[:, :]),
                         after=["bc:cb1"])
                    P.op("dv", lambda h: h.tensor_add(h2st[:, :], h2st[:, :], sbvx[:, :]),
                         after=["bc:sb"], key=f"h2:{c}")
                    P.op("dv", lambda h: h.bn_stats(szstat[:, 0:6], h2st[:, 0:512]))
                    P.op("dv", lambda h: h.bn_stats(szstat[:, 6:12], h2st[:, 512:1024]))
                    P.op("dv", lambda h: h.bn_aggr(szv[:, 4:6], szstat[:, 0:12]),
                         key=f"h2s:{c}")
                    P.op("ac", lambda h: h.activation(szv[:, 6:7], szv[:, 4:5], AF.Square),
                         after=[f"h2s:{c}"], key=f"h2sq:{c}")
                    P.op("dv", lambda h: h.tensor_add(szv[:, 6:7], szv[:, 6:7], szv[:, 5:6]),
                         after=[f"h2sq:{c}"], key=f"ms:{c}")
                    P.op("ac", lambda h: h.activation(szv[:, 6:7], szv[:, 6:7], AF.Ln,
                                                      bias=RMS_EPS), after=[f"ms:{c}"])
                    P.op("ac", lambda h: h.activation(szv[:, 7:8], szv[:, 6:7], AF.Exp,
                                                      scale=-0.5), key=f"h2rs:{c}")
                    P.op("dv", lambda h: h.tensor_scalar(
                        h2st[:, :], h2st[:, :], szv[:, 7:8], None, ALU.mult),
                        after=[f"h2rs:{c}"])
                    P.op("dv", lambda h: h.tensor_mul(xnst[:, :], h2st[:, :], rmswx[:, :]),
                         after=["ld:rmw"] + ([f"xntr:{c - 1}:7"] if c else []),
                         key=f"xnst:{c}")
                    for b in range(8):
                        pt = ptb2[b % 2]
                        aft = [f"xnst:{c}", f"kte:{c}:1:{6 + (b % 2)}"]
                        if b >= 2:
                            aft.append(f"xnte:{c}:{b - 2}")
                        P.op("te", (lambda pt=pt, b=b: lambda h: h.transpose(
                            pt[:, :], xnst[:, 128 * b: 128 * (b + 1)], sidb[:, :]))(),
                            after=aft, key=f"xntr:{c}:{b}")
                        P.op("ac", (lambda pt=pt, c=c, b=b: lambda h: h.copy(
                            xnT[:, T * b + 128 * c: T * b + 128 * (c + 1)], pt[:, :]))(),
                            after=[f"xntr:{c}:{b}"], key=f"xnte:{c}:{b}")
                    P.op("te", (lambda c=c: lambda h: _seq_mm(
                        h, mu[0][:, 0:8],
                        [xnT[:, T * kb + 128 * c: T * kb + 128 * (c + 1)] for kb in range(8)],
                        [sgateT[:, 8 * kb: 8 * (kb + 1)] for kb in range(8)]))(),
                        after=[f"xnte:{c}:7", f"zdv:{c}:0"], key=f"lg:mm:{c}")
                    P.op("dv", lambda h: h.reduce_max(szv[:, 0:1], mu[0][:, 0:8], AX.X,
                                                      negate=True), after=[f"lg:mm:{c}"])
                    P.op("dv", lambda h: h.reduce_max(szv[:, 1:2], mu[0][:, 0:8], AX.X),
                         key=f"lgmx:{c}")
                    P.op("ac", lambda h: h.activation(sel8[:, :], mu[0][:, 0:8], AF.Exp,
                                                      bias=szv[:, 0:1]),
                         after=[f"lgmx:{c}"], key=f"lgel:{c}")
                    P.op("dv", lambda h: h.reduce_sum(szv[:, 3:4], sel8[:, :], AX.X),
                         after=[f"lgel:{c}"])
                    P.op("dv", lambda h: h.reciprocal(szv[:, 4:5], szv[:, 3:4]),
                         key=f"lgsc:{c}")
                    P.op("dv", (lambda c=c: lambda h: h.tensor_scalar(
                        sel8[:, :], mu[0][:, 0:8], szv[:, 1:2], None, ALU.is_ge))(),
                        after=[f"lgel:{c}", f"lgsc:{c}"], key=f"lgr:{c}")
                    P.op("dv", (lambda c=c: lambda h: h.tensor_scalar(
                        msc[:, 8 * c: 8 * (c + 1)], sel8[:, :], szv[:, 4:5], None,
                        ALU.mult))(), key=f"msc:{c}")
                    P.op("dv", lambda h: h.drain(),
                         after=[f"msc:{c}", f"u:mm:{c}:15", f"zbf:{c}:15", f"sp:x:{c}"],
                         key=f"ch:{c}:done")
                if DEBUG:
                    P.op("gp", lambda h: h.dma_start(d_mix[:, :], mixa[:, :]),
                         after=["ch:15:done"], key="dbg:mix", dma=True)
                    P.op("gp", lambda h: h.dma_start(d_msc[:, :], msc[:, :]),
                         after=["ch:15:done"], key="dbg:msc", dma=True)
                    P.op("gp", lambda h: h.dma_start(d_xnt[:, :], xnT[:, :]),
                         after=["ch:15:done"], key="dbg:xnt", dma=True)
                P.fence("p2")

        with ExitStack() as p3:
            W1bf = p3.enter_context(nc.sbuf_tensor("W1bf", [128, 8 * FF], BF16))
            W2bf = p3.enter_context(nc.sbuf_tensor("W2bf", [128, 32 * DM], BF16))
            hTb = p3.enter_context(nc.sbuf_tensor("hTb", [128, 32 * 512], BF16))
            b1c = p3.enter_context(nc.sbuf_tensor("b1c", [128, 32], F32))
            b1row = p3.enter_context(nc.sbuf_tensor("b1row", [128, 128], F32))
            b2bf = p3.enter_context(nc.sbuf_tensor("b2bf", [1, DM], BF16))
            ctb = [p3.enter_context(nc.sbuf_tensor(f"ctb{i}", [128, 512], F32))
                   for i in range(2)]
            pm1 = [p3.enter_context(nc.psum_tensor(f"pm1{i}", [128, 512], F32))
                   for i in range(2)]
            pm2 = [p3.enter_context(nc.psum_tensor(f"pm2{i}", [128, 512], F32))
                   for i in range(2)]

            uac_keys = []
            P.op("dv", lambda h: h.memset(b1row[:, :], 0.0), key="b1r:z")
            for e in range(E):
                aft1 = [f"m1:{e - 1}:3:31"] if e else []
                P.op("gp", (lambda e=e: lambda h: h.dma_start(
                    W1bf[:, :].rearrange("p (r f) -> p r f", f=FF),
                    bass.AP(W1, e * DM * FF, [[FF, 128], [128 * FF, 8], [1, FF]])))(),
                    after=aft1, key=f"w1ld:{e}", dma=True)
                aft2 = [f"m2:{e - 1}:3:3:1"] if e else []
                P.op("gp", (lambda e=e: lambda h: h.dma_start(
                    W2bf[:, :].rearrange("p (r d) -> p r d", d=DM),
                    bass.AP(W2, e * FF * DM, [[DM, 128], [128 * DM, 32], [1, DM]])))(),
                    after=aft2, key=f"w2ld:{e}", dma=True)
                P.op("sy", (lambda e=e: lambda h: h.dma_start(
                    b1row[0:32, :], bass.AP(b1, e * FF, [[128, 32], [1, 128]])))(),
                    after=([f"b1tr:{e - 1}"] if e else ["b1r:z"]), key=f"b1ld:{e}", dma=True)
                P.op("te", (lambda e=e: lambda h: h.transpose(
                    pm1[0][:, 0:128], b1row[:, :], sidf[:, :]))(),
                    after=[f"b1ld:{e}"] + ([f"m1ev:{e - 1}:3:31"] if e else []),
                    key=f"b1tr:{e}")
                P.op("ac", (lambda e=e: lambda h: h.copy(b1c[:, :], pm1[0][:, 0:32]))(),
                    after=[f"b1tr:{e}"], key=f"b1ev:{e}")
                P.op("gp", (lambda e=e: lambda h: h.dma_start(
                    b2bf[0:1, :], bass.AP(b2, e * DM, [[DM, 1], [1, DM]])))(),
                    after=aft2, key=f"b2ld:{e}", dma=True)
                for q in range(4):
                    for mf in range(32):
                        sl = mf % 2
                        aft = [f"w1ld:{e}"]
                        if q == 0 and mf == 0:
                            aft.append(f"b1ev:{e}")
                        if mf >= 2:
                            aft.append(f"m1ev:{e}:{q}:{mf - 2}")
                        elif q > 0:
                            aft.append(f"m1ev:{e}:{q - 1}:{30 + mf}")
                        elif e > 0:
                            aft.append(f"m1ev:{e - 1}:3:{30 + mf}")
                        P.op("te", (lambda e=e, q=q, mf=mf, sl=sl: lambda h: _seq_mm(
                            h, pm1[sl][:, 0:512],
                            [W1bf[:, FF * kc + 128 * mf: FF * kc + 128 * (mf + 1)]
                             for kc in range(8)],
                            [xnT[:, T * kc + 512 * q: T * kc + 512 * (q + 1)]
                             for kc in range(8)]))(),
                            after=aft, key=f"m1:{e}:{q}:{mf}")
                        P.op("ac", (lambda mf=mf, sl=sl: lambda h: h.activation(
                            hTb[:, 512 * mf: 512 * (mf + 1)], pm1[sl][:, 0:512],
                            AF.Silu, bias=b1c[:, mf:mf + 1]))(),
                            after=[f"m1:{e}:{q}:{mf}", f"b1ev:{e}"],
                            key=f"m1ev:{e}:{q}:{mf}")
                    for tt4 in range(4):
                        tau = 4 * q + tt4
                        for n in range(2):
                            idx = tt4 * 2 + n
                            sl = idx % 2
                            aft = [f"m1ev:{e}:{q}:31", f"w2ld:{e}", f"b2ld:{e}"]
                            if idx >= 2:
                                aft.append(f"m2ev:{e}:{q}:{tt4 - 1}:{n}")
                            elif q > 0:
                                aft.append(f"m2ev:{e}:{q - 1}:3:{n}")
                            elif e > 0:
                                aft.append(f"m2ev:{e - 1}:3:3:{n}")
                            def m2f(e=e, q=q, tt4=tt4, n=n, sl=sl):
                                def f(h):
                                    for kc in range(32):
                                        h.matmul(pm2[sl][:, 0:512],
                                                 hTb[:, 512 * kc + 128 * tt4: 512 * kc + 128 * (tt4 + 1)],
                                                 W2bf[:, DM * kc + 512 * n: DM * kc + 512 * (n + 1)],
                                                 start=(kc == 0), stop=False)
                                    return h.matmul(pm2[sl][:, 0:512], sones[0:1, 0:128],
                                                    b2bf[0:1, 512 * n: 512 * (n + 1)],
                                                    start=False, stop=True)
                                return f
                            P.op("te", m2f(), after=aft, key=f"m2:{e}:{q}:{tt4}:{n}")
                            aftv = [f"m2:{e}:{q}:{tt4}:{n}"]
                            if uac_keys[-2:] and len(uac_keys) >= 2:
                                aftv.append(uac_keys[-2])
                            P.op("ac", (lambda e=e, tau=tau, sl=sl: lambda h: h.activation(
                                ctb[sl][:, :], pm2[sl][:, 0:512], AF.Copy,
                                scale=msc[:, 8 * tau + e: 8 * tau + e + 1]))(),
                                after=aftv, key=f"m2ev:{e}:{q}:{tt4}:{n}")
                            uk = f"uac:{e}:{tau}:{n}"
                            P.op("gp", (lambda e=e, tau=tau, n=n, sl=sl: lambda h: h.dma_start(
                                uacc[:, DM * tau + 512 * n: DM * tau + 512 * (n + 1)],
                                ctb[sl][:, :],
                                accum_op=(ALU.bypass if e == 0 else ALU.add)))(),
                                after=[f"m2ev:{e}:{q}:{tt4}:{n}"], key=uk, dma=True)
                            uac_keys.append(uk)
            P.fence("p3")

        with ExitStack() as p4:
            gbx = p4.enter_context(nc.sbuf_tensor("gbx", [128, DM], F32))
            uf = [p4.enter_context(nc.sbuf_tensor(f"uf{i}", [128, DM], F32))
                  for i in range(2)]
            xf = [p4.enter_context(nc.sbuf_tensor(f"xf{i}", [128, DM], F32))
                  for i in range(2)]
            of = [p4.enter_context(nc.sbuf_tensor(f"of{i}", [128, DM], F32))
                  for i in range(2)]
            P.op("sy", lambda h: h.dma_start(
                gbx[:, :], bass.AP(modsp, 6 * DM, [[0, 128], [1, DM]])), key="bc:gb", dma=True)
            outk = []
            for t in range(NT):
                aft = [f"out:{t - 2}"] if t >= 2 else []
                P.op("sy", (lambda t=t: lambda h: h.dma_start(
                    uf[t % 2][:, :], uacc[:, DM * t: DM * (t + 1)]))(),
                    after=aft + [f"uac:7:{t}:1"], key=f"l4u:{t}", dma=True,
                    stream=f"l4u{t % 2}")
                P.op("sy", (lambda t=t: lambda h: h.dma_start(
                    xf[t % 2][:, :], xsp[:, DM * t: DM * (t + 1)]))(),
                    after=aft, key=f"l4x:{t}", dma=True, stream=f"l4x{t % 2}")
                P.op("dv", (lambda t=t: lambda h: h.tensor_mul(
                    of[t % 2][:, :], uf[t % 2][:, :], gbx[:, :]))(),
                    after=[f"l4u:{t}", "bc:gb"] + ([f"out:{t - 2}"] if t >= 2 else []))
                P.op("dv", (lambda t=t: lambda h: h.tensor_add(
                    of[t % 2][:, :], of[t % 2][:, :], xf[t % 2][:, :]))(),
                    after=[f"l4x:{t}"], key=f"o4:{t}")
                P.op("gp", (lambda t=t: lambda h: h.dma_start(
                    out[128 * t: 128 * (t + 1), :], of[t % 2][:, :]))(),
                    after=[f"o4:{t}"], key=f"out:{t}", dma=True)
                outk.append(f"out:{t}")
            P.op("gp", lambda h: h.drain(), after=outk, key="final")
        if TRUNC is not None:
            if TRUNC in P.keys:
                idx = P.keys[TRUNC]
            else:
                idx = max(P.keys[f"F:{TRUNC}:{e}"] for e in _ENG)
            P.ops = P.ops[:idx + 1]
    P.emit()
    return nc


def make_constants():
    half = 16
    try:
        import jax
        import jax.numpy as jnp
        with jax.default_device(jax.devices("cpu")[0]):
            invf = 1.0 / 10000.0 ** (jnp.arange(half, dtype=jnp.float32) / half)
            fr = jnp.arange(T, dtype=jnp.float32)[:, None] * invf
            cosv = np.asarray(jnp.cos(fr), dtype=np.float32)
            sinv = np.asarray(jnp.sin(fr), dtype=np.float32)
    except Exception:
        invf = np.float32(1.0) / np.float32(10000.0) ** (
            np.arange(half, dtype=np.float32) / np.float32(half))
        fr = (np.arange(T, dtype=np.float32)[:, None] * invf[None, :])
        cosv = np.cos(fr).astype(np.float32)
        sinv = np.sin(fr).astype(np.float32)
    cosT = np.concatenate([cosv, cosv], axis=1)
    sinT = np.concatenate([-sinv, sinv], axis=1)
    cosTl = np.zeros_like(cosT)
    sinTl = np.zeros_like(sinT)
    identf = np.eye(128, dtype=np.float32)
    identb = np.eye(128, dtype=ml_dtypes.bfloat16)
    maskb = np.triu(np.ones((128, 128), dtype=np.float32)).astype(ml_dtypes.bfloat16)
    Lsel = np.zeros((64, 128), dtype=np.float32)
    for hm in range(64):
        for p in range(128):
            if (hm // 2) % 4 == p // 32:
                Lsel[hm, p] = 1.0
    Rsel = np.zeros((64, 16), dtype=np.float32)
    for g in range(16):
        H4, m = g // 2, g % 2
        for hh in range(4):
            Rsel[2 * (4 * H4 + hh) + m, g] = 1.0
    onesb = np.ones((1, 128), dtype=ml_dtypes.bfloat16)
    return dict(identf=identf, identb=identb, maskb=maskb, cosT=cosT, sinT=sinT,
                cosTl=cosTl, sinTl=sinTl,
                Lsel=Lsel, Rsel=Rsel, onesb=onesb,
            fconst=np.array([list(FCONSTS) + [0.0] * (16 - len(FCONSTS))],
                            dtype=np.float32))


def kernel(**inputs):
    nc = build_program()
    consts = make_constants()
    f32 = lambda a: np.ascontiguousarray(np.asarray(a, dtype=np.float32))
    in_maps = []
    for c in range(8):
        m = dict(
            x=f32(inputs["L0_attn_x"][c]),
            cvec=f32(inputs["L0_c"][c:c + 1]),
            Wq=f32(inputs["L0_attn_Wq"]), Wk=f32(inputs["L0_attn_Wk"]),
            Wv=f32(inputs["L0_attn_Wv"]), Wout=f32(inputs["L0_attn_Wout"]),
            Wbeta=f32(inputs["L0_attn_Wbeta"]),
            bbeta=f32(inputs["L0_attn_bbeta"]).reshape(1, 64),
            Wmix=f32(inputs["L0_attn_Wmix"]),
            bmix=f32(inputs["L0_attn_bmix"]).reshape(1, 64),
            boutv=f32(inputs["L0_attn_bout"]).reshape(1, DM),
            ln_g=f32(inputs["L0_attn_ln_g"]).reshape(1, DM),
            ln_b=f32(inputs["L0_attn_ln_b"]).reshape(1, DM),
            rms_w=f32(inputs["L0_ffn_rms_w"]).reshape(1, DM),
            gate_w=f32(inputs["L0_ffn_gate_w"]),
            W1=f32(inputs["L0_ffn_W1"]), b1=f32(inputs["L0_ffn_b1"]),
            W2=f32(inputs["L0_ffn_W2"]), b2=f32(inputs["L0_ffn_b2"]),
            wmod=f32(inputs["L0_wmod"]),
            **consts,
        )
        in_maps.append(m)
    res = run_bass_kernel_spmd(nc, in_maps, core_ids=list(range(8)))
    return np.stack([r["out"] for r in res.results], axis=0)
